# version 11
# baseline (speedup 1.0000x reference)
import numpy as np
import ml_dtypes
import concourse.bass as bass
import concourse.mybir as mybir
from concourse.bass_types import AP
from concourse.bass_utils import run_bass_kernel_spmd

F32 = mybir.dt.float32
BF16 = mybir.dt.bfloat16
I32 = mybir.dt.int32
AF = mybir.ActivationFunctionType
ALU = mybir.AluOpType
AX = mybir.AxisListType

B, S, D = 8, 2048, 1024
NT = S // 128
ND = D // 128
H, HD = 16, 64
NP = H // 2
FFN_I = 2816
MOE_H = 2048
NE = 8


class _NullInst:
    def then_inc(self, *a, **k):
        return self


class _NullEng:
    def __getattr__(self, name):
        def f(*a, **k):
            return _NullInst()
        return f


class Ctr:
    def __init__(self, handle):
        self.h = handle
        self.v = 0

    def bump(self, n=1):
        self.v += n
        return self.v


class Prog:
    def __init__(self, which, eng, sems):
        null = _NullEng()
        self.which = which
        self.te = eng if which == "tensor" else null
        self.ve = eng if which == "vector" else null
        self.se = eng if which == "scalar" else null
        self.gp = eng if which == "gpsimd" else null
        self.sy = eng if which == "sync" else null
        for name, h in sems.items():
            setattr(self, name, Ctr(h))


def run_block(nc, sems, narrative):
    with nc.Block() as block:
        @block.sync
        def _(e):
            narrative(Prog("sync", e, sems))

        @block.gpsimd
        def _(e):
            narrative(Prog("gpsimd", e, sems))

        @block.tensor
        def _(e):
            narrative(Prog("tensor", e, sems))

        @block.vector
        def _(e):
            narrative(Prog("vector", e, sems))

        @block.scalar
        def _(e):
            narrative(Prog("scalar", e, sems))


def alloc_sems(nc, names):
    return {n: nc.semaphore(n).__enter__() for n in names}


SEM_NAMES = ["sx0", "sx1", "sw0", "sw1", "sw2", "sw3", "gw0", "gw1", "gw2",
             "gpd", "sac", "sb2", "ted", "ved", "sed", "std0", "std1",
             "stx0", "stx1", "pod"]


class _NSWrap:

    def __init__(self, nc, pfx):
        self._nc = nc
        self._pfx = pfx

    def alloc_sbuf_tensor(self, name, shape, dtype, **kw):
        return self._nc.alloc_sbuf_tensor(self._pfx + name, shape, dtype, **kw)

    def alloc_sbuf_tensor_at(self, name, shape, dtype, **kw):
        return self._nc.alloc_sbuf_tensor_at(self._pfx + name, shape, dtype, **kw)

    def __getattr__(self, a):
        return getattr(self._nc, a)


def phase_barrier(p):
    ctrs = [getattr(p, n) for n in SEM_NAMES]
    for eng in (p.te, p.ve, p.se, p.gp, p.sy):
        for c in ctrs:
            if c.v > 0:
                eng.wait_ge(c.h, c.v)


def strided_ap(t, offset, dims):
    base = t[:].ap[0]
    return AP(t[:].tensor, offset, [list(base)] + [list(d) for d in dims])


def make_l0_attn(nc, dram, pq, pfx="a_"):
    x_h = dram["x"]
    w_h = dram["wqkvT"]
    bq_h = dram["bq_act"]
    bk_h = dram["bk_act"]
    bv_h = dram["bv_row"]
    mk_h = dram["masks"]
    id_h = dram["ident"]
    s0_h = dram["s0"]
    wgu_h = dram.get("wgu")
    wqp_h = dram.get("wq_pre")
    _sb = nc.alloc_sbuf_tensor
    def alloc_sbuf(name, shape, dtype):
        return _sb(pfx + name, shape, dtype)
    nc = _NSWrap(nc, pfx)

    xt = [nc.alloc_sbuf_tensor(f"xt{i}", [128, D], F32) for i in range(2)]
    xnb = [nc.alloc_sbuf_tensor(f"xnb{i}", [128, D], BF16) for i in range(2)]
    xnT = nc.alloc_sbuf_tensor("xnT", [128, ND * S], BF16)
    wT = nc.alloc_sbuf_tensor("wT", [128, ND * 3 * D], BF16)
    qT = nc.alloc_sbuf_tensor("qT", [128, NP * S], BF16)
    kT = nc.alloc_sbuf_tensor("kT", [128, NP * S], BF16)
    vsb = nc.alloc_sbuf_tensor("vsb", [128, NT * H * 65], BF16)
    _wa = nc.lookup_mloc(wT).addr
    sTqe = nc.alloc_sbuf_tensor_at("sTqe", [65, NP * 512], F32, offset=_wa)
    sTqo = nc.alloc_sbuf_tensor_at("sTqo", [65, NP * 512], F32, offset=_wa + 16384)
    expA = nc.alloc_sbuf_tensor_at("expA", [128, 2 * 512], BF16, offset=_wa + 32768)
    expB = nc.alloc_sbuf_tensor_at("expB", [128, 2 * 512], BF16, offset=_wa + 34816)
    s0c = [nc.alloc_sbuf_tensor_at(f"s0c{i}", [128, H * 65], F32,
                                   offset=_wa + 36864 + i * 4160) for i in range(2)]
    rsum = nc.alloc_sbuf_tensor("rsum", [128, 16], F32)
    identf = nc.alloc_sbuf_tensor("identf", [128, 128], F32)
    _xa = nc.lookup_mloc(xnT).addr
    wrowA = [nc.alloc_sbuf_tensor_at(f"wrowA{i}", [128, D], F32,
                                     offset=_xa + i * 4096) for i in range(2)]
    q1A = nc.alloc_sbuf_tensor_at("q1A", [128, D], F32, offset=_xa + 8192)
    q2A = nc.alloc_sbuf_tensor_at("q2A", [128, D], F32, offset=_xa + 12288)
    qbfA = [nc.alloc_sbuf_tensor_at(f"qbfA{i}", [128, D], BF16,
                                    offset=_xa + 16384 + i * 2048) for i in range(2)]
    blkA = nc.alloc_sbuf_tensor_at("blkA", [128, 256], F32, offset=_xa + 20480)
    masks = nc.alloc_sbuf_tensor("masks_s", [128, 4 * 512], BF16)
    ident = nc.alloc_sbuf_tensor("ident_s", [128, 128], BF16)
    bqs = nc.alloc_sbuf_tensor("bqs", [128, 8], F32)
    bks = nc.alloc_sbuf_tensor("bks", [128, 8], F32)
    bvb = nc.alloc_sbuf_tensor("bvb", [128, D], F32)
    bvr = nc.alloc_sbuf_tensor("bvr", [1, D], BF16)
    onesb = nc.alloc_sbuf_tensor("onesb", [1, 128], BF16)
    onesf = nc.alloc_sbuf_tensor("onesf", [1, 64], F32)
    stats = nc.alloc_sbuf_tensor("stats", [128, 32], F32)
    rrow = nc.alloc_sbuf_tensor("rrow", [1, 512], F32)



    def narrative(p):
        for j in range(ND):
            p.gp.dma_start(out=wT[:, j * 3 * D:(j + 1) * 3 * D],
                           in_=w_h.ap()[j * 128:(j + 1) * 128, :]).then_inc(p.gpd.h, 16)
            p.gpd.bump(16)
        p.gp.dma_start(out=masks[:].rearrange("p (m f) -> p m f", m=4),
                       in_=mk_h.ap().rearrange("m p f -> p m f")).then_inc(p.gpd.h, 16)
        p.gpd.bump(16)
        p.gp.dma_start(out=ident[:], in_=id_h.ap()).then_inc(p.gpd.h, 16)
        p.gpd.bump(16)
        p.gp.dma_start(out=bqs[:], in_=bq_h.ap()).then_inc(p.gpd.h, 16)
        p.gpd.bump(16)
        p.gp.dma_start(out=bks[:], in_=bk_h.ap()).then_inc(p.gpd.h, 16)
        p.gpd.bump(16)
        p.gp.dma_start(out=bvr[:], in_=bv_h.ap()).then_inc(p.gpd.h, 16)
        p.gpd.bump(16)
        p.gp.dma_start(out=identf[:], in_=id_h.ap()).then_inc(p.gpd.h, 16)
        p.gpd.bump(16)
        gp_all = p.gpd.v

        p.ve.wait_ge(p.gpd.h, gp_all)
        p.ve.tensor_scalar(bqs[:], bqs[:], 0.125, None, ALU.mult)
        p.ve.memset(onesb[:], 1.0)
        p.ve.memset(onesf[:], 1.0)
        p.ve.memset(strided_ap(vsb, 64, [[65, NT * H], [1, 1]]), 1.0) \
            .then_inc(p.ved.h, 1)
        ve_init = p.ved.bump()

        sx = [p.sx0, p.sx1]
        se_xn = []
        ve_evac = []
        for i in range(NT):
            sl = i % 2
            if i >= 2:
                p.sy.wait_ge(p.sed.h, se_xn[i - 2])
            p.sy.dma_start(out=xt[sl][:], in_=x_h.ap()[i * 128:(i + 1) * 128, :]) \
                .then_inc(sx[sl].h, 16)
            xv = sx[sl].bump(16)

            p.ve.wait_ge(sx[sl].h, xv)
            if i >= 2:
                p.ve.wait_ge(p.sed.h, se_xn[i - 2])
            sb16 = 16 * sl
            st_ = stats[:, sb16:sb16 + 16]
            p.ve.bn_stats(st_[:, 0:6], xt[sl][:, 0:512])
            p.ve.bn_stats(st_[:, 6:12], xt[sl][:, 512:1024])
            p.ve.drain()
            p.ve.bn_aggr(st_[:, 12:14], st_[:, 0:12])
            p.ve.drain()
            p.ve.tensor_scalar(st_[:, 14:15], st_[:, 13:14], 1e-5, None, ALU.add) \
                .then_inc(p.ved.h, 1)
            ve_st = p.ved.bump()
            p.se.wait_ge(p.ved.h, ve_st)
            p.se.activation(st_[:, 15:16], st_[:, 14:15], AF.Sqrt).then_inc(p.sed.h, 1)
            se_sq = p.sed.bump()
            p.ve.wait_ge(p.sed.h, se_sq)
            p.ve.reciprocal(st_[:, 15:16], st_[:, 15:16])
            p.ve.drain()
            p.ve.scalar_tensor_tensor(st_[:, 12:13], st_[:, 12:13], -1.0,
                                      st_[:, 15:16], ALU.mult, ALU.mult) \
                .then_inc(p.ved.h, 1)
            ve_rs = p.ved.bump()
            p.se.wait_ge(p.ved.h, ve_rs)
            p.se.activation(xnb[sl][:], xt[sl][:], AF.Identity,
                            bias=st_[:, 12:13], scale=st_[:, 15:16]) \
                .then_inc(p.sed.h, 1)
            se_xn.append(p.sed.bump())

            bank = pq[i % 2]
            pb_bf = bank[:].bitcast(BF16)
            p.te.wait_ge(p.sed.h, se_xn[i])
            if i >= 2:
                p.te.wait_ge(p.ved.h, ve_evac[i - 2])
            if i == 0:
                p.te.wait_ge(p.gpd.h, gp_all)
            for j in range(ND):
                inst = p.te.transpose(pb_bf[:, j * 128:(j + 1) * 128],
                                      xnb[sl][:, j * 128:(j + 1) * 128], ident[:])
                if j == ND - 1:
                    inst.then_inc(p.ted.h, 1)
            te_tr = p.ted.bump()

            p.ve.wait_ge(p.ted.h, te_tr)
            p.ve.tensor_copy(
                strided_ap(xnT, i * 128, [[S, ND], [1, 128]]),
                pb_bf[:].rearrange("p (j c) -> p j c", j=ND)).then_inc(p.ved.h, 1)
            ve_evac.append(p.ved.bump())
        ve_xnT = ve_evac[-1]

        def xnT_c(j):
            return xnT[:, j * S:(j + 1) * S]

        def wT_c(j, c0, c1):
            return wT[:, j * 3 * D + c0: j * 3 * D + c1]

        p.te.wait_ge(p.ved.h, ve_xnT)
        se_qkv = []
        for which, dst, bias, scl in (("q", qT, bqs, 0.125), ("k", kT, bks, 1.0)):
            base = 0 if which == "q" else D
            for hp in range(NP):
                for qt in range(4):
                    bank = pq[len(se_qkv) % 2]
                    if len(se_qkv) >= 2:
                        p.te.wait_ge(p.sed.h, se_qkv[-2])
                    for j in range(ND):
                        inst = p.te.matmul(bank[:],
                                           wT_c(j, base + hp * 128, base + hp * 128 + 128),
                                           xnT_c(j)[:, qt * 512:(qt + 1) * 512],
                                           start=(j == 0), stop=(j == ND - 1))
                    inst.then_inc(p.ted.h, 1)
                    te_mm = p.ted.bump()
                    p.se.wait_ge(p.ted.h, te_mm)
                    if not se_qkv:
                        p.se.wait_ge(p.ved.h, ve_init)
                    p.se.activation(dst[:, hp * S + qt * 512: hp * S + (qt + 1) * 512],
                                    bank[:], AF.Identity, bias=bias[:, hp:hp + 1],
                                    scale=scl).then_inc(p.sed.h, 1)
                    se_qkv.append(p.sed.bump())

        ve_bvb = None
        for vh in range(2):
            bank = pq[2 + vh]
            p.te.wait_ge(p.ved.h, ve_init)
            p.te.matmul(bank[:], onesb[:], bvr[:, vh * 512:(vh + 1) * 512],
                        start=True, stop=True).then_inc(p.ted.h, 1)
            te_b = p.ted.bump()
            p.ve.wait_ge(p.ted.h, te_b)
            p.ve.tensor_copy(bvb[:, vh * 512:(vh + 1) * 512], bank[:]) \
                .then_inc(p.ved.h, 1)
            ve_bvb = p.ved.bump()

        ve_v = []
        for tc in range(NT):
            for vh in range(2):
                bank = pq[4 + (len(ve_v) % 2)]
                if len(ve_v) >= 2:
                    p.te.wait_ge(p.ved.h, ve_v[-2])
                for j in range(ND):
                    inst = p.te.matmul(bank[:],
                                       xnT_c(j)[:, tc * 128:(tc + 1) * 128],
                                       wT_c(j, 2 * D + vh * 512, 2 * D + (vh + 1) * 512),
                                       start=(j == 0), stop=(j == ND - 1))
                inst.then_inc(p.ted.h, 1)
                te_mm = p.ted.bump()
                p.ve.wait_ge(p.ted.h, te_mm)
                if not ve_v:
                    p.ve.wait_ge(p.ved.h, ve_bvb)
                dst = strided_ap(vsb, tc * H * 65 + vh * 8 * 65, [[65, 8], [1, 64]])
                p.ve.tensor_tensor(dst, bank[:].rearrange("p (h c) -> p h c", h=8),
                                   bvb[:, vh * 512:(vh + 1) * 512]
                                   .rearrange("p (h c) -> p h c", h=8),
                                   ALU.add).then_inc(p.ved.h, 1)
                ve_v.append(p.ved.bump())
        ve_vall = ve_v[-1]
        te_v_done = p.ted.v

        wq_state = {"u": 0, "head": {}, "pt": {}}

        def emit_wq_unit():
            u = wq_state["u"]
            if wgu_h is None or u >= len(WQ_UNITS):
                return
            wq_state["u"] = u + 1
            sl_ = u % 2
            if u < 2:
                p.gp.wait_ge(p.ted.h, te_v_done)
            else:
                p.gp.wait_ge(p.ved.h, wq_state["head"][u - 2])
            srow = WQ_UNITS[u]
            p.gp.dma_start(out=wrowA[sl_][:],
                           in_=wgu_h.ap()[srow:srow + 128, :]) \
                .then_inc(p.gw0.h, 16)
            gv = p.gw0.bump(16)
            st_ = [p.gw1, p.gw2][sl_]
            p.ve.wait_ge(p.gw0.h, gv)
            if u >= 2:
                p.ve.wait_ge(st_.h, st_.v)
            vh, pt = _emit_quant(p, qbfA[sl_][:], wrowA[sl_][:], q1A, q2A,
                                 blkA, D, boff=0,
                                 wait_pod=wq_state["pt"].get(u - 1))
            wq_state["head"][u] = vh
            wq_state["pt"][u] = pt
            p.gp.wait_ge(p.pod.h, pt)
            p.gp.dma_start(out=wqp_h.ap()[u * 128:(u + 1) * 128, :],
                           in_=qbfA[sl_][:]).then_inc(st_.h, 16)
            st_.bump(16)

        std_base = [p.std0.v, p.std1.v]
        se_exp_prev = None
        se_stq_prev = None
        ve_str_last = None
        ve_tail02 = None
        ve_tail13 = None
        te_str = None
        for qt in range(4):
            nkc = (qt + 1) * 4
            for hp in range(NP):
                se_exp = {}
                te_av = {}
                for kc in range(nkc):
                    sa_ = pq[kc % 2]
                    sb_ = pq[2 + kc % 2]
                    if kc == 0:
                        p.te.wait_ge(p.ved.h, ve_vall)
                        p.te.wait_ge(p.sed.h, se_qkv[-1])
                        if se_exp_prev is not None:
                            p.te.wait_ge(p.sed.h, se_exp_prev)
                        if hp == 0 and ve_tail02 is not None:
                            p.te.wait_ge(p.ved.h, ve_tail02)
                    if kc >= 2:
                        p.te.wait_ge(p.sed.h, se_exp[kc - 2])
                    p.te.matmul(sa_[:],
                                kT[0:64, hp * S + kc * 128: hp * S + kc * 128 + 128],
                                qT[0:64, hp * S + qt * 512: hp * S + (qt + 1) * 512],
                                start=True, stop=True, tile_position=(0, 0))
                    p.te.matmul(sb_[:],
                                kT[64:128, hp * S + kc * 128: hp * S + kc * 128 + 128],
                                qT[64:128, hp * S + qt * 512: hp * S + (qt + 1) * 512],
                                start=True, stop=True, tile_position=(64, 0)) \
                        .then_inc(p.ted.h, 1)
                    te_s = p.ted.bump()

                    esl = kc % 2
                    p.se.wait_ge(p.ted.h, te_s)
                    if kc >= 2:
                        p.se.wait_ge(p.ted.h, te_av[kc - 2])
                    p.se.activation(expA[:, esl * 512:(esl + 1) * 512], sa_[:], AF.Exp)
                    p.se.activation(expB[:, esl * 512:(esl + 1) * 512], sb_[:], AF.Exp) \
                        .then_inc(p.sed.h, 1)
                    se_exp[kc] = p.sed.bump()

                    masked = kc >= nkc - 4
                    if masked:
                        mi = kc - (nkc - 4)
                        p.ve.wait_ge(p.sed.h, se_exp[kc])
                        p.ve.tensor_tensor(expA[:, esl * 512:(esl + 1) * 512],
                                           expA[:, esl * 512:(esl + 1) * 512],
                                           masks[:, mi * 512:(mi + 1) * 512], ALU.mult)
                        p.ve.tensor_tensor(expB[:, esl * 512:(esl + 1) * 512],
                                           expB[:, esl * 512:(esl + 1) * 512],
                                           masks[:, mi * 512:(mi + 1) * 512], ALU.mult) \
                            .then_inc(p.ved.h, 1)
                        ve_mask = p.ved.bump()
                        p.te.wait_ge(p.ved.h, ve_mask)
                    else:
                        p.te.wait_ge(p.sed.h, se_exp[kc])
                    if kc == 0 and se_stq_prev is not None:
                        p.te.wait_ge(p.sed.h, se_stq_prev)
                    if kc == 0 and hp == 0 and ve_tail13 is not None:
                        p.te.wait_ge(p.ved.h, ve_tail13)
                    headA, headB = 2 * hp, 2 * hp + 1
                    vA = vsb[:, kc * H * 65 + headA * 65: kc * H * 65 + headA * 65 + 65]
                    vB = vsb[:, kc * H * 65 + headB * 65: kc * H * 65 + headB * 65 + 65]
                    p.te.matmul(pq[4][0:65, :], vA, expA[:, esl * 512:(esl + 1) * 512],
                                start=(kc == 0), stop=(kc == nkc - 1))
                    p.te.matmul(pq[5][0:65, :], vB, expB[:, esl * 512:(esl + 1) * 512],
                                start=(kc == 0), stop=(kc == nkc - 1)) \
                        .then_inc(p.ted.h, 1)
                    te_av[kc] = p.ted.bump()

                te_fin = te_av[nkc - 1]
                se_exp_prev = se_exp[nkc - 1]
                p.se.wait_ge(p.ted.h, te_fin)
                if hp == 0 and qt >= 1:
                    p.se.wait_ge(p.ted.h, te_str)
                cs = hp * 512
                p.se.activation(sTqe[:, cs:cs + 512], pq[4][0:65, :], AF.Identity)
                p.se.activation(sTqo[:, cs:cs + 512], pq[5][0:65, :], AF.Identity) \
                    .then_inc(p.sed.h, 1)
                se_stq_prev = p.sed.bump()
                emit_wq_unit()
                if qt in (1, 2):
                    emit_wq_unit()
            se_stq_all = se_stq_prev

            ve_tail = {}
            for tt in range(4):
                ssl = tt % 2
                b0 = 0 if tt % 2 == 0 else 4
                p.te.wait_ge(p.sed.h, se_stq_all)
                if tt >= 2 and ve_tail.get(tt - 2) is not None:
                    p.te.wait_ge(p.ved.h, ve_tail[tt - 2])
                for hp in range(NP):
                    b = b0 + hp // 4
                    c = (hp % 4) * 65
                    p.te.transpose(
                        pq[b][:, c:c + 65],
                        sTqe[:, hp * 512 + tt * 128: hp * 512 + (tt + 1) * 128],
                        identf[0:65, 0:65])
                    inst = p.te.transpose(
                        pq[b0 + 2 + hp // 4][:, c:c + 65],
                        sTqo[:, hp * 512 + tt * 128: hp * 512 + (tt + 1) * 128],
                        identf[0:65, 0:65])
                inst.then_inc(p.ted.h, 1)
                te_str = p.ted.bump()
                p.ve.wait_ge(p.ted.h, te_str)
                if not (qt == 0 and tt < 2):
                    std_ = [p.std0, p.std1][ssl]
                    p.ve.wait_ge(std_.h, std_base[ssl] + 16 * ((qt * 4 + tt) // 2))
                p.ve.tensor_copy(
                    strided_ap(s0c[ssl], 0, [[130, 4], [1, 65]]),
                    strided_ap(pq[b0 + 0], 0, [[65, 4], [1, 65]]))
                p.ve.tensor_copy(
                    strided_ap(s0c[ssl], 4 * 130, [[130, 4], [1, 65]]),
                    strided_ap(pq[b0 + 1], 0, [[65, 4], [1, 65]]))
                p.ve.tensor_copy(
                    strided_ap(s0c[ssl], 65, [[130, 4], [1, 65]]),
                    strided_ap(pq[b0 + 2], 0, [[65, 4], [1, 65]]))
                p.ve.tensor_copy(
                    strided_ap(s0c[ssl], 4 * 130 + 65, [[130, 4], [1, 65]]),
                    strided_ap(pq[b0 + 3], 0, [[65, 4], [1, 65]])) \
                    .then_inc(p.ved.h, 1)
                ve_tail[tt] = p.ved.bump()
                ve_str_last = ve_tail[tt]
                p.ve.drain()
                p.ve.reciprocal(rsum[:], strided_ap(s0c[ssl], 64, [[65, 16], [1, 1]]))
                p.ve.drain()
                p.ve.tensor_tensor(
                    strided_ap(s0c[ssl], 0, [[64, 16], [1, 64]]),
                    strided_ap(s0c[ssl], 0, [[65, 16], [1, 64]]),
                    AP(rsum[:].tensor, rsum[:].offset,
                       [list(rsum[:].ap[0]), [1, 16], [0, 64]]),
                    ALU.mult).then_inc(p.ved.h, 1)
                ve_tail[tt] = p.ved.bump()
                ve_str_last = ve_tail[tt]
                std_ = [p.std0, p.std1][ssl]
                p.sy.wait_ge(p.ved.h, ve_str_last)
                p.sy.dma_start(
                    out=s0_h.ap()[(qt * 4 + tt) * 128:(qt * 4 + tt + 1) * 128, :],
                    in_=s0c[ssl][:, 0:D]).then_inc(std_.h, 16)
                std_.bump(16)
            ve_tail02 = ve_tail[2]
            ve_tail13 = ve_tail[3]

        if wgu_h is not None:
            assert wq_state["u"] == len(WQ_UNITS), wq_state["u"]
        p.sy.wait_ge(p.std0.h, p.std0.v)
        p.sy.wait_ge(p.std1.h, p.std1.v)

    return narrative


def _emit_quant(p, out_bf, t0, t1, t2, blk, W, boff=0, wait_pod=None,
                gp_wait_ted=None):
    nb = W // 16
    bm, hs = blk[:, boff:boff + nb], blk[:, boff + nb:boff + 2 * nb]
    r3 = lambda ap: ap.rearrange("p (nb w) -> p nb w", w=16)
    b3 = lambda ap: AP(ap.tensor, ap.offset, [list(ap.ap[0]), list(ap.ap[1]), [0, 16]])
    t1 = t1[:][:, 0:W]
    t2 = t2[:][:, 0:W]
    if wait_pod:
        p.ve.wait_ge(p.pod.h, wait_pod)
    p.ve.tensor_reduce(bm.rearrange("p (nb one) -> p nb one", one=1), r3(t0),
                       axis=AX.X, op=ALU.max, apply_absolute_value=True)
    p.ve.drain()
    p.ve.tensor_scalar(bm[:], bm[:], 1.0 / 6.0, 1e-12, ALU.mult, ALU.max)
    p.ve.drain()
    p.ve.tensor_scalar(hs[:], bm[:], 0.5, None, ALU.mult)
    p.ve.reciprocal(bm[:], bm[:])
    p.ve.drain()
    p.ve.tensor_tensor(r3(t1), r3(t0), b3(bm[:]), ALU.mult)
    p.ve.drain()
    p.ve.tensor_scalar(t2.bitcast(I32), t1.bitcast(I32), 0x7F800000, None,
                       ALU.bitwise_and)
    p.ve.drain()
    p.ve.tensor_scalar(t2, t2, 1.0, None, ALU.max)
    p.ve.drain()
    p.ve.tensor_scalar(t2.bitcast(I32), t2.bitcast(I32), 0x7F800000, None,
                       ALU.bitwise_xor)
    p.ve.drain()
    p.ve.tensor_tensor(t1, t1, t2, ALU.mult)
    p.ve.drain()
    p.ve.tensor_scalar(t1, t1, 8388608.0, 8388608.0, ALU.add, ALU.subtract)
    p.ve.tensor_scalar(t2.bitcast(I32), t2.bitcast(I32), 0x7F800000, None,
                       ALU.bitwise_xor).then_inc(p.ved.h, 1)
    ved_head = p.ved.bump()
    p.gp.wait_ge(p.ved.h, ved_head)
    if gp_wait_ted:
        p.gp.wait_ge(p.ted.h, gp_wait_ted)
    p.gp.tensor_tensor(r3(t2), r3(t2), b3(hs[:]), ALU.mult) \
        .then_inc(p.pod.h, 1)
    p.gp.wait_ge(p.pod.h, p.pod.bump())
    p.gp.tensor_tensor(out_bf, t1, t2, ALU.mult).then_inc(p.pod.h, 1)
    pod_tail = p.pod.bump()
    return ved_head, pod_tail


def _emit_quant2(p, out_bfs, t0s, t1s, t2s, blk, boffs, W, wait_pod=None):
    nb = W // 16
    r3 = lambda ap: ap.rearrange("p (nb w) -> p nb w", w=16)
    b3 = lambda ap: AP(ap.tensor, ap.offset, [list(ap.ap[0]), list(ap.ap[1]), [0, 16]])
    if wait_pod:
        p.ve.wait_ge(p.pod.h, wait_pod)
    last = [0, 0]
    chains = []
    for c in range(2):
        bm = blk[:, boffs[c]:boffs[c] + nb]
        hs = blk[:, boffs[c] + nb:boffs[c] + 2 * nb]
        t0, t1, t2 = t0s[c], t1s[c][:, 0:W], t2s[c][:, 0:W]
        out = out_bfs[c]
        chains.append([
            lambda t0=t0, bm=bm: p.ve.tensor_reduce(
                bm.rearrange("p (nb one) -> p nb one", one=1), r3(t0),
                axis=AX.X, op=ALU.max, apply_absolute_value=True),
            lambda bm=bm: p.ve.tensor_scalar(bm[:], bm[:], 1.0 / 6.0, 1e-12,
                                             ALU.mult, ALU.max),
            lambda bm=bm, hs=hs: p.ve.tensor_scalar(hs[:], bm[:], 0.5, None,
                                                    ALU.mult),
            lambda bm=bm: p.ve.reciprocal(bm[:], bm[:]),
            lambda t0=t0, t1=t1, bm=bm: p.ve.tensor_tensor(
                r3(t1), r3(t0), b3(bm[:]), ALU.mult),
            lambda t1=t1, t2=t2: p.ve.tensor_scalar(
                t2.bitcast(I32), t1.bitcast(I32), 0x7F800000, None,
                ALU.bitwise_and),
            lambda t2=t2: p.ve.tensor_scalar(t2, t2, 1.0, None, ALU.max),
            lambda t2=t2: p.ve.tensor_scalar(
                t2.bitcast(I32), t2.bitcast(I32), 0x7F800000, None,
                ALU.bitwise_xor),
            lambda t1=t1, t2=t2: p.ve.tensor_tensor(t1, t1, t2, ALU.mult),
            lambda t1=t1: p.ve.tensor_scalar(t1, t1, 8388608.0, 8388608.0,
                                             ALU.add, ALU.subtract),
            lambda t2=t2: p.ve.tensor_scalar(
                t2.bitcast(I32), t2.bitcast(I32), 0x7F800000, None,
                ALU.bitwise_xor),
            lambda t2=t2, hs=hs: p.ve.tensor_tensor(r3(t2), r3(t2), b3(hs[:]),
                                                    ALU.mult),
            lambda t1=t1, t2=t2, out=out: p.ve.tensor_tensor(out, t1, t2,
                                                             ALU.mult),
        ])
    nst = len(chains[0])
    for s in range(nst):
        for c in range(2):
            if last[c]:
                p.ve.wait_ge(p.ved.h, last[c])
            chains[c][s]().then_inc(p.ved.h, 1)
            last[c] = p.ved.bump()
    return max(last)


def _emit_layernorm(p, xn_out, xtile, stt, eps=1e-5, out2=None):
    p.ve.bn_stats(stt[:, 0:6], xtile[:, 0:512])
    p.ve.bn_stats(stt[:, 6:12], xtile[:, 512:1024])
    p.ve.drain()
    p.ve.bn_aggr(stt[:, 12:14], stt[:, 0:12])
    p.ve.drain()
    p.ve.tensor_scalar(stt[:, 14:15], stt[:, 13:14], eps, None, ALU.add) \
        .then_inc(p.ved.h, 1)
    ve_va = p.ved.bump()
    p.se.wait_ge(p.ved.h, ve_va)
    p.se.activation(stt[:, 15:16], stt[:, 14:15], AF.Sqrt).then_inc(p.sed.h, 1)
    se_sq = p.sed.bump()
    p.ve.wait_ge(p.sed.h, se_sq)
    p.ve.reciprocal(stt[:, 15:16], stt[:, 15:16])
    p.ve.drain()
    p.ve.scalar_tensor_tensor(stt[:, 12:13], stt[:, 12:13], -1.0,
                              stt[:, 15:16], ALU.mult, ALU.mult) \
        .then_inc(p.ved.h, 1)
    ve_rs = p.ved.bump()
    p.se.wait_ge(p.ved.h, ve_rs)
    inst = p.se.activation(xn_out, xtile, AF.Identity,
                           bias=stt[:, 12:13], scale=stt[:, 15:16])
    if out2 is not None:
        inst = p.se.activation(out2, xtile, AF.Identity,
                               bias=stt[:, 12:13], scale=stt[:, 15:16])
    return inst


def make_l0_ffn(nc, dram, pq, pfx="b_"):
    x_h = dram["x"]
    s_h = dram["s0"]
    wgu_h = dram["wgu"]
    wd_h = dram["wd"]
    id_h = dram["ident"]
    x1_h = dram["x1"]
    wqp_h = dram.get("wq_pre")
    nc = _NSWrap(nc, pfx)

    NHC = FFN_I // 128
    SLICES = [(i * 512, min(512, FFN_I - i * 512)) for i in range((FFN_I + 511) // 512)]

    xt = [nc.alloc_sbuf_tensor(f"xt{i}", [128, D], F32) for i in range(2)]
    q0 = nc.alloc_sbuf_tensor("q0", [128, D], F32)
    q1 = nc.alloc_sbuf_tensor("q1", [128, D], F32)
    q2 = nc.alloc_sbuf_tensor("q2", [128, D], F32)
    q3 = nc.alloc_sbuf_tensor("q3", [128, D], F32)
    q4 = nc.alloc_sbuf_tensor("q4", [128, D], F32)
    qbf = nc.alloc_sbuf_tensor("qbf", [128, 2 * D], BF16)
    qbfB = nc.alloc_sbuf_tensor("qbfB", [128, 2 * D], BF16)
    blk = nc.alloc_sbuf_tensor("blk", [128, 544], F32)
    hqT = nc.alloc_sbuf_tensor("hqT", [128, NHC * S], BF16)
    ident = nc.alloc_sbuf_tensor("ident_s", [128, 128], BF16)
    gt = nc.alloc_sbuf_tensor("gt", [128, D], F32)
    st_ = nc.alloc_sbuf_tensor("st_", [128, D], F32)
    x1t = nc.alloc_sbuf_tensor("x1t", [128, D], F32)
    wdr = nc.alloc_sbuf_tensor_at("wdr", [128, FFN_I], F32,
                                  offset=nc.lookup_mloc(q3).addr)
    xqT = nc.alloc_sbuf_tensor("xqT", [128, ND * S], BF16)
    wgTs = nc.alloc_sbuf_tensor("wgTs", [128, ND * 512], BF16)
    wuTs = nc.alloc_sbuf_tensor("wuTs", [128, ND * 512], BF16)
    wrow = [nc.alloc_sbuf_tensor(f"wrow{i}", [128, D], F32) for i in range(2)]
    sg = nc.alloc_sbuf_tensor("sg", [128, 2 * 512], F32)
    hh = nc.alloc_sbuf_tensor("hh", [128, 2 * 512], F32)
    hb2 = [nc.alloc_sbuf_tensor(f"hb{i}", [128, 1024], BF16) for i in range(2)]
    _wa = nc.lookup_mloc(xqT).addr
    wdT = nc.alloc_sbuf_tensor_at("wdT", [128, NHC * D], BF16, offset=_wa)



    def narrative(p):
        p.gp.dma_start(out=ident[:], in_=id_h.ap()).then_inc(p.gpd.h, 16)
        gp_all = p.gpd.bump(16)
        sx = [p.sx0, p.sx1]
        sw = [p.sw0, p.sw1]
        stt = blk[:, 528:544]

        ve_q = []
        quv = []
        se_xn_prev = {}
        te_tr = []
        ve_evac = []
        for i in range(NT):
            sl = i % 2
            if i >= 2:
                p.sy.wait_ge(p.sed.h, se_xn_prev[i - 2])
            p.sy.dma_start(out=xt[sl][:], in_=x_h.ap()[i * 128:(i + 1) * 128, :]) \
                .then_inc(sx[sl].h, 16)
            xv = sx[sl].bump(16)

            p.ve.wait_ge(sx[sl].h, xv)
            _emit_layernorm(p, q0[:], xt[sl][:], stt).then_inc(p.sed.h, 1)
            se_xn = p.sed.bump()
            se_xn_prev[i] = se_xn
            p.ve.wait_ge(p.sed.h, se_xn)
            t12 = (q1, q2) if i % 2 == 0 else (q3, q4)
            vh, pt = _emit_quant(
                p, qbf[:, (i % 2) * D:(i % 2) * D + D], q0[:], t12[0], t12[1],
                blk, D, boff=(i % 2) * 256,
                wait_pod=(quv[i - 2][1] if i >= 2 else None),
                gp_wait_ted=(te_tr[i - 2] if i >= 2 else None))
            quv.append((vh, pt))
            ve_q.append(vh)

            bank = pq[4 + i % 2]
            pb_bf = bank[:].bitcast(BF16)
            p.te.wait_ge(p.pod.h, pt)
            if i == 0:
                p.te.wait_ge(p.gpd.h, gp_all)
            if i >= 2:
                p.te.wait_ge(p.sed.h, ve_evac[i - 2])
            for j in range(ND):
                inst = p.te.transpose(
                    pb_bf[:, j * 128:(j + 1) * 128],
                    qbf[:, (i % 2) * D + j * 128: (i % 2) * D + (j + 1) * 128],
                    ident[:])
            inst.then_inc(p.ted.h, 1)
            te_tr.append(p.ted.bump())
            p.se.wait_ge(p.ted.h, te_tr[i])
            p.se.activation(
                strided_ap(xqT, i * 128, [[S, ND], [1, 128]]),
                pb_bf[:].rearrange("p (j c) -> p j c", j=ND),
                AF.Identity, scale=1.0).then_inc(p.sed.h, 1)
            ve_evac.append(p.sed.bump())
        se_xqT = ve_evac[-1]

        wk = 0
        ve_wq = {}
        se_wtr_evac = {}
        te_wtr_prev = None
        te_wtr_hist = {}
        ve_hprev = {0: None, 1: None}
        se_htr_evac = None
        hq_pt = {}
        hq_te = {}
        hq_k = 0
        hq_pend = None
        te_htr = None
        slice_idx = -1

        def emit_htr(pend):
            nonlocal se_htr_evac, te_htr
            k_, pt_, tc0_, ntr_, hc0_ = pend
            bank = pq[6 + k_ % 2]
            pb_bf = bank[:].bitcast(BF16)
            hbs_ = hb2[k_ % 2]
            p.te.wait_ge(p.pod.h, pt_)
            if se_htr_evac is not None:
                p.te.wait_ge(p.sed.h, se_htr_evac)
            for c in range(2):
                for j in range(ntr_):
                    inst = p.te.transpose(
                        pb_bf[:, c * 512 + j * 128: c * 512 + (j + 1) * 128],
                        hbs_[:, c * 512 + j * 128: c * 512 + (j + 1) * 128],
                        ident[:])
            inst.then_inc(p.ted.h, 1)
            te_htr = p.ted.bump()
            hq_te[k_] = te_htr
            p.se.wait_ge(p.ted.h, te_htr)
            p.se.activation(
                strided_ap(hqT, hc0_ * S + tc0_ * 128, [[S, ntr_], [1, 128]]),
                AP(pb_bf.tensor, pb_bf.offset,
                   [list(pb_bf.ap[0]), [128, ntr_], [1, 128]]),
                AF.Identity, scale=1.0)
            p.se.activation(
                strided_ap(hqT, hc0_ * S + (tc0_ + 1) * 128, [[S, ntr_], [1, 128]]),
                AP(pb_bf.tensor, pb_bf.offset + 512,
                   [list(pb_bf.ap[0]), [128, ntr_], [1, 128]]),
                AF.Identity, scale=1.0).then_inc(p.sed.h, 1)
            se_htr_evac = p.sed.bump()
        pu = 0
        for (hs0, hw) in SLICES:
            slice_idx += 1
            pre = wqp_h is not None
            nrc = hw // 128
            for gu in range(2):
                for rp in range(nrc // 2):
                    row0 = gu * FFN_I + hs0 + rp * 256
                    if pre:
                        u0 = pu
                        pu += 2
                        pj = wk // 2
                        qb_ = [qbf, qbfB][pj % 2]
                        if pj == 0:
                            p.sy.wait_ge(p.gw1.h, p.gw1.v)
                            p.sy.wait_ge(p.gw2.h, p.gw2.v)
                            p.sy.wait_ge(p.ted.h, te_tr[-1])
                        elif pj >= 2:
                            p.sy.wait_ge(p.ted.h, te_wtr_hist[pj - 2])
                        sp_ = [p.sw2, p.sw3][pj % 2]
                        p.sy.dma_start(
                            out=qb_[:].rearrange("p (c d) -> p c d", c=2),
                            in_=wqp_h.ap()[u0 * 128:(u0 + 2) * 128, :]
                            .rearrange("(c p) d -> p c d", p=128)) \
                            .then_inc(sp_.h, 16)
                        wv = sp_.bump(16)
                        wk += 2
                        p.te.wait_ge(sp_.h, wv)
                    else:
                        qb_ = qbf
                        for c in range(2):
                            if wk >= 2 and ve_wq.get(wk - 2):
                                p.sy.wait_ge(p.ved.h, ve_wq[wk - 2])
                            p.sy.dma_start(out=wrow[c][:],
                                           in_=wgu_h.ap()[row0 + c * 128:
                                                          row0 + (c + 1) * 128, :]) \
                                .then_inc(sw[c].h, 16)
                            wv = sw[c].bump(16)
                            p.ve.wait_ge(sw[c].h, wv)
                            wk += 1
                        if te_wtr_prev is not None:
                            p.ve.wait_ge(p.ted.h, te_wtr_prev)
                        else:
                            p.ve.wait_ge(p.ted.h, te_tr[-1])
                            p.ve.wait_ge(p.pod.h, p.pod.v)
                        vq = _emit_quant2(
                            p, [qbf[:, 0:D], qbf[:, D:2 * D]],
                            [wrow[0][:], wrow[1][:]], [q1, q3], [q2, q4],
                            blk, [0, 256], D,
                            wait_pod=(p.pod.v if p.pod.v else None))
                        ve_wq[wk - 2] = vq
                        ve_wq[wk - 1] = vq
                        p.te.wait_ge(p.ved.h, vq)

                    if wk >= 4:
                        p.te.wait_ge(p.sed.h, se_wtr_evac[wk - 4])
                    elif wk == 2:
                        p.te.wait_ge(p.sed.h, se_xqT)
                    for c in range(2):
                        pb_bf = pq[4 + c][:].bitcast(BF16)
                        for j in range(ND):
                            inst = p.te.transpose(
                                pb_bf[:, j * 128:(j + 1) * 128],
                                qb_[:, c * D + j * 128: c * D + (j + 1) * 128],
                                ident[:])
                    inst.then_inc(p.ted.h, 1)
                    te_wtr_prev = p.ted.bump()
                    te_wtr_hist[wk // 2 - 1] = te_wtr_prev
                    dstT = wgTs if gu == 0 else wuTs
                    p.se.wait_ge(p.ted.h, te_wtr_prev)
                    p.se.activation(
                        strided_ap(dstT, rp * 256, [[512, ND], [1, 128]]),
                        pq[4][:].bitcast(BF16).rearrange("p (j c) -> p j c", j=ND),
                        AF.Identity, scale=1.0)
                    p.se.activation(
                        strided_ap(dstT, rp * 256 + 128, [[512, ND], [1, 128]]),
                        pq[5][:].bitcast(BF16).rearrange("p (j c) -> p j c", j=ND),
                        AF.Identity, scale=1.0).then_inc(p.sed.h, 1)
                    se_wtr_evac[wk - 2] = p.sed.bump()
            ve_wslice = se_wtr_evac[wk - 2]

            hc0 = hs0 // 128
            ntr = hw // 128
            for tc in range(NT):
                gb = pq[(tc % 2) * 2]
                ub = pq[(tc % 2) * 2 + 1]
                p.te.wait_ge(p.sed.h, ve_wslice)
                if tc == 0 and hs0 == 0:
                    p.te.wait_ge(p.sed.h, se_xqT)
                if ve_hprev[tc % 2] is not None:
                    p.te.wait_ge(p.ved.h, ve_hprev[tc % 2])
                for j in range(ND):
                    p.te.matmul(gb[:, 0:hw],
                                xqT[:, j * S + tc * 128: j * S + (tc + 1) * 128],
                                wgTs[:, j * 512: j * 512 + hw],
                                start=(j == 0), stop=(j == ND - 1))
                for j in range(ND):
                    inst = p.te.matmul(ub[:, 0:hw],
                                       xqT[:, j * S + tc * 128: j * S + (tc + 1) * 128],
                                       wuTs[:, j * 512: j * 512 + hw],
                                       start=(j == 0), stop=(j == ND - 1))
                inst.then_inc(p.ted.h, 1)
                te_gu = p.ted.bump()

                ssl = tc % 2
                p.se.wait_ge(p.ted.h, te_gu)
                p.se.activation(sg[:, ssl * 512: ssl * 512 + hw], gb[:, 0:hw],
                                AF.Silu).then_inc(p.sed.h, 1)
                se_sg = p.sed.bump()
                p.ve.wait_ge(p.sed.h, se_sg)
                p.ve.tensor_tensor(hh[:, ssl * 512: ssl * 512 + hw],
                                   sg[:, ssl * 512: ssl * 512 + hw], ub[:, 0:hw],
                                   ALU.mult).then_inc(p.ved.h, 1)
                ve_hprev[tc % 2] = p.ved.bump()
                if tc % 2 == 0:
                    continue
                p.ve.drain()
                par = hq_k % 2
                t12 = (q1, q2) if par == 0 else (q3, q4)
                wp = hq_pt.get(hq_k - 2)
                gwt = hq_te.get(hq_k - 2)
                hbs = hb2[par]
                if hw == 512:
                    vh, pt = _emit_quant(p, hbs[:, 0:1024], hh[:][:, 0:1024],
                                         t12[0], t12[1], blk, 1024,
                                         boff=par * 256, wait_pod=wp,
                                         gp_wait_ted=gwt)
                else:
                    wpa = hq_pt.get(hq_k - 1)
                    _emit_quant(p, hbs[:, 0:hw], hh[:][:, 0:hw],
                                q1, q2, blk, hw, boff=0,
                                wait_pod=(wpa - 2 if wpa else wp),
                                gp_wait_ted=gwt)
                    vh, pt = _emit_quant(p, hbs[:, 512:512 + hw],
                                         hh[:][:, 512:512 + hw],
                                         q3, q4, blk, hw, boff=256,
                                         wait_pod=wpa)
                hq_pt[hq_k] = pt
                pend_prev = hq_pend
                hq_pend = (hq_k, pt, tc - 1, ntr, hc0)
                hq_k += 1
                if pend_prev is not None:
                    emit_htr(pend_prev)
        if hq_pend is not None:
            emit_htr(hq_pend)
            hq_pend = None
        ve_hqT = se_htr_evac

        PIECES = [(0, 1024), (1024, 1024), (2048, 768)]
        se_wdtr_evac = {}
        wd_pt = {}
        te_wdtr = None
        pi_k = 0
        for rc in range(ND):
            if rc == 0:
                p.sy.wait_ge(sw[0].h, sw[0].v)
                p.sy.wait_ge(sw[1].h, sw[1].v)
                p.sy.wait_ge(p.ved.h, p.ved.v)
                p.sy.wait_ge(p.ted.h, p.ted.v)
                p.sy.wait_ge(p.pod.h, p.pod.v)
            if rc >= 1:
                p.sy.wait_ge(p.ved.h, ve_wdq)
            p.sy.dma_start(out=wdr[:], in_=wd_h.ap()[rc * 128:(rc + 1) * 128, :]) \
                .then_inc(sw[0].h, 16)
            wv = sw[0].bump(16)
            p.ve.wait_ge(sw[0].h, wv)
            if rc == 0:
                p.ve.wait_ge(p.pod.h, p.pod.v)
            for (c0, cw) in PIECES:
                vh, pt = _emit_quant(
                    p, hb2[0][:, 0:cw], wdr[:][:, c0:c0 + cw], q1, q2,
                    blk, cw, boff=0,
                    wait_pod=wd_pt.get(pi_k - 1),
                    gp_wait_ted=(te_wdtr if pi_k >= 1 else te_htr))
                wd_pt[pi_k] = pt
                ve_wdq = vh

                bank = pq[4 + pi_k % 2]
                pb_bf = bank[:].bitcast(BF16)
                p.te.wait_ge(p.pod.h, pt)
                if pi_k >= 2:
                    p.te.wait_ge(p.sed.h, se_wdtr_evac[pi_k - 2])
                npc = cw // 128
                for j in range(npc):
                    inst = p.te.transpose(pb_bf[:, j * 128:(j + 1) * 128],
                                          hb2[0][:, j * 128:(j + 1) * 128],
                                          ident[:])
                inst.then_inc(p.ted.h, 1)
                te_wdtr = p.ted.bump()
                p.se.wait_ge(p.ted.h, te_wdtr)
                p.se.activation(
                    strided_ap(wdT, (c0 // 128) * D + rc * 128, [[D, npc], [1, 128]]),
                    AP(pb_bf.tensor, pb_bf.offset,
                       [list(pb_bf.ap[0]), [128, npc], [1, 128]]),
                    AF.Identity, scale=1.0).then_inc(p.sed.h, 1)
                se_wdtr_evac[pi_k] = p.sed.bump()
                pi_k += 1
        ve_wdT = se_wdtr_evac[pi_k - 1]

        ve_upd = {}
        std_base = [p.std0.v, p.std1.v]
        for tc in range(NT):
            b0 = pq[(tc % 2) * 2]
            b1 = pq[(tc % 2) * 2 + 1]
            if tc == 0:
                p.te.wait_ge(p.sed.h, ve_wdT)
                p.te.wait_ge(p.sed.h, ve_hqT)
            if tc >= 2:
                p.te.wait_ge(p.ved.h, ve_upd[tc - 2])
            for j in range(NHC):
                p.te.matmul(b0[:], hqT[:, j * S + tc * 128: j * S + (tc + 1) * 128],
                            wdT[:, j * D: j * D + 512],
                            start=(j == 0), stop=(j == NHC - 1))
            for j in range(NHC):
                inst = p.te.matmul(b1[:],
                                   hqT[:, j * S + tc * 128: j * S + (tc + 1) * 128],
                                   wdT[:, j * D + 512: j * D + 1024],
                                   start=(j == 0), stop=(j == NHC - 1))
            inst.then_inc(p.ted.h, 1)
            te_dn = p.ted.bump()

            sl = tc % 2
            if tc < 2:
                p.sy.wait_ge(sx[sl].h, sx[sl].v)
            if tc >= 2:
                p.sy.wait_ge(p.ved.h, ve_upd[tc - 2])
            p.sy.dma_start(out=xt[sl][:], in_=x_h.ap()[tc * 128:(tc + 1) * 128, :]) \
                .then_inc(sx[sl].h, 16)
            sx[sl].bump(16)
            if tc >= 1:
                p.sy.wait_ge(p.ved.h, ve_upd[tc - 1])
            p.sy.dma_start(out=st_[:], in_=s_h.ap()[tc * 128:(tc + 1) * 128, :]) \
                .then_inc(sx[sl].h, 16)
            sv = sx[sl].bump(16)
            p.se.wait_ge(p.ted.h, te_dn)
            if tc >= 1:
                p.se.wait_ge(p.ved.h, ve_upd[tc - 1])
            p.se.activation(gt[:, 0:512], b0[:], AF.Silu)
            p.se.activation(gt[:, 512:1024], b1[:], AF.Silu).then_inc(p.sed.h, 1)
            se_g = p.sed.bump()
            p.ve.wait_ge(p.sed.h, se_g)
            p.ve.wait_ge(sx[sl].h, sv)
            if tc >= 1:
                std_p = [p.std0, p.std1][(tc - 1) % 2]
                p.ve.wait_ge(std_p.h, std_p.v)
            p.ve.tensor_tensor(gt[:], st_[:], gt[:], ALU.mult)
            p.ve.drain()
            p.ve.tensor_tensor(x1t[:], xt[sl][:], gt[:], ALU.add) \
                .then_inc(p.ved.h, 1)
            ve_upd[tc] = p.ved.bump()
            std_ = [p.std0, p.std1][tc % 2]
            p.sy.wait_ge(p.ved.h, ve_upd[tc])
            p.sy.dma_start(out=x1_h.ap()[tc * 128:(tc + 1) * 128, :], in_=x1t[:]) \
                .then_inc(std_.h, 16)
            std_.bump(16)

        p.sy.wait_ge(p.std0.h, p.std0.v)
        p.sy.wait_ge(p.std1.h, p.std1.v)

    return narrative


def make_l1_attn(nc, dram, pq, pfx="c_"):
    x_h = dram["x1"]
    w_h = dram["wabcdT"]
    ab_h = dram["ab_act"]
    ow_h = dram["outwT"]
    ob_h = dram["outb_row"]
    nw_h = dram["normw_row"]
    id_h = dram["ident"]
    s1_h = dram["s1"]
    xn1_h = dram["xn1"]
    nc = _NSWrap(nc, pfx)

    xt = [nc.alloc_sbuf_tensor(f"xt{i}", [128, D], F32) for i in range(2)]
    xn1t = [nc.alloc_sbuf_tensor(f"xn1t{i}", [128, D], F32) for i in range(2)]
    xn2b = [nc.alloc_sbuf_tensor(f"xn2b{i}", [128, D], BF16) for i in range(2)]
    tmp = nc.alloc_sbuf_tensor("tmp", [128, D], F32)
    xn2T = nc.alloc_sbuf_tensor("xn2T", [128, ND * S], BF16)
    hT = nc.alloc_sbuf_tensor("hT", [128, ND * S], BF16)
    wsl = [nc.alloc_sbuf_tensor(f"wsl{i}", [128, 4 * ND * 128], BF16) for i in range(2)]
    owT = nc.alloc_sbuf_tensor("owT", [128, ND * D], BF16)
    ab_sb = nc.alloc_sbuf_tensor("ab_sb", [128, 8], F32)
    nwb = nc.alloc_sbuf_tensor("nwb", [128, D], F32)
    obb = nc.alloc_sbuf_tensor("obb", [128, D], F32)
    nwr = nc.alloc_sbuf_tensor("nwr", [1, D], BF16)
    obr = nc.alloc_sbuf_tensor("obr", [1, D], BF16)
    onesb = nc.alloc_sbuf_tensor("onesb", [1, 128], BF16)
    ident = nc.alloc_sbuf_tensor("ident_s", [128, 128], BF16)
    stats = nc.alloc_sbuf_tensor("stats", [128, 32], F32)
    ga = nc.alloc_sbuf_tensor("ga", [128, S], F32)
    gb_ = nc.alloc_sbuf_tensor("gb_", [128, S], F32)
    gc = nc.alloc_sbuf_tensor("gc", [128, S], F32)
    gd = nc.alloc_sbuf_tensor("gd", [128, S], F32)
    hs_ = nc.alloc_sbuf_tensor("hs_", [128, S], F32)
    zcol = nc.alloc_sbuf_tensor("zcol", [128, 1], F32)
    s1t = [nc.alloc_sbuf_tensor(f"s1t{i}", [128, D], F32) for i in range(2)]



    def narrative(p):
        p.gp.dma_start(out=ident[:], in_=id_h.ap()).then_inc(p.gpd.h, 16)
        p.gpd.bump(16)
        p.gp.dma_start(out=ab_sb[:], in_=ab_h.ap()).then_inc(p.gpd.h, 16)
        p.gpd.bump(16)
        p.gp.dma_start(out=nwr[:], in_=nw_h.ap()).then_inc(p.gpd.h, 16)
        p.gpd.bump(16)
        p.gp.dma_start(out=obr[:], in_=ob_h.ap()).then_inc(p.gpd.h, 16)
        p.gpd.bump(16)
        for j in range(ND):
            p.gp.dma_start(out=owT[:, j * D:(j + 1) * D],
                           in_=ow_h.ap()[j * 128:(j + 1) * 128, :]) \
                .then_inc(p.gpd.h, 16)
            p.gpd.bump(16)
        gp_all = p.gpd.v

        p.ve.wait_ge(p.gpd.h, gp_all)
        p.ve.memset(onesb[:], 1.0)
        p.ve.memset(zcol[:], 0.0).then_inc(p.ved.h, 1)
        ve_init = p.ved.bump()

        se_bc = []
        for (dst, row) in ((nwb, nwr), (obb, obr)):
            for vh in range(2):
                bank = pq[6 + vh]
                p.te.wait_ge(p.ved.h, ve_init)
                if len(se_bc) >= 2:
                    p.te.wait_ge(p.sed.h, se_bc[-2])
                p.te.matmul(bank[:], onesb[:], row[:, vh * 512:(vh + 1) * 512],
                            start=True, stop=True).then_inc(p.ted.h, 1)
                te_b = p.ted.bump()
                p.se.wait_ge(p.ted.h, te_b)
                p.se.activation(dst[:, vh * 512:(vh + 1) * 512], bank[:],
                                AF.Identity, scale=1.0).then_inc(p.sed.h, 1)
                se_bc.append(p.sed.bump())
        se_bcast = se_bc[-1]

        sx = [p.sx0, p.sx1]
        stx = [p.stx0, p.stx1]
        stx_base = [p.stx0.v, p.stx1.v]
        se_xn = []
        ve_x2 = []
        te_tr = []
        ve_evac = []
        for i in range(NT):
            sl = i % 2
            if i >= 2:
                p.sy.wait_ge(p.ved.h, ve_x2[i - 2])
                p.sy.wait_ge(stx[sl].h, stx_base[sl] + 16 * (i // 2))
            p.sy.dma_start(out=xt[sl][:], in_=x_h.ap()[i * 128:(i + 1) * 128, :]) \
                .then_inc(sx[sl].h, 16)
            xv = sx[sl].bump(16)
            p.ve.wait_ge(sx[sl].h, xv)
            if i >= 2:
                p.ve.wait_ge(p.sed.h, se_xn[i - 2])
            stt = stats[:, 16 * sl:16 * sl + 16]
            _emit_layernorm(p, xn1t[sl][:], xt[sl][:], stt).then_inc(p.sed.h, 1)
            se_xn.append(p.sed.bump())
            p.sy.wait_ge(p.sed.h, se_xn[i])
            p.sy.dma_start(out=xn1_h.ap()[i * 128:(i + 1) * 128, :],
                           in_=xn1t[sl][:]).then_inc(stx[sl].h, 16)
            stx[sl].bump(16)
            p.se.wait_ge(p.sed.h, se_xn[i])
            p.se.activation(tmp[:], xn1t[sl][:], AF.Square,
                            accum_out=stt[:, 0:1]).then_inc(p.sed.h, 1)
            se_ssq = p.sed.bump()
            p.ve.wait_ge(p.sed.h, se_ssq)
            p.ve.tensor_scalar(stt[:, 1:2], stt[:, 0:1], 1.0 / D, 1e-5,
                               ALU.mult, ALU.add).then_inc(p.ved.h, 1)
            ve_ss = p.ved.bump()
            p.se.wait_ge(p.ved.h, ve_ss)
            p.se.activation(stt[:, 2:3], stt[:, 1:2], AF.Sqrt).then_inc(p.sed.h, 1)
            se_rm = p.sed.bump()
            p.ve.wait_ge(p.sed.h, se_rm)
            p.ve.reciprocal(stt[:, 2:3], stt[:, 2:3])
            p.ve.drain()
            if se_bcast is not None and i == 0:
                p.ve.wait_ge(p.sed.h, se_bcast)
            if i >= 2:
                p.ve.wait_ge(p.ted.h, te_tr[i - 2])
            p.ve.tensor_tensor(tmp[:], xn1t[sl][:], nwb[:], ALU.mult) \
                .then_inc(p.ved.h, 1)
            ve_t = p.ved.bump()
            p.se.wait_ge(p.ved.h, ve_t)
            p.se.activation(xn2b[sl][:], tmp[:], AF.Identity,
                            scale=stt[:, 2:3]).then_inc(p.sed.h, 1)
            se_x2 = p.sed.bump()
            p.ve.drain()
            p.ve.nop()
            ve_x2.append(p.ved.v)
            bank = pq[4 + i % 2]
            pb_bf = bank[:].bitcast(BF16)
            p.te.wait_ge(p.sed.h, se_x2)
            if i >= 2:
                p.te.wait_ge(p.ved.h, ve_evac[i - 2])
            for j in range(ND):
                inst = p.te.transpose(pb_bf[:, j * 128:(j + 1) * 128],
                                      xn2b[sl][:, j * 128:(j + 1) * 128], ident[:])
            inst.then_inc(p.ted.h, 1)
            te_tr.append(p.ted.bump())
            p.ve.wait_ge(p.ted.h, te_tr[i])
            p.ve.tensor_copy(
                strided_ap(xn2T, i * 128, [[S, ND], [1, 128]]),
                pb_bf[:].rearrange("p (j c) -> p j c", j=ND)).then_inc(p.ved.h, 1)
            ve_evac.append(p.ved.bump())
            ve_x2[i] = ve_evac[i]
        ve_xn2T = ve_evac[-1]

        sw = [p.gw0, p.gw1]
        ve_wl_prev = None
        ve_hT_evac = None
        se_gprev = None
        ve_dbprev = None
        for nk in range(ND):
            sl = nk % 2
            if nk >= 2:
                p.gp.wait_ge(p.ted.h, te_wused[nk - 2])
            src4 = w_h.ap().rearrange("(j pp) (w n) -> pp w j n", pp=128, w=4) \
                [:, :, :, nk * 128:(nk + 1) * 128]
            p.gp.dma_start(
                out=wsl[sl][:].rearrange("p (w j n) -> p w j n", w=4, j=ND),
                in_=src4).then_inc(sw[sl].h, 16)
            wv = sw[sl].bump(16)

            p.te.wait_ge(sw[sl].h, wv)
            if nk == 0:
                p.te.wait_ge(p.ved.h, ve_xn2T)
            gbuf = {0: ga, 1: gb_, 2: gc, 3: gd}
            for w in range(4):
                for ch in range(4):
                    bank = pq[(w * 4 + ch) % 2]
                    if not (nk == 0 and w == 0 and ch < 2):
                        p.te.wait_ge(p.sed.h, se_ev[(w * 4 + ch - 2) % 100]
                                     if False else se_gprev2)
                    for j in range(ND):
                        inst = p.te.matmul(
                            bank[:],
                            wsl[sl][:, (w * ND + j) * 128:(w * ND + j + 1) * 128],
                            xn2T[:, j * S + ch * 512: j * S + (ch + 1) * 512],
                            start=(j == 0), stop=(j == ND - 1))
                    inst.then_inc(p.ted.h, 1)
                    te_mm = p.ted.bump()
                    p.se.wait_ge(p.ted.h, te_mm)
                    if w == 0:
                        p.se.wait_ge(p.ved.h, ve_dbprev) if ve_dbprev else None
                        p.se.activation(ga[:, ch * 512:(ch + 1) * 512], bank[:],
                                        AF.Sigmoid, bias=ab_sb[:, nk:nk + 1]) \
                            .then_inc(p.sed.h, 1)
                    elif w == 1:
                        p.se.activation(gb_[:, ch * 512:(ch + 1) * 512], bank[:],
                                        AF.Identity, scale=1.0).then_inc(p.sed.h, 1)
                    elif w == 2:
                        p.se.activation(gc[:, ch * 512:(ch + 1) * 512], bank[:],
                                        AF.Sigmoid).then_inc(p.sed.h, 1)
                    else:
                        p.se.activation(gd[:, ch * 512:(ch + 1) * 512], bank[:],
                                        AF.Sigmoid).then_inc(p.sed.h, 1)
                    se_gprev2 = p.sed.bump()
            te_wused = {} if nk == 0 else te_wused
            te_wused[nk] = p.ted.v
            se_abcd = se_gprev2

            p.ve.wait_ge(p.sed.h, se_abcd)
            p.ve.tensor_tensor(gb_[:], gd[:], gb_[:], ALU.mult)
            p.ve.drain()
            p.ve.tensor_tensor_scan(hs_[:], ga[:], gb_[:], zcol[:],
                                    ALU.mult, ALU.add)
            p.ve.drain()
            p.ve.tensor_tensor(hT[:, nk * S:(nk + 1) * S], gc[:], hs_[:],
                               ALU.mult).then_inc(p.ved.h, 1)
            ve_db = p.ved.bump()
            ve_dbprev = ve_db
        ve_hT = ve_dbprev

        std_base2 = [p.std0.v, p.std1.v]
        for tc in range(NT):
            sl = tc % 2
            b0 = pq[2 + (tc % 2) * 2]
            b1 = pq[2 + (tc % 2) * 2 + 1]
            if tc == 0:
                p.te.wait_ge(p.ved.h, ve_hT)
            if tc >= 2:
                p.te.wait_ge(p.ved.h, ve_s1[tc - 2])
            for half in range(2):
                bk = b0 if half == 0 else b1
                for j in range(ND):
                    inst = p.te.matmul(
                        bk[:], hT[:, j * S + tc * 128: j * S + (tc + 1) * 128],
                        owT[:, j * D + half * 512: j * D + (half + 1) * 512],
                        start=(j == 0), stop=(j == ND - 1))
            inst.then_inc(p.ted.h, 1)
            te_y = p.ted.bump()
            if tc < 2:
                p.sy.wait_ge(sx[sl].h, sx[sl].v)
                p.sy.wait_ge(p.stx0.h, p.stx0.v)
                p.sy.wait_ge(p.stx1.h, p.stx1.v)
            if tc >= 2:
                p.sy.wait_ge(p.ved.h, ve_s1[tc - 2])
            p.sy.dma_start(out=xt[sl][:], in_=xn1_h.ap()[tc * 128:(tc + 1) * 128, :]) \
                .then_inc(sx[sl].h, 16)
            xv = sx[sl].bump(16)
            p.ve.wait_ge(p.ted.h, te_y)
            p.ve.wait_ge(sx[sl].h, xv)
            if tc >= 2:
                gsl = tc % 2
                std_g = [p.std0, p.std1][gsl]
                p.ve.wait_ge(std_g.h, std_base2[gsl] + 16 * (tc // 2))
            p.ve.tensor_tensor(s1t[sl][:, 0:512], b0[:], obb[:, 0:512], ALU.add)
            p.ve.tensor_tensor(s1t[sl][:, 512:1024], b1[:], obb[:, 512:1024], ALU.add)
            p.ve.drain()
            p.ve.tensor_tensor(s1t[sl][:], s1t[sl][:], xt[sl][:], ALU.add) \
                .then_inc(p.ved.h, 1)
            if tc == 0:
                ve_s1 = {}
            ve_s1[tc] = p.ved.bump()
            std_ = [p.std0, p.std1][tc % 2]
            p.sy.wait_ge(p.ved.h, ve_s1[tc])
            p.sy.dma_start(out=s1_h.ap()[tc * 128:(tc + 1) * 128, :], in_=s1t[sl][:]) \
                .then_inc(std_.h, 16)
            std_.bump(16)

        p.sy.wait_ge(p.std0.h, p.std0.v)
        p.sy.wait_ge(p.std1.h, p.std1.v)
        p.sy.wait_ge(p.stx0.h, p.stx0.v)
        p.sy.wait_ge(p.stx1.h, p.stx1.v)

    return narrative


def make_l1_ffn(nc, dram, pq, pfx="d_"):
    x_h = dram["x1"]
    xn_h = dram["xn1"]
    s_h = dram["s1"]
    nw_h = dram["normw2_row"]
    gw_h = dram["gwT"]
    w1_h = dram["w1T"]
    w2_h = dram["w2T"]
    b1_h = dram["b1_act"]
    b2_h = dram["b2_row"]
    id_h = dram["ident"]
    o_h = dram["out"]
    nc = _NSWrap(nc, pfx)

    NHC2 = MOE_H // 128

    xt = [nc.alloc_sbuf_tensor(f"xt{i}", [128, D], F32) for i in range(2)]
    tmp = nc.alloc_sbuf_tensor("tmp", [128, D], F32)
    tb = [nc.alloc_sbuf_tensor(f"tb{i}", [128, D], BF16) for i in range(2)]
    tT = nc.alloc_sbuf_tensor("tT", [128, ND * S], BF16)
    acc = nc.alloc_sbuf_tensor("acc", [128, NT * D], F32)
    hTh = nc.alloc_sbuf_tensor("hTh", [128, NHC2 * 512], BF16)
    w1e = nc.alloc_sbuf_tensor("w1e", [128, ND * MOE_H], BF16)
    w2e = nc.alloc_sbuf_tensor("w2e", [128, NHC2 * D], BF16)
    gwTs = nc.alloc_sbuf_tensor("gwTs", [128, ND * NE], BF16)
    b1s = nc.alloc_sbuf_tensor("b1s", [128, 128], F32)
    b2r = nc.alloc_sbuf_tensor("b2r", [1, D], BF16)
    b2b = nc.alloc_sbuf_tensor("b2b", [128, D], F32)
    nwb = nc.alloc_sbuf_tensor("nwb", [128, D], F32)
    nwr = nc.alloc_sbuf_tensor("nwr", [1, D], BF16)
    onesb = nc.alloc_sbuf_tensor("onesb", [1, 128], BF16)
    ident = nc.alloc_sbuf_tensor("ident_s", [128, 128], BF16)
    stats = nc.alloc_sbuf_tensor("stats", [128, 32], F32)
    gl = nc.alloc_sbuf_tensor("gl", [128, NT * NE], F32)
    comb = nc.alloc_sbuf_tensor("comb", [128, NT * NE], F32)
    gsc = nc.alloc_sbuf_tensor("gsc", [128, 16], F32)
    _ha = nc.lookup_mloc(hTh).addr
    s1t = nc.alloc_sbuf_tensor_at("s1t", [128, D], F32, offset=_ha)
    ot = nc.alloc_sbuf_tensor_at("ot", [128, D], F32, offset=_ha + 4096)



    def narrative(p):
        p.gp.dma_start(out=ident[:], in_=id_h.ap()).then_inc(p.gpd.h, 16)
        p.gpd.bump(16)
        p.gp.dma_start(out=nwr[:], in_=nw_h.ap()).then_inc(p.gpd.h, 16)
        p.gpd.bump(16)

        p.gp.dma_start(out=b1s[:], in_=b1_h.ap()).then_inc(p.gpd.h, 16)
        p.gpd.bump(16)
        p.gp.dma_start(out=gwTs[:].rearrange("p (j e) -> p j e", j=ND),
                       in_=gw_h.ap().rearrange("(j p) e -> p j e", p=128)) \
            .then_inc(p.gpd.h, 16)
        p.gpd.bump(16)
        gp_all = p.gpd.v
        p.sy.dma_start(out=acc[:].rearrange("p (tc d) -> p tc d", tc=NT),
                       in_=xn_h.ap().rearrange("(tc p) d -> p tc d", p=128)) \
            .then_inc(p.sac.h, 16)
        p.sac.bump(16)

        p.ve.wait_ge(p.gpd.h, gp_all)
        p.ve.memset(onesb[:], 1.0).then_inc(p.ved.h, 1)
        ve_init = p.ved.bump()

        se_bc = []
        for vh in range(2):
            bank = pq[6 + vh]
            p.te.wait_ge(p.ved.h, ve_init)
            p.te.matmul(bank[:], onesb[:], nwr[:, vh * 512:(vh + 1) * 512],
                        start=True, stop=True).then_inc(p.ted.h, 1)
            te_b = p.ted.bump()
            p.se.wait_ge(p.ted.h, te_b)
            p.se.activation(nwb[:, vh * 512:(vh + 1) * 512], bank[:],
                            AF.Identity, scale=1.0).then_inc(p.sed.h, 1)
            se_bc.append(p.sed.bump())
        se_bcast = se_bc[-1]

        sx = [p.sx0, p.sx1]
        se_t = []
        te_tr = []
        ve_evac = []
        for i in range(NT):
            sl = i % 2
            if i >= 2:
                p.sy.wait_ge(p.sed.h, se_t[i - 2])
            p.sy.dma_start(out=xt[sl][:], in_=xn_h.ap()[i * 128:(i + 1) * 128, :]) \
                .then_inc(sx[sl].h, 16)
            xv = sx[sl].bump(16)
            stt = stats[:, 16 * sl:16 * sl + 16]
            p.se.wait_ge(sx[sl].h, xv)
            if i >= 1:
                p.se.wait_ge(p.sed.h, se_t[i - 1])
            if i >= 2:
                p.se.wait_ge(p.ved.h, ve_evac[i - 2])
            p.se.activation(tmp[:], xt[sl][:], AF.Square,
                            accum_out=stt[:, 0:1]).then_inc(p.sed.h, 1)
            se_sq = p.sed.bump()
            p.ve.wait_ge(p.sed.h, se_sq)
            p.ve.tensor_scalar(stt[:, 1:2], stt[:, 0:1], 1.0 / D, 1e-6,
                               ALU.mult, ALU.add).then_inc(p.ved.h, 1)
            ve_ss = p.ved.bump()
            p.se.wait_ge(p.ved.h, ve_ss)
            p.se.activation(stt[:, 2:3], stt[:, 1:2], AF.Sqrt).then_inc(p.sed.h, 1)
            se_rm = p.sed.bump()
            p.ve.wait_ge(p.sed.h, se_rm)
            p.ve.reciprocal(stt[:, 2:3], stt[:, 2:3])
            p.ve.drain()
            if i == 0:
                p.ve.wait_ge(p.sed.h, se_bcast)
            p.ve.tensor_tensor(tmp[:], xt[sl][:], nwb[:], ALU.mult) \
                .then_inc(p.ved.h, 1)
            ve_t = p.ved.bump()
            p.se.wait_ge(p.ved.h, ve_t)
            if i >= 2:
                p.se.wait_ge(p.ted.h, te_tr[i - 2])
            p.se.activation(tb[sl][:], tmp[:], AF.Identity,
                            scale=stt[:, 2:3]).then_inc(p.sed.h, 1)
            se_t.append(p.sed.bump())

            bank = pq[4 + i % 2]
            pb_bf = bank[:].bitcast(BF16)
            p.te.wait_ge(p.sed.h, se_t[i])
            if i == 0:
                p.te.wait_ge(p.gpd.h, gp_all)
            if i >= 2:
                p.te.wait_ge(p.ved.h, ve_evac[i - 2])
            for j in range(ND):
                inst = p.te.transpose(pb_bf[:, j * 128:(j + 1) * 128],
                                      tb[sl][:, j * 128:(j + 1) * 128], ident[:])
            inst.then_inc(p.ted.h, 1)
            te_tr.append(p.ted.bump())
            p.ve.wait_ge(p.ted.h, te_tr[i])
            p.ve.tensor_copy(
                strided_ap(tT, i * 128, [[S, ND], [1, 128]]),
                pb_bf[:].rearrange("p (j c) -> p j c", j=ND)).then_inc(p.ved.h, 1)
            ve_evac.append(p.ved.bump())
        ve_tT = ve_evac[-1]

        se_lg = {}
        for tc in range(NT):
            bank = pq[tc % 2]
            p.te.wait_ge(p.ved.h, ve_tT)
            if tc >= 2:
                p.te.wait_ge(p.sed.h, se_lg[tc - 2])
            for j in range(ND):
                inst = p.te.matmul(bank[:, 0:NE],
                                   tT[:, j * S + tc * 128: j * S + (tc + 1) * 128],
                                   gwTs[:, j * NE:(j + 1) * NE],
                                   start=(j == 0), stop=(j == ND - 1))
            inst.then_inc(p.ted.h, 1)
            te_g = p.ted.bump()
            p.se.wait_ge(p.ted.h, te_g)
            p.se.activation(gl[:, tc * NE:(tc + 1) * NE], bank[:, 0:NE],
                            AF.Identity, scale=1.0).then_inc(p.sed.h, 1)
            se_lg[tc] = p.sed.bump()
        se_logits = se_lg[NT - 1]

        ve_cb = {}
        for tc in range(NT):
            g8 = gl[:, tc * NE:(tc + 1) * NE]
            c8 = comb[:, tc * NE:(tc + 1) * NE]
            p.ve.wait_ge(p.sed.h, se_lg[tc])
            p.ve.tensor_reduce(gsc[:, 0:1], g8, axis=AX.X, op=ALU.max)
            p.ve.drain()
            p.ve.tensor_scalar(c8, g8, gsc[:, 0:1], 1e30, ALU.is_equal, ALU.mult)
            p.ve.drain()
            p.ve.tensor_tensor(c8, g8, c8, ALU.subtract)
            p.ve.drain()
            p.ve.tensor_reduce(gsc[:, 1:2], c8, axis=AX.X, op=ALU.max)
            p.ve.drain()
            p.ve.tensor_scalar(c8, g8, gsc[:, 1:2], None, ALU.is_ge)
            p.ve.tensor_scalar(gsc[:, 2:3], gsc[:, 0:1], -1.0, None, ALU.mult) \
                .then_inc(p.ved.h, 1)
            ve_i = p.ved.bump()
            p.se.wait_ge(p.ved.h, ve_i)
            p.se.activation(gsc[:, 4:12], g8, AF.Exp, bias=gsc[:, 2:3]) \
                .then_inc(p.sed.h, 1)
            se_e = p.sed.bump()
            p.ve.wait_ge(p.sed.h, se_e)
            p.ve.tensor_tensor(c8, c8, gsc[:, 4:12], ALU.mult)
            p.ve.drain()
            p.ve.tensor_reduce(gsc[:, 12:13], c8, axis=AX.X, op=ALU.add)
            p.ve.drain()
            p.ve.reciprocal(gsc[:, 12:13], gsc[:, 12:13])
            p.ve.drain()
            p.ve.tensor_scalar(c8, c8, gsc[:, 12:13], None, ALU.mult) \
                .then_inc(p.ved.h, 1)
            ve_cb[tc] = p.ved.bump()
        ve_comb = ve_cb[NT - 1]

        sw1, sw2 = p.gw0, p.gw1
        se_hprev = {}
        ve_accprev = {}
        ve_b2used = None
        for e in range(NE):
            if e >= 1:
                p.gp.wait_ge(p.ted.h, te_a_done)
            p.gp.dma_start(out=w1e[:].rearrange("p (j h) -> p j h", j=ND),
                           in_=w1_h.ap()[e].rearrange("(j p) h -> p j h", p=128)) \
                .then_inc(sw1.h, 16)
            w1v = sw1.bump(16)
            if e >= 1:
                p.gp.wait_ge(p.ted.h, te_b_done)
            p.gp.dma_start(out=w2e[:].rearrange("p (j d) -> p j d", j=NHC2),
                           in_=w2_h.ap()[e].rearrange("(j p) d -> p j d", p=128)) \
                .then_inc(sw2.h, 16)
            w2v = sw2.bump(16)
            if e >= 1:
                p.gp.wait_ge(p.sed.h, se_b2prev)
            p.gp.dma_start(out=b2r[:], in_=b2_h.ap()[:, e * D:(e + 1) * D]) \
                .then_inc(p.sb2.h, 16)
            gb2v = p.sb2.bump(16)

            p.te.wait_ge(p.sb2.h, gb2v)
            if ve_b2used is not None:
                p.te.wait_ge(p.ved.h, ve_b2used)
            for vh in range(2):
                bank = pq[4 + vh]
                p.te.matmul(bank[:], onesb[:], b2r[:, vh * 512:(vh + 1) * 512],
                            start=True, stop=True).then_inc(p.ted.h, 1)
                te_b2 = p.ted.bump()
                p.se.wait_ge(p.ted.h, te_b2)
                p.se.activation(b2b[:, vh * 512:(vh + 1) * 512], bank[:],
                                AF.Identity, scale=1.0).then_inc(p.sed.h, 1)
                se_b2prev = p.sed.bump()

            for hf in range(4):
                p.te.wait_ge(sw1.h, w1v)
                if e == 0 and hf == 0:
                    p.te.wait_ge(p.ved.h, ve_tT)
                    p.te.wait_ge(p.sed.h, se_logits)
                for hc in range(NHC2):
                    bank = pq[hc % 4]
                    kk = se_hprev.get(hc % 4)
                    if kk is not None:
                        p.te.wait_ge(p.sed.h, kk)
                    for j in range(ND):
                        inst = p.te.matmul(
                            bank[:],
                            w1e[:, j * MOE_H + hc * 128: j * MOE_H + (hc + 1) * 128],
                            tT[:, j * S + hf * 512: j * S + (hf + 1) * 512],
                            start=(j == 0), stop=(j == ND - 1))
                    inst.then_inc(p.ted.h, 1)
                    te_h = p.ted.bump()
                    p.se.wait_ge(p.ted.h, te_h)
                    p.se.activation(
                        hTh[:, hc * 512:(hc + 1) * 512],
                        bank[:], AF.Gelu, bias=b1s[:, e * 16 + hc: e * 16 + hc + 1]) \
                        .then_inc(p.sed.h, 1)
                    se_hprev[hc % 4] = p.sed.bump()
                te_a_done = p.ted.v
                se_a_done = p.sed.v

                p.te.wait_ge(sw2.h, w2v)
                p.te.wait_ge(p.sed.h, se_a_done)
                for tr in range(4):
                    tc = hf * 4 + tr
                    for dh in range(2):
                        bank = pq[4 + (tr % 2) * 2 + dh]
                        kk = ve_accprev.get((tr % 2) * 2 + dh)
                        if kk is not None:
                            p.te.wait_ge(p.ved.h, kk)
                        for j in range(NHC2):
                            inst = p.te.matmul(
                                bank[:],
                                hTh[:, j * 512 + tr * 128: j * 512 + (tr + 1) * 128],
                                w2e[:, j * D + dh * 512: j * D + (dh + 1) * 512],
                                start=(j == 0), stop=(j == NHC2 - 1))
                        inst.then_inc(p.ted.h, 1)
                        te_o = p.ted.bump()
                        p.ve.wait_ge(p.ted.h, te_o)
                        if e == 0 and tc < 2:
                            p.ve.wait_ge(p.sac.h, p.sac.v)
                        if e == 0 and tc == 0 and dh == 0:
                            p.ve.wait_ge(p.ved.h, ve_comb)
                        if tc == 0 and dh == 0:
                            p.ve.wait_ge(p.sed.h, se_b2prev)
                        a_sl = acc[:, tc * D + dh * 512: tc * D + (dh + 1) * 512]
                        ce = comb[:, tc * NE + e: tc * NE + e + 1]
                        p.ve.scalar_tensor_tensor(a_sl, bank[:], ce, a_sl,
                                                  ALU.mult, ALU.add)
                        p.ve.drain()
                        p.ve.scalar_tensor_tensor(
                            a_sl, b2b[:, dh * 512:(dh + 1) * 512], ce, a_sl,
                            ALU.mult, ALU.add).then_inc(p.ved.h, 1)
                        ve_accprev[(tr % 2) * 2 + dh] = p.ved.bump()
                        p.ve.drain()
                te_b_done = p.ted.v
            ve_b2used = p.ved.v

        ve_moe = p.ved.v

        std_base3 = [p.std0.v, p.std1.v]
        ve_fin = {}
        for tc in range(NT):
            sl = tc % 2
            if tc < 2:
                p.sy.wait_ge(sx[sl].h, sx[sl].v)
                p.sy.wait_ge(p.ved.h, ve_tT)
            if tc >= 2:
                p.sy.wait_ge(p.ved.h, ve_fin[tc - 2])
            p.sy.dma_start(out=xt[sl][:], in_=x_h.ap()[tc * 128:(tc + 1) * 128, :]) \
                .then_inc(sx[sl].h, 16)
            sx[sl].bump(16)
            if tc == 0:
                p.sy.wait_ge(p.ved.h, ve_moe)
            if tc >= 1:
                p.sy.wait_ge(p.ved.h, ve_fin[tc - 1])
            p.sy.dma_start(out=s1t[:], in_=s_h.ap()[tc * 128:(tc + 1) * 128, :]) \
                .then_inc(sx[sl].h, 16)
            sv = sx[sl].bump(16)
            p.se.wait_ge(p.ved.h, ve_moe)
            if tc >= 1:
                p.se.wait_ge(p.ved.h, ve_fin[tc - 1])
            p.se.activation(tmp[:], acc[:, tc * D:(tc + 1) * D], AF.Silu) \
                .then_inc(p.sed.h, 1)
            se_g = p.sed.bump()
            p.ve.wait_ge(p.sed.h, se_g)
            p.ve.wait_ge(sx[sl].h, sv)
            if tc >= 2:
                gsl = tc % 2
                std_g = [p.std0, p.std1][gsl]
                p.ve.wait_ge(std_g.h, std_base3[gsl] + 16 * (tc // 2))
            if tc >= 1:
                std_p = [p.std0, p.std1][(tc - 1) % 2]
                p.ve.wait_ge(std_p.h, std_p.v)
            p.ve.tensor_tensor(ot[:], s1t[:], tmp[:], ALU.mult)
            p.ve.drain()
            p.ve.tensor_tensor(ot[:], xt[sl][:], ot[:], ALU.add) \
                .then_inc(p.ved.h, 1)
            ve_fin[tc] = p.ved.bump()
            std_ = [p.std0, p.std1][tc % 2]
            p.sy.wait_ge(p.ved.h, ve_fin[tc])
            p.sy.dma_start(out=o_h.ap()[tc * 128:(tc + 1) * 128, :], in_=ot[:]) \
                .then_inc(std_.h, 16)
            std_.bump(16)

        p.sy.wait_ge(p.std0.h, p.std0.v)
        p.sy.wait_ge(p.std1.h, p.std1.v)

    return narrative


_EXT_INPUTS = [
    ("x", [S, D]), ("wqkvT", [D, 3 * D]), ("bq_act", [128, 8]),
    ("bk_act", [128, 8]), ("bv_row", [1, D]), ("masks", [4, 128, 512]),
    ("ident", [128, 128]), ("wgu", [2 * FFN_I, D]), ("wd", [D, FFN_I]),
    ("wabcdT", [D, 4 * D]), ("ab_act", [128, 8]), ("outwT", [D, D]),
    ("outb_row", [1, D]), ("normw_row", [1, D]), ("normw2_row", [1, D]),
    ("gwT", [D, NE]), ("w1T", [NE, D, MOE_H]), ("w2T", [NE, MOE_H, D]),
    ("b1_act", [128, 128]), ("b2_row", [1, NE * D]),
]
_INTERNALS = [("s0", [S, D]), ("x1", [S, D]), ("xn1", [S, D]), ("s1", [S, D])]
def _wq_units():
    units = []
    for i in range((FFN_I + 511) // 512):
        hs0, hw = i * 512, min(512, FFN_I - i * 512)
        for gu in range(2):
            for rp in range((hw // 128) // 2):
                row0 = gu * FFN_I + hs0 + rp * 256
                units.append(row0)
                units.append(row0 + 128)
    return units


WQ_UNITS = _wq_units()

_INTERNALS_BF = [("wq_pre", [len(WQ_UNITS) * 128, D])]


def build_fused():
    nc = bass.Bass("TRN2", target_bir_lowering=False)
    dram = {}
    for n, sh in _EXT_INPUTS:
        dram[n] = nc.dram_tensor(n, sh, F32, kind="ExternalInput")
    for n, sh in _INTERNALS:
        dram[n] = nc.dram_tensor(n, sh, F32, kind="Internal")
    for n, sh in _INTERNALS_BF:
        dram[n] = nc.dram_tensor(n, sh, BF16, kind="Internal")
    dram["out"] = nc.dram_tensor("out", [S, D], F32, kind="ExternalOutput")
    pq = [nc.alloc_psum_tensor(f"pq{i}", [128, 512], F32) for i in range(8)]
    sems = alloc_sems(nc, SEM_NAMES)
    base = nc.sbuf_base
    nc.sbuf_base = base
    narrA = make_l0_attn(nc, dram, pq)
    nc.sbuf_base = base
    narrB = make_l0_ffn(nc, dram, pq)
    nc.sbuf_base = base
    narrC = make_l1_attn(nc, dram, pq)
    nc.sbuf_base = base
    narrD = make_l1_ffn(nc, dram, pq)

    def narrative(p):
        narrA(p)
        phase_barrier(p)
        narrB(p)
        phase_barrier(p)
        narrC(p)
        phase_barrier(p)
        narrD(p)

    run_block(nc, sems, narrative)
    return nc


def _build_single(make_fn, in_names, internal_names, out_names):
    nc = bass.Bass("TRN2", target_bir_lowering=False)
    shapes = dict(_EXT_INPUTS + _INTERNALS + [("out", [S, D])])
    dram = {}
    for n in in_names:
        dram[n] = nc.dram_tensor(n, shapes[n], F32, kind="ExternalInput")
    for n in internal_names:
        dram[n] = nc.dram_tensor(n, shapes[n], F32, kind="Internal")
    for n in out_names:
        dram[n] = nc.dram_tensor(n, shapes[n], F32, kind="ExternalOutput")
    pq = [nc.alloc_psum_tensor(f"pq{i}", [128, 512], F32) for i in range(8)]
    sems = alloc_sems(nc, SEM_NAMES)
    run_block(nc, sems, make_fn(nc, dram, pq))
    return nc


def build_l0_attn():
    return _build_single(make_l0_attn,
                         ["x", "wqkvT", "bq_act", "bk_act", "bv_row", "masks",
                          "ident"], [], ["s0"])


def build_l0_ffn():
    return _build_single(make_l0_ffn,
                         ["x", "s0", "wgu", "wd", "ident"], [], ["x1"])


def build_l1_attn():
    return _build_single(make_l1_attn,
                         ["x1", "wabcdT", "ab_act", "outwT", "outb_row",
                          "normw_row", "ident"], [], ["s1", "xn1"])


def build_l1_ffn():
    return _build_single(make_l1_ffn,
                         ["x1", "xn1", "s1", "normw2_row", "gwT", "w1T", "w2T",
                          "b1_act", "b2_row", "ident"], [], ["out"])


def _prep_inputs(inputs):
    x = np.ascontiguousarray(np.asarray(inputs["L0_attn_x"], dtype=np.float32))
    Wq = np.asarray(inputs["L0_attn_Wq"], dtype=np.float32).reshape(D, D)
    Wk = np.asarray(inputs["L0_attn_Wk"], dtype=np.float32).reshape(D, D)
    Wv = np.asarray(inputs["L0_attn_Wv"], dtype=np.float32).reshape(D, D)
    wqkvT = np.ascontiguousarray(
        np.concatenate([Wq.T, Wk.T, Wv.T], axis=1).astype(np.float32))
    bq = np.asarray(inputs["L0_attn_bq"], dtype=np.float32).reshape(H * HD)
    bk = np.asarray(inputs["L0_attn_bk"], dtype=np.float32).reshape(H * HD)
    bv = np.asarray(inputs["L0_attn_bv"], dtype=np.float32).reshape(1, H * HD)
    m = np.arange(4)[:, None, None]
    pp = np.arange(128)[None, :, None]
    ff = np.arange(512)[None, None, :]
    shared = {
        "wqkvT": wqkvT,
        "bq_act": np.ascontiguousarray(bq.reshape(8, 128).T),
        "bk_act": np.ascontiguousarray(bk.reshape(8, 128).T),
        "bv_row": np.ascontiguousarray(bv),
        "masks": np.ascontiguousarray((ff >= 128 * m + pp).astype(np.float32)),
        "ident": np.eye(128, dtype=np.float32),
        "wgu": np.ascontiguousarray(np.concatenate(
            [np.asarray(inputs["L0_ffn_w_gate"], dtype=np.float32),
             np.asarray(inputs["L0_ffn_w_up"], dtype=np.float32)], axis=0)),
        "wd": np.ascontiguousarray(
            np.asarray(inputs["L0_ffn_w_down"], dtype=np.float32)),
        "wabcdT": np.ascontiguousarray(np.concatenate(
            [np.asarray(inputs["L1_attn_A_w"], dtype=np.float32).T,
             np.asarray(inputs["L1_attn_B_w"], dtype=np.float32).T,
             np.asarray(inputs["L1_attn_C_w"], dtype=np.float32).T,
             np.asarray(inputs["L1_attn_D_w"], dtype=np.float32).T], axis=1)),
        "ab_act": np.ascontiguousarray(
            np.asarray(inputs["L1_attn_A_b"], dtype=np.float32).reshape(8, 128).T),
        "outwT": np.ascontiguousarray(
            np.asarray(inputs["L1_attn_out_w"], dtype=np.float32).T),
        "outb_row": np.ascontiguousarray(
            np.asarray(inputs["L1_attn_out_b"], dtype=np.float32).reshape(1, D)),
        "normw_row": np.ascontiguousarray(
            np.asarray(inputs["L1_attn_norm_w"], dtype=np.float32).reshape(1, D)),
        "normw2_row": np.ascontiguousarray(
            np.asarray(inputs["L1_ffn_norm_weight"],
                       dtype=np.float32).reshape(1, D)),
        "gwT": np.ascontiguousarray(
            np.asarray(inputs["L1_ffn_gate_w"], dtype=np.float32).T),
        "w1T": np.ascontiguousarray(
            np.asarray(inputs["L1_ffn_w1"], dtype=np.float32).transpose(0, 2, 1)),
        "w2T": np.ascontiguousarray(
            np.asarray(inputs["L1_ffn_w2"], dtype=np.float32).transpose(0, 2, 1)),
        "b1_act": np.ascontiguousarray(
            np.asarray(inputs["L1_ffn_b1"], dtype=np.float32)
            .reshape(8, 16, 128).transpose(2, 0, 1).reshape(128, 128)),
        "b2_row": np.ascontiguousarray(
            np.asarray(inputs["L1_ffn_b2"], dtype=np.float32).reshape(1, NE * D)),
    }
    return shared, x


def kernel(**inputs):
    shared, x = _prep_inputs(inputs)
    nc = build_fused()
    maps = [dict(shared, x=np.ascontiguousarray(x[b])) for b in range(B)]
    res = run_bass_kernel_spmd(nc, maps, core_ids=list(range(B)))
    return np.stack([r["out"] for r in res.results])


# version 12
# speedup vs baseline: 1.0512x; 1.0512x over previous
import numpy as np
import ml_dtypes
import concourse.bass as bass
import concourse.mybir as mybir
from concourse.bass_types import AP
from concourse.bass_utils import run_bass_kernel_spmd

F32 = mybir.dt.float32
BF16 = mybir.dt.bfloat16
I32 = mybir.dt.int32
AF = mybir.ActivationFunctionType
ALU = mybir.AluOpType
AX = mybir.AxisListType

B, S, D = 8, 2048, 1024
NT = S // 128
ND = D // 128
H, HD = 16, 64
NP = H // 2
FFN_I = 2816
MOE_H = 2048
NE = 8


class _NullInst:
    def then_inc(self, *a, **k):
        return self


class _NullEng:
    def __getattr__(self, name):
        def f(*a, **k):
            return _NullInst()
        return f


class Ctr:
    def __init__(self, handle):
        self.h = handle
        self.v = 0

    def bump(self, n=1):
        self.v += n
        return self.v


class Prog:
    def __init__(self, which, eng, sems):
        null = _NullEng()
        self.which = which
        self.te = eng if which == "tensor" else null
        self.ve = eng if which == "vector" else null
        self.se = eng if which == "scalar" else null
        self.gp = eng if which == "gpsimd" else null
        self.sy = eng if which == "sync" else null
        for name, h in sems.items():
            setattr(self, name, Ctr(h))


def run_block(nc, sems, narrative):
    with nc.Block() as block:
        @block.sync
        def _(e):
            narrative(Prog("sync", e, sems))

        @block.gpsimd
        def _(e):
            narrative(Prog("gpsimd", e, sems))

        @block.tensor
        def _(e):
            narrative(Prog("tensor", e, sems))

        @block.vector
        def _(e):
            narrative(Prog("vector", e, sems))

        @block.scalar
        def _(e):
            narrative(Prog("scalar", e, sems))


def alloc_sems(nc, names):
    return {n: nc.semaphore(n).__enter__() for n in names}


SEM_NAMES = ["sx0", "sx1", "sw0", "sw1", "sw2", "sw3", "gw0", "gw1", "gw2",
             "gw3", "gpd", "sac", "sb2", "ted", "ved", "sed", "std0", "std1",
             "stx0", "stx1", "pod"]


class _NSWrap:

    def __init__(self, nc, pfx):
        self._nc = nc
        self._pfx = pfx

    def alloc_sbuf_tensor(self, name, shape, dtype, **kw):
        return self._nc.alloc_sbuf_tensor(self._pfx + name, shape, dtype, **kw)

    def alloc_sbuf_tensor_at(self, name, shape, dtype, **kw):
        return self._nc.alloc_sbuf_tensor_at(self._pfx + name, shape, dtype, **kw)

    def __getattr__(self, a):
        return getattr(self._nc, a)


def phase_barrier(p):
    ctrs = [getattr(p, n) for n in SEM_NAMES]
    for eng in (p.te, p.ve, p.se, p.gp, p.sy):
        for c in ctrs:
            if c.v > 0:
                eng.wait_ge(c.h, c.v)


def strided_ap(t, offset, dims):
    base = t[:].ap[0]
    return AP(t[:].tensor, offset, [list(base)] + [list(d) for d in dims])


def make_l0_attn(nc, dram, pq, pfx="a_"):
    x_h = dram["x"]
    w_h = dram["wqkvT"]
    bq_h = dram["bq_act"]
    bk_h = dram["bk_act"]
    bv_h = dram["bv_row"]
    mk_h = dram["masks"]
    id_h = dram["ident"]
    s0_h = dram["s0"]
    wgu_h = dram.get("wgu")
    wqp_h = dram.get("wq_pre")
    _sb = nc.alloc_sbuf_tensor
    def alloc_sbuf(name, shape, dtype):
        return _sb(pfx + name, shape, dtype)
    nc = _NSWrap(nc, pfx)

    xt = [nc.alloc_sbuf_tensor(f"xt{i}", [128, D], F32) for i in range(2)]
    xnb = [nc.alloc_sbuf_tensor(f"xnb{i}", [128, D], BF16) for i in range(2)]
    xnT = nc.alloc_sbuf_tensor("xnT", [128, ND * S], BF16)
    wT = nc.alloc_sbuf_tensor("wT", [128, ND * 3 * D], BF16)
    qT = nc.alloc_sbuf_tensor("qT", [128, NP * S], BF16)
    kT = nc.alloc_sbuf_tensor("kT", [128, NP * S], BF16)
    vsb = nc.alloc_sbuf_tensor("vsb", [128, NT * H * 65], BF16)
    _wa = nc.lookup_mloc(wT).addr
    sTqe = nc.alloc_sbuf_tensor_at("sTqe", [65, NP * 512], F32, offset=_wa)
    sTqo = nc.alloc_sbuf_tensor_at("sTqo", [65, NP * 512], F32, offset=_wa + 16384)
    expA = nc.alloc_sbuf_tensor_at("expA", [128, 2 * 512], BF16, offset=_wa + 32768)
    expB = nc.alloc_sbuf_tensor_at("expB", [128, 2 * 512], BF16, offset=_wa + 34816)
    s0c = [nc.alloc_sbuf_tensor_at(f"s0c{i}", [128, H * 65], F32,
                                   offset=_wa + 36864 + i * 4160) for i in range(2)]
    rsum = nc.alloc_sbuf_tensor("rsum", [128, 16], F32)
    identf = nc.alloc_sbuf_tensor("identf", [128, 128], F32)
    _xa = nc.lookup_mloc(xnT).addr
    wrowA = [nc.alloc_sbuf_tensor_at(f"wrowA{i}", [128, D], F32,
                                     offset=_xa + i * 4096) for i in range(2)]
    q1A = nc.alloc_sbuf_tensor_at("q1A", [128, D], F32, offset=_xa + 8192)
    q2A = nc.alloc_sbuf_tensor_at("q2A", [128, D], F32, offset=_xa + 12288)
    qbfA = [nc.alloc_sbuf_tensor_at(f"qbfA{i}", [128, D], BF16,
                                    offset=_xa + 16384 + i * 2048) for i in range(2)]
    blkA = nc.alloc_sbuf_tensor_at("blkA", [128, 256], F32, offset=_xa + 20480)
    masks = nc.alloc_sbuf_tensor("masks_s", [128, 4 * 512], BF16)
    ident = nc.alloc_sbuf_tensor("ident_s", [128, 128], BF16)
    bqs = nc.alloc_sbuf_tensor("bqs", [128, 8], F32)
    bks = nc.alloc_sbuf_tensor("bks", [128, 8], F32)
    bvb = nc.alloc_sbuf_tensor("bvb", [128, D], F32)
    bvr = nc.alloc_sbuf_tensor("bvr", [1, D], BF16)
    onesb = nc.alloc_sbuf_tensor("onesb", [1, 128], BF16)
    onesf = nc.alloc_sbuf_tensor("onesf", [1, 64], F32)
    stats = nc.alloc_sbuf_tensor("stats", [128, 32], F32)
    rrow = nc.alloc_sbuf_tensor("rrow", [1, 512], F32)



    def narrative(p):
        for j in range(ND):
            p.gp.dma_start(out=wT[:, j * 3 * D:(j + 1) * 3 * D],
                           in_=w_h.ap()[j * 128:(j + 1) * 128, :]).then_inc(p.gpd.h, 16)
            p.gpd.bump(16)
        p.gp.dma_start(out=masks[:].rearrange("p (m f) -> p m f", m=4),
                       in_=mk_h.ap().rearrange("m p f -> p m f")).then_inc(p.gpd.h, 16)
        p.gpd.bump(16)
        p.gp.dma_start(out=ident[:], in_=id_h.ap()).then_inc(p.gpd.h, 16)
        p.gpd.bump(16)
        p.gp.dma_start(out=bqs[:], in_=bq_h.ap()).then_inc(p.gpd.h, 16)
        p.gpd.bump(16)
        p.gp.dma_start(out=bks[:], in_=bk_h.ap()).then_inc(p.gpd.h, 16)
        p.gpd.bump(16)
        p.gp.dma_start(out=bvr[:], in_=bv_h.ap()).then_inc(p.gpd.h, 16)
        p.gpd.bump(16)
        p.gp.dma_start(out=identf[:], in_=id_h.ap()).then_inc(p.gpd.h, 16)
        p.gpd.bump(16)
        gp_all = p.gpd.v

        p.ve.wait_ge(p.gpd.h, gp_all)
        p.ve.tensor_scalar(bqs[:], bqs[:], 0.125, None, ALU.mult)
        p.ve.memset(onesb[:], 1.0)
        p.ve.memset(onesf[:], 1.0)
        p.ve.memset(strided_ap(vsb, 64, [[65, NT * H], [1, 1]]), 1.0) \
            .then_inc(p.ved.h, 1)
        ve_init = p.ved.bump()

        sx = [p.sx0, p.sx1]
        se_xn = []
        ve_evac = []
        for i in range(NT):
            sl = i % 2
            if i >= 2:
                p.sy.wait_ge(p.sed.h, se_xn[i - 2])
            p.sy.dma_start(out=xt[sl][:], in_=x_h.ap()[i * 128:(i + 1) * 128, :]) \
                .then_inc(sx[sl].h, 16)
            xv = sx[sl].bump(16)

            p.ve.wait_ge(sx[sl].h, xv)
            if i >= 2:
                p.ve.wait_ge(p.sed.h, se_xn[i - 2])
            sb16 = 16 * sl
            st_ = stats[:, sb16:sb16 + 16]
            p.ve.bn_stats(st_[:, 0:6], xt[sl][:, 0:512])
            p.ve.bn_stats(st_[:, 6:12], xt[sl][:, 512:1024])
            p.ve.drain()
            p.ve.bn_aggr(st_[:, 12:14], st_[:, 0:12])
            p.ve.drain()
            p.ve.tensor_scalar(st_[:, 14:15], st_[:, 13:14], 1e-5, None, ALU.add) \
                .then_inc(p.ved.h, 1)
            ve_st = p.ved.bump()
            p.se.wait_ge(p.ved.h, ve_st)
            p.se.activation(st_[:, 15:16], st_[:, 14:15], AF.Sqrt).then_inc(p.sed.h, 1)
            se_sq = p.sed.bump()
            p.ve.wait_ge(p.sed.h, se_sq)
            p.ve.reciprocal(st_[:, 15:16], st_[:, 15:16])
            p.ve.drain()
            p.ve.scalar_tensor_tensor(st_[:, 12:13], st_[:, 12:13], -1.0,
                                      st_[:, 15:16], ALU.mult, ALU.mult) \
                .then_inc(p.ved.h, 1)
            ve_rs = p.ved.bump()
            p.se.wait_ge(p.ved.h, ve_rs)
            p.se.activation(xnb[sl][:], xt[sl][:], AF.Identity,
                            bias=st_[:, 12:13], scale=st_[:, 15:16]) \
                .then_inc(p.sed.h, 1)
            se_xn.append(p.sed.bump())

            bank = pq[i % 2]
            pb_bf = bank[:].bitcast(BF16)
            p.te.wait_ge(p.sed.h, se_xn[i])
            if i >= 2:
                p.te.wait_ge(p.ved.h, ve_evac[i - 2])
            if i == 0:
                p.te.wait_ge(p.gpd.h, gp_all)
            for j in range(ND):
                inst = p.te.transpose(pb_bf[:, j * 128:(j + 1) * 128],
                                      xnb[sl][:, j * 128:(j + 1) * 128], ident[:])
                if j == ND - 1:
                    inst.then_inc(p.ted.h, 1)
            te_tr = p.ted.bump()

            p.ve.wait_ge(p.ted.h, te_tr)
            p.ve.tensor_copy(
                strided_ap(xnT, i * 128, [[S, ND], [1, 128]]),
                pb_bf[:].rearrange("p (j c) -> p j c", j=ND)).then_inc(p.ved.h, 1)
            ve_evac.append(p.ved.bump())
        ve_xnT = ve_evac[-1]

        def xnT_c(j):
            return xnT[:, j * S:(j + 1) * S]

        def wT_c(j, c0, c1):
            return wT[:, j * 3 * D + c0: j * 3 * D + c1]

        p.te.wait_ge(p.ved.h, ve_xnT)
        se_qkv = []
        for which, dst, bias, scl in (("q", qT, bqs, 0.125), ("k", kT, bks, 1.0)):
            base = 0 if which == "q" else D
            for hp in range(NP):
                for qt in range(4):
                    bank = pq[len(se_qkv) % 2]
                    if len(se_qkv) >= 2:
                        p.te.wait_ge(p.sed.h, se_qkv[-2])
                    for j in range(ND):
                        inst = p.te.matmul(bank[:],
                                           wT_c(j, base + hp * 128, base + hp * 128 + 128),
                                           xnT_c(j)[:, qt * 512:(qt + 1) * 512],
                                           start=(j == 0), stop=(j == ND - 1))
                    inst.then_inc(p.ted.h, 1)
                    te_mm = p.ted.bump()
                    p.se.wait_ge(p.ted.h, te_mm)
                    if not se_qkv:
                        p.se.wait_ge(p.ved.h, ve_init)
                    p.se.activation(dst[:, hp * S + qt * 512: hp * S + (qt + 1) * 512],
                                    bank[:], AF.Identity, bias=bias[:, hp:hp + 1],
                                    scale=scl).then_inc(p.sed.h, 1)
                    se_qkv.append(p.sed.bump())

        ve_bvb = None
        for vh in range(2):
            bank = pq[2 + vh]
            p.te.wait_ge(p.ved.h, ve_init)
            p.te.matmul(bank[:], onesb[:], bvr[:, vh * 512:(vh + 1) * 512],
                        start=True, stop=True).then_inc(p.ted.h, 1)
            te_b = p.ted.bump()
            p.ve.wait_ge(p.ted.h, te_b)
            p.ve.tensor_copy(bvb[:, vh * 512:(vh + 1) * 512], bank[:]) \
                .then_inc(p.ved.h, 1)
            ve_bvb = p.ved.bump()

        ve_v = []
        for tc in range(NT):
            for vh in range(2):
                bank = pq[4 + (len(ve_v) % 2)]
                if len(ve_v) >= 2:
                    p.te.wait_ge(p.ved.h, ve_v[-2])
                for j in range(ND):
                    inst = p.te.matmul(bank[:],
                                       xnT_c(j)[:, tc * 128:(tc + 1) * 128],
                                       wT_c(j, 2 * D + vh * 512, 2 * D + (vh + 1) * 512),
                                       start=(j == 0), stop=(j == ND - 1))
                inst.then_inc(p.ted.h, 1)
                te_mm = p.ted.bump()
                p.ve.wait_ge(p.ted.h, te_mm)
                if not ve_v:
                    p.ve.wait_ge(p.ved.h, ve_bvb)
                dst = strided_ap(vsb, tc * H * 65 + vh * 8 * 65, [[65, 8], [1, 64]])
                p.ve.tensor_tensor(dst, bank[:].rearrange("p (h c) -> p h c", h=8),
                                   bvb[:, vh * 512:(vh + 1) * 512]
                                   .rearrange("p (h c) -> p h c", h=8),
                                   ALU.add).then_inc(p.ved.h, 1)
                ve_v.append(p.ved.bump())
        ve_vall = ve_v[-1]
        te_v_done = p.ted.v

        wq_state = {"u": 0, "head": {}, "pt": {}}

        def emit_wq_unit():
            u = wq_state["u"]
            if wgu_h is None or u >= len(WQ_UNITS):
                return
            wq_state["u"] = u + 1
            sl_ = u % 2
            if u < 2:
                p.gp.wait_ge(p.ted.h, te_v_done)
            else:
                p.gp.wait_ge(p.ved.h, wq_state["head"][u - 2])
            srow = WQ_UNITS[u]
            p.gp.dma_start(out=wrowA[sl_][:],
                           in_=wgu_h.ap()[srow:srow + 128, :]) \
                .then_inc(p.gw0.h, 16)
            gv = p.gw0.bump(16)
            st_ = [p.gw1, p.gw2][sl_]
            p.ve.wait_ge(p.gw0.h, gv)
            if u >= 2:
                p.ve.wait_ge(st_.h, st_.v)
            vh, pt = _emit_quant(p, qbfA[sl_][:], wrowA[sl_][:], q1A, q2A,
                                 blkA, D, boff=0,
                                 wait_pod=wq_state["pt"].get(u - 1))
            wq_state["head"][u] = vh
            wq_state["pt"][u] = pt
            p.gp.wait_ge(p.pod.h, pt)
            p.gp.dma_start(out=wqp_h.ap()[u * 128:(u + 1) * 128, :],
                           in_=qbfA[sl_][:]).then_inc(st_.h, 16)
            st_.bump(16)

        std_base = [p.std0.v, p.std1.v]
        se_exp_prev = None
        se_stq_prev = None
        ve_str_last = None
        ve_tail02 = None
        ve_tail13 = None
        te_str = None
        for qt in range(4):
            nkc = (qt + 1) * 4
            for hp in range(NP):
                se_exp = {}
                te_av = {}
                for kc in range(nkc):
                    sa_ = pq[kc % 2]
                    sb_ = pq[2 + kc % 2]
                    if kc == 0:
                        p.te.wait_ge(p.ved.h, ve_vall)
                        p.te.wait_ge(p.sed.h, se_qkv[-1])
                        if se_exp_prev is not None:
                            p.te.wait_ge(p.sed.h, se_exp_prev)
                        if hp == 0 and ve_tail02 is not None:
                            p.te.wait_ge(p.ved.h, ve_tail02)
                    if kc >= 2:
                        p.te.wait_ge(p.sed.h, se_exp[kc - 2])
                    p.te.matmul(sa_[:],
                                kT[0:64, hp * S + kc * 128: hp * S + kc * 128 + 128],
                                qT[0:64, hp * S + qt * 512: hp * S + (qt + 1) * 512],
                                start=True, stop=True, tile_position=(0, 0))
                    p.te.matmul(sb_[:],
                                kT[64:128, hp * S + kc * 128: hp * S + kc * 128 + 128],
                                qT[64:128, hp * S + qt * 512: hp * S + (qt + 1) * 512],
                                start=True, stop=True, tile_position=(64, 0)) \
                        .then_inc(p.ted.h, 1)
                    te_s = p.ted.bump()

                    esl = kc % 2
                    p.se.wait_ge(p.ted.h, te_s)
                    if kc >= 2:
                        p.se.wait_ge(p.ted.h, te_av[kc - 2])
                    p.se.activation(expA[:, esl * 512:(esl + 1) * 512], sa_[:], AF.Exp)
                    p.se.activation(expB[:, esl * 512:(esl + 1) * 512], sb_[:], AF.Exp) \
                        .then_inc(p.sed.h, 1)
                    se_exp[kc] = p.sed.bump()

                    masked = kc >= nkc - 4
                    if masked:
                        mi = kc - (nkc - 4)
                        p.ve.wait_ge(p.sed.h, se_exp[kc])
                        p.ve.tensor_tensor(expA[:, esl * 512:(esl + 1) * 512],
                                           expA[:, esl * 512:(esl + 1) * 512],
                                           masks[:, mi * 512:(mi + 1) * 512], ALU.mult)
                        p.ve.tensor_tensor(expB[:, esl * 512:(esl + 1) * 512],
                                           expB[:, esl * 512:(esl + 1) * 512],
                                           masks[:, mi * 512:(mi + 1) * 512], ALU.mult) \
                            .then_inc(p.ved.h, 1)
                        ve_mask = p.ved.bump()
                        p.te.wait_ge(p.ved.h, ve_mask)
                    else:
                        p.te.wait_ge(p.sed.h, se_exp[kc])
                    if kc == 0 and se_stq_prev is not None:
                        p.te.wait_ge(p.sed.h, se_stq_prev)
                    if kc == 0 and hp == 0 and ve_tail13 is not None:
                        p.te.wait_ge(p.ved.h, ve_tail13)
                    headA, headB = 2 * hp, 2 * hp + 1
                    vA = vsb[:, kc * H * 65 + headA * 65: kc * H * 65 + headA * 65 + 65]
                    vB = vsb[:, kc * H * 65 + headB * 65: kc * H * 65 + headB * 65 + 65]
                    p.te.matmul(pq[4][0:65, :], vA, expA[:, esl * 512:(esl + 1) * 512],
                                start=(kc == 0), stop=(kc == nkc - 1))
                    p.te.matmul(pq[5][0:65, :], vB, expB[:, esl * 512:(esl + 1) * 512],
                                start=(kc == 0), stop=(kc == nkc - 1)) \
                        .then_inc(p.ted.h, 1)
                    te_av[kc] = p.ted.bump()

                te_fin = te_av[nkc - 1]
                se_exp_prev = se_exp[nkc - 1]
                p.se.wait_ge(p.ted.h, te_fin)
                if hp == 0 and qt >= 1:
                    p.se.wait_ge(p.ted.h, te_str)
                cs = hp * 512
                p.se.activation(sTqe[:, cs:cs + 512], pq[4][0:65, :], AF.Identity)
                p.se.activation(sTqo[:, cs:cs + 512], pq[5][0:65, :], AF.Identity) \
                    .then_inc(p.sed.h, 1)
                se_stq_prev = p.sed.bump()
                emit_wq_unit()
                if qt in (1, 2):
                    emit_wq_unit()
            se_stq_all = se_stq_prev

            ve_tail = {}
            for tt in range(4):
                ssl = tt % 2
                b0 = 0 if tt % 2 == 0 else 4
                p.te.wait_ge(p.sed.h, se_stq_all)
                if tt >= 2 and ve_tail.get(tt - 2) is not None:
                    p.te.wait_ge(p.ved.h, ve_tail[tt - 2])
                for hp in range(NP):
                    b = b0 + hp // 4
                    c = (hp % 4) * 65
                    p.te.transpose(
                        pq[b][:, c:c + 65],
                        sTqe[:, hp * 512 + tt * 128: hp * 512 + (tt + 1) * 128],
                        identf[0:65, 0:65])
                    inst = p.te.transpose(
                        pq[b0 + 2 + hp // 4][:, c:c + 65],
                        sTqo[:, hp * 512 + tt * 128: hp * 512 + (tt + 1) * 128],
                        identf[0:65, 0:65])
                inst.then_inc(p.ted.h, 1)
                te_str = p.ted.bump()
                p.ve.wait_ge(p.ted.h, te_str)
                if not (qt == 0 and tt < 2):
                    std_ = [p.std0, p.std1][ssl]
                    p.ve.wait_ge(std_.h, std_base[ssl] + 16 * ((qt * 4 + tt) // 2))
                p.ve.tensor_copy(
                    strided_ap(s0c[ssl], 0, [[130, 4], [1, 65]]),
                    strided_ap(pq[b0 + 0], 0, [[65, 4], [1, 65]]))
                p.ve.tensor_copy(
                    strided_ap(s0c[ssl], 4 * 130, [[130, 4], [1, 65]]),
                    strided_ap(pq[b0 + 1], 0, [[65, 4], [1, 65]]))
                p.ve.tensor_copy(
                    strided_ap(s0c[ssl], 65, [[130, 4], [1, 65]]),
                    strided_ap(pq[b0 + 2], 0, [[65, 4], [1, 65]]))
                p.ve.tensor_copy(
                    strided_ap(s0c[ssl], 4 * 130 + 65, [[130, 4], [1, 65]]),
                    strided_ap(pq[b0 + 3], 0, [[65, 4], [1, 65]])) \
                    .then_inc(p.ved.h, 1)
                ve_tail[tt] = p.ved.bump()
                ve_str_last = ve_tail[tt]
                p.ve.drain()
                p.ve.reciprocal(rsum[:], strided_ap(s0c[ssl], 64, [[65, 16], [1, 1]]))
                p.ve.drain()
                p.ve.tensor_tensor(
                    strided_ap(s0c[ssl], 0, [[64, 16], [1, 64]]),
                    strided_ap(s0c[ssl], 0, [[65, 16], [1, 64]]),
                    AP(rsum[:].tensor, rsum[:].offset,
                       [list(rsum[:].ap[0]), [1, 16], [0, 64]]),
                    ALU.mult).then_inc(p.ved.h, 1)
                ve_tail[tt] = p.ved.bump()
                ve_str_last = ve_tail[tt]
                std_ = [p.std0, p.std1][ssl]
                p.sy.wait_ge(p.ved.h, ve_str_last)
                p.sy.dma_start(
                    out=s0_h.ap()[(qt * 4 + tt) * 128:(qt * 4 + tt + 1) * 128, :],
                    in_=s0c[ssl][:, 0:D]).then_inc(std_.h, 16)
                std_.bump(16)
            ve_tail02 = ve_tail[2]
            ve_tail13 = ve_tail[3]

        if wgu_h is not None:
            assert wq_state["u"] == len(WQ_UNITS), wq_state["u"]
        p.sy.wait_ge(p.std0.h, p.std0.v)
        p.sy.wait_ge(p.std1.h, p.std1.v)

    return narrative


def _emit_quant(p, out_bf, t0, t1, t2, blk, W, boff=0, wait_pod=None,
                gp_wait_ted=None):
    nb = W // 16
    bm, hs = blk[:, boff:boff + nb], blk[:, boff + nb:boff + 2 * nb]
    r3 = lambda ap: ap.rearrange("p (nb w) -> p nb w", w=16)
    b3 = lambda ap: AP(ap.tensor, ap.offset, [list(ap.ap[0]), list(ap.ap[1]), [0, 16]])
    t1 = t1[:][:, 0:W]
    t2 = t2[:][:, 0:W]
    if wait_pod:
        p.ve.wait_ge(p.pod.h, wait_pod)
    p.ve.tensor_reduce(bm.rearrange("p (nb one) -> p nb one", one=1), r3(t0),
                       axis=AX.X, op=ALU.max, apply_absolute_value=True)
    p.ve.drain()
    p.ve.tensor_scalar(bm[:], bm[:], 1.0 / 6.0, 1e-12, ALU.mult, ALU.max)
    p.ve.drain()
    p.ve.tensor_scalar(hs[:], bm[:], 0.5, None, ALU.mult)
    p.ve.reciprocal(bm[:], bm[:])
    p.ve.drain()
    p.ve.tensor_tensor(r3(t1), r3(t0), b3(bm[:]), ALU.mult)
    p.ve.drain()
    p.ve.tensor_scalar(t2.bitcast(I32), t1.bitcast(I32), 0x7F800000, None,
                       ALU.bitwise_and)
    p.ve.drain()
    p.ve.tensor_scalar(t2, t2, 1.0, None, ALU.max)
    p.ve.drain()
    p.ve.tensor_scalar(t2.bitcast(I32), t2.bitcast(I32), 0x7F800000, None,
                       ALU.bitwise_xor)
    p.ve.drain()
    p.ve.tensor_tensor(t1, t1, t2, ALU.mult)
    p.ve.drain()
    p.ve.tensor_scalar(t1, t1, 8388608.0, 8388608.0, ALU.add, ALU.subtract)
    p.ve.tensor_scalar(t2.bitcast(I32), t2.bitcast(I32), 0x7F800000, None,
                       ALU.bitwise_xor).then_inc(p.ved.h, 1)
    ved_head = p.ved.bump()
    p.gp.wait_ge(p.ved.h, ved_head)
    if gp_wait_ted:
        p.gp.wait_ge(p.ted.h, gp_wait_ted)
    p.gp.tensor_tensor(r3(t2), r3(t2), b3(hs[:]), ALU.mult) \
        .then_inc(p.pod.h, 1)
    p.gp.wait_ge(p.pod.h, p.pod.bump())
    p.gp.tensor_tensor(out_bf, t1, t2, ALU.mult).then_inc(p.pod.h, 1)
    pod_tail = p.pod.bump()
    return ved_head, pod_tail


def _emit_quant2(p, out_bfs, t0s, t1s, t2s, blk, boffs, W, wait_pod=None):
    nb = W // 16
    r3 = lambda ap: ap.rearrange("p (nb w) -> p nb w", w=16)
    b3 = lambda ap: AP(ap.tensor, ap.offset, [list(ap.ap[0]), list(ap.ap[1]), [0, 16]])
    if wait_pod:
        p.ve.wait_ge(p.pod.h, wait_pod)
    last = [0, 0]
    chains = []
    for c in range(2):
        bm = blk[:, boffs[c]:boffs[c] + nb]
        hs = blk[:, boffs[c] + nb:boffs[c] + 2 * nb]
        t0, t1, t2 = t0s[c], t1s[c][:, 0:W], t2s[c][:, 0:W]
        out = out_bfs[c]
        chains.append([
            lambda t0=t0, bm=bm: p.ve.tensor_reduce(
                bm.rearrange("p (nb one) -> p nb one", one=1), r3(t0),
                axis=AX.X, op=ALU.max, apply_absolute_value=True),
            lambda bm=bm: p.ve.tensor_scalar(bm[:], bm[:], 1.0 / 6.0, 1e-12,
                                             ALU.mult, ALU.max),
            lambda bm=bm, hs=hs: p.ve.tensor_scalar(hs[:], bm[:], 0.5, None,
                                                    ALU.mult),
            lambda bm=bm: p.ve.reciprocal(bm[:], bm[:]),
            lambda t0=t0, t1=t1, bm=bm: p.ve.tensor_tensor(
                r3(t1), r3(t0), b3(bm[:]), ALU.mult),
            lambda t1=t1, t2=t2: p.ve.tensor_scalar(
                t2.bitcast(I32), t1.bitcast(I32), 0x7F800000, None,
                ALU.bitwise_and),
            lambda t2=t2: p.ve.tensor_scalar(t2, t2, 1.0, None, ALU.max),
            lambda t2=t2: p.ve.tensor_scalar(
                t2.bitcast(I32), t2.bitcast(I32), 0x7F800000, None,
                ALU.bitwise_xor),
            lambda t1=t1, t2=t2: p.ve.tensor_tensor(t1, t1, t2, ALU.mult),
            lambda t1=t1: p.ve.tensor_scalar(t1, t1, 8388608.0, 8388608.0,
                                             ALU.add, ALU.subtract),
            lambda t2=t2: p.ve.tensor_scalar(
                t2.bitcast(I32), t2.bitcast(I32), 0x7F800000, None,
                ALU.bitwise_xor),
            lambda t2=t2, hs=hs: p.ve.tensor_tensor(r3(t2), r3(t2), b3(hs[:]),
                                                    ALU.mult),
            lambda t1=t1, t2=t2, out=out: p.ve.tensor_tensor(out, t1, t2,
                                                             ALU.mult),
        ])
    nst = len(chains[0])
    for s in range(nst):
        for c in range(2):
            if last[c]:
                p.ve.wait_ge(p.ved.h, last[c])
            chains[c][s]().then_inc(p.ved.h, 1)
            last[c] = p.ved.bump()
    return max(last)


def _emit_layernorm(p, xn_out, xtile, stt, eps=1e-5, out2=None):
    p.ve.bn_stats(stt[:, 0:6], xtile[:, 0:512])
    p.ve.bn_stats(stt[:, 6:12], xtile[:, 512:1024])
    p.ve.drain()
    p.ve.bn_aggr(stt[:, 12:14], stt[:, 0:12])
    p.ve.drain()
    p.ve.tensor_scalar(stt[:, 14:15], stt[:, 13:14], eps, None, ALU.add) \
        .then_inc(p.ved.h, 1)
    ve_va = p.ved.bump()
    p.se.wait_ge(p.ved.h, ve_va)
    p.se.activation(stt[:, 15:16], stt[:, 14:15], AF.Sqrt).then_inc(p.sed.h, 1)
    se_sq = p.sed.bump()
    p.ve.wait_ge(p.sed.h, se_sq)
    p.ve.reciprocal(stt[:, 15:16], stt[:, 15:16])
    p.ve.drain()
    p.ve.scalar_tensor_tensor(stt[:, 12:13], stt[:, 12:13], -1.0,
                              stt[:, 15:16], ALU.mult, ALU.mult) \
        .then_inc(p.ved.h, 1)
    ve_rs = p.ved.bump()
    p.se.wait_ge(p.ved.h, ve_rs)
    inst = p.se.activation(xn_out, xtile, AF.Identity,
                           bias=stt[:, 12:13], scale=stt[:, 15:16])
    if out2 is not None:
        inst = p.se.activation(out2, xtile, AF.Identity,
                               bias=stt[:, 12:13], scale=stt[:, 15:16])
    return inst


def make_l0_ffn(nc, dram, pq, pfx="b_"):
    x_h = dram["x"]
    s_h = dram["s0"]
    wgu_h = dram["wgu"]
    wd_h = dram["wd"]
    id_h = dram["ident"]
    x1_h = dram["x1"]
    wqp_h = dram.get("wq_pre")
    nc = _NSWrap(nc, pfx)

    NHC = FFN_I // 128
    SLICES = [(i * 512, min(512, FFN_I - i * 512)) for i in range((FFN_I + 511) // 512)]

    xt = [nc.alloc_sbuf_tensor(f"xt{i}", [128, D], F32) for i in range(2)]
    q0 = nc.alloc_sbuf_tensor("q0", [128, D], F32)
    q1 = nc.alloc_sbuf_tensor("q1", [128, D], F32)
    q2 = nc.alloc_sbuf_tensor("q2", [128, D], F32)
    q3 = nc.alloc_sbuf_tensor("q3", [128, D], F32)
    q4 = nc.alloc_sbuf_tensor("q4", [128, D], F32)
    qbf = nc.alloc_sbuf_tensor("qbf", [128, 2 * D], BF16)
    qbfB = nc.alloc_sbuf_tensor("qbfB", [128, 2 * D], BF16)
    blk = nc.alloc_sbuf_tensor("blk", [128, 544], F32)
    hqT = nc.alloc_sbuf_tensor("hqT", [128, NHC * S], BF16)
    ident = nc.alloc_sbuf_tensor("ident_s", [128, 128], BF16)
    gt = nc.alloc_sbuf_tensor("gt", [128, D], F32)
    st_ = nc.alloc_sbuf_tensor("st_", [128, D], F32)
    x1t = nc.alloc_sbuf_tensor("x1t", [128, D], F32)
    wdr = nc.alloc_sbuf_tensor_at("wdr", [128, FFN_I], F32,
                                  offset=nc.lookup_mloc(q3).addr)
    xqT = nc.alloc_sbuf_tensor("xqT", [128, ND * S], BF16)
    wgTs = nc.alloc_sbuf_tensor("wgTs", [128, ND * 512], BF16)
    wuTs = nc.alloc_sbuf_tensor("wuTs", [128, ND * 512], BF16)
    wrow = [nc.alloc_sbuf_tensor(f"wrow{i}", [128, D], F32) for i in range(2)]
    sg = nc.alloc_sbuf_tensor("sg", [128, 2 * 512], F32)
    hh = nc.alloc_sbuf_tensor("hh", [128, 2 * 512], F32)
    hb2 = [nc.alloc_sbuf_tensor(f"hb{i}", [128, 1024], BF16) for i in range(2)]
    _wa = nc.lookup_mloc(xqT).addr
    wdT = nc.alloc_sbuf_tensor_at("wdT", [128, NHC * D], BF16, offset=_wa)



    def narrative(p):
        p.gp.dma_start(out=ident[:], in_=id_h.ap()).then_inc(p.gpd.h, 16)
        gp_all = p.gpd.bump(16)
        sx = [p.sx0, p.sx1]
        sw = [p.sw0, p.sw1]
        stt = blk[:, 528:544]

        ve_q = []
        quv = []
        se_xn_prev = {}
        te_tr = []
        ve_evac = []
        for i in range(NT):
            sl = i % 2
            if i >= 2:
                p.sy.wait_ge(p.sed.h, se_xn_prev[i - 2])
            p.sy.dma_start(out=xt[sl][:], in_=x_h.ap()[i * 128:(i + 1) * 128, :]) \
                .then_inc(sx[sl].h, 16)
            xv = sx[sl].bump(16)

            p.ve.wait_ge(sx[sl].h, xv)
            _emit_layernorm(p, q0[:], xt[sl][:], stt).then_inc(p.sed.h, 1)
            se_xn = p.sed.bump()
            se_xn_prev[i] = se_xn
            p.ve.wait_ge(p.sed.h, se_xn)
            t12 = (q1, q2) if i % 2 == 0 else (q3, q4)
            vh, pt = _emit_quant(
                p, qbf[:, (i % 2) * D:(i % 2) * D + D], q0[:], t12[0], t12[1],
                blk, D, boff=(i % 2) * 256,
                wait_pod=(quv[i - 2][1] if i >= 2 else None),
                gp_wait_ted=(te_tr[i - 2] if i >= 2 else None))
            quv.append((vh, pt))
            ve_q.append(vh)

            bank = pq[4 + i % 2]
            pb_bf = bank[:].bitcast(BF16)
            p.te.wait_ge(p.pod.h, pt)
            if i == 0:
                p.te.wait_ge(p.gpd.h, gp_all)
            if i >= 2:
                p.te.wait_ge(p.sed.h, ve_evac[i - 2])
            for j in range(ND):
                inst = p.te.transpose(
                    pb_bf[:, j * 128:(j + 1) * 128],
                    qbf[:, (i % 2) * D + j * 128: (i % 2) * D + (j + 1) * 128],
                    ident[:])
            inst.then_inc(p.ted.h, 1)
            te_tr.append(p.ted.bump())
            p.se.wait_ge(p.ted.h, te_tr[i])
            p.se.activation(
                strided_ap(xqT, i * 128, [[S, ND], [1, 128]]),
                pb_bf[:].rearrange("p (j c) -> p j c", j=ND),
                AF.Identity, scale=1.0).then_inc(p.sed.h, 1)
            ve_evac.append(p.sed.bump())
        se_xqT = ve_evac[-1]

        wk = 0
        ve_wq = {}
        se_wtr_evac = {}
        te_wtr_prev = None
        te_wtr_hist = {}
        ve_hprev = {0: None, 1: None}
        se_htr_evac = None
        hq_pt = {}
        hq_te = {}
        hq_k = 0
        hq_pend = None
        te_htr = None
        slice_idx = -1

        def emit_htr(pend):
            nonlocal se_htr_evac, te_htr
            k_, pt_, tc0_, ntr_, hc0_ = pend
            bank = pq[6 + k_ % 2]
            pb_bf = bank[:].bitcast(BF16)
            hbs_ = hb2[k_ % 2]
            p.te.wait_ge(p.pod.h, pt_)
            if se_htr_evac is not None:
                p.te.wait_ge(p.sed.h, se_htr_evac)
            for c in range(2):
                for j in range(ntr_):
                    inst = p.te.transpose(
                        pb_bf[:, c * 512 + j * 128: c * 512 + (j + 1) * 128],
                        hbs_[:, c * 512 + j * 128: c * 512 + (j + 1) * 128],
                        ident[:])
            inst.then_inc(p.ted.h, 1)
            te_htr = p.ted.bump()
            hq_te[k_] = te_htr
            p.se.wait_ge(p.ted.h, te_htr)
            p.se.activation(
                strided_ap(hqT, hc0_ * S + tc0_ * 128, [[S, ntr_], [1, 128]]),
                AP(pb_bf.tensor, pb_bf.offset,
                   [list(pb_bf.ap[0]), [128, ntr_], [1, 128]]),
                AF.Identity, scale=1.0)
            p.se.activation(
                strided_ap(hqT, hc0_ * S + (tc0_ + 1) * 128, [[S, ntr_], [1, 128]]),
                AP(pb_bf.tensor, pb_bf.offset + 512,
                   [list(pb_bf.ap[0]), [128, ntr_], [1, 128]]),
                AF.Identity, scale=1.0).then_inc(p.sed.h, 1)
            se_htr_evac = p.sed.bump()
        pu = 0
        for (hs0, hw) in SLICES:
            slice_idx += 1
            pre = wqp_h is not None
            nrc = hw // 128
            for gu in range(2):
                for rp in range(nrc // 2):
                    row0 = gu * FFN_I + hs0 + rp * 256
                    if pre:
                        u0 = pu
                        pu += 2
                        pj = wk // 2
                        qb_ = [qbf, qbfB][pj % 2]
                        if pj == 0:
                            p.sy.wait_ge(p.gw1.h, p.gw1.v)
                            p.sy.wait_ge(p.gw2.h, p.gw2.v)
                            p.sy.wait_ge(p.ted.h, te_tr[-1])
                        elif pj >= 2:
                            p.sy.wait_ge(p.ted.h, te_wtr_hist[pj - 2])
                        sp_ = [p.sw2, p.sw3][pj % 2]
                        p.sy.dma_start(
                            out=qb_[:].rearrange("p (c d) -> p c d", c=2),
                            in_=wqp_h.ap()[u0 * 128:(u0 + 2) * 128, :]
                            .rearrange("(c p) d -> p c d", p=128)) \
                            .then_inc(sp_.h, 16)
                        wv = sp_.bump(16)
                        wk += 2
                        p.te.wait_ge(sp_.h, wv)
                    else:
                        qb_ = qbf
                        for c in range(2):
                            if wk >= 2 and ve_wq.get(wk - 2):
                                p.sy.wait_ge(p.ved.h, ve_wq[wk - 2])
                            p.sy.dma_start(out=wrow[c][:],
                                           in_=wgu_h.ap()[row0 + c * 128:
                                                          row0 + (c + 1) * 128, :]) \
                                .then_inc(sw[c].h, 16)
                            wv = sw[c].bump(16)
                            p.ve.wait_ge(sw[c].h, wv)
                            wk += 1
                        if te_wtr_prev is not None:
                            p.ve.wait_ge(p.ted.h, te_wtr_prev)
                        else:
                            p.ve.wait_ge(p.ted.h, te_tr[-1])
                            p.ve.wait_ge(p.pod.h, p.pod.v)
                        vq = _emit_quant2(
                            p, [qbf[:, 0:D], qbf[:, D:2 * D]],
                            [wrow[0][:], wrow[1][:]], [q1, q3], [q2, q4],
                            blk, [0, 256], D,
                            wait_pod=(p.pod.v if p.pod.v else None))
                        ve_wq[wk - 2] = vq
                        ve_wq[wk - 1] = vq
                        p.te.wait_ge(p.ved.h, vq)

                    if wk >= 4:
                        p.te.wait_ge(p.sed.h, se_wtr_evac[wk - 4])
                    elif wk == 2:
                        p.te.wait_ge(p.sed.h, se_xqT)
                    for c in range(2):
                        pb_bf = pq[4 + c][:].bitcast(BF16)
                        for j in range(ND):
                            inst = p.te.transpose(
                                pb_bf[:, j * 128:(j + 1) * 128],
                                qb_[:, c * D + j * 128: c * D + (j + 1) * 128],
                                ident[:])
                    inst.then_inc(p.ted.h, 1)
                    te_wtr_prev = p.ted.bump()
                    te_wtr_hist[wk // 2 - 1] = te_wtr_prev
                    dstT = wgTs if gu == 0 else wuTs
                    p.se.wait_ge(p.ted.h, te_wtr_prev)
                    p.se.activation(
                        strided_ap(dstT, rp * 256, [[512, ND], [1, 128]]),
                        pq[4][:].bitcast(BF16).rearrange("p (j c) -> p j c", j=ND),
                        AF.Identity, scale=1.0)
                    p.se.activation(
                        strided_ap(dstT, rp * 256 + 128, [[512, ND], [1, 128]]),
                        pq[5][:].bitcast(BF16).rearrange("p (j c) -> p j c", j=ND),
                        AF.Identity, scale=1.0).then_inc(p.sed.h, 1)
                    se_wtr_evac[wk - 2] = p.sed.bump()
            ve_wslice = se_wtr_evac[wk - 2]

            hc0 = hs0 // 128
            ntr = hw // 128
            for tc in range(NT):
                gb = pq[(tc % 2) * 2]
                ub = pq[(tc % 2) * 2 + 1]
                p.te.wait_ge(p.sed.h, ve_wslice)
                if tc == 0 and hs0 == 0:
                    p.te.wait_ge(p.sed.h, se_xqT)
                if ve_hprev[tc % 2] is not None:
                    p.te.wait_ge(p.ved.h, ve_hprev[tc % 2])
                for j in range(ND):
                    p.te.matmul(gb[:, 0:hw],
                                xqT[:, j * S + tc * 128: j * S + (tc + 1) * 128],
                                wgTs[:, j * 512: j * 512 + hw],
                                start=(j == 0), stop=(j == ND - 1))
                for j in range(ND):
                    inst = p.te.matmul(ub[:, 0:hw],
                                       xqT[:, j * S + tc * 128: j * S + (tc + 1) * 128],
                                       wuTs[:, j * 512: j * 512 + hw],
                                       start=(j == 0), stop=(j == ND - 1))
                inst.then_inc(p.ted.h, 1)
                te_gu = p.ted.bump()

                ssl = tc % 2
                p.se.wait_ge(p.ted.h, te_gu)
                p.se.activation(sg[:, ssl * 512: ssl * 512 + hw], gb[:, 0:hw],
                                AF.Silu).then_inc(p.sed.h, 1)
                se_sg = p.sed.bump()
                p.ve.wait_ge(p.sed.h, se_sg)
                p.ve.tensor_tensor(hh[:, ssl * 512: ssl * 512 + hw],
                                   sg[:, ssl * 512: ssl * 512 + hw], ub[:, 0:hw],
                                   ALU.mult).then_inc(p.ved.h, 1)
                ve_hprev[tc % 2] = p.ved.bump()
                if tc % 2 == 0:
                    continue
                p.ve.drain()
                par = hq_k % 2
                t12 = (q1, q2) if par == 0 else (q3, q4)
                wp = hq_pt.get(hq_k - 2)
                gwt = hq_te.get(hq_k - 2)
                hbs = hb2[par]
                if hw == 512:
                    vh, pt = _emit_quant(p, hbs[:, 0:1024], hh[:][:, 0:1024],
                                         t12[0], t12[1], blk, 1024,
                                         boff=par * 256, wait_pod=wp,
                                         gp_wait_ted=gwt)
                else:
                    wpa = hq_pt.get(hq_k - 1)
                    _emit_quant(p, hbs[:, 0:hw], hh[:][:, 0:hw],
                                q1, q2, blk, hw, boff=0,
                                wait_pod=(wpa - 2 if wpa else wp),
                                gp_wait_ted=gwt)
                    vh, pt = _emit_quant(p, hbs[:, 512:512 + hw],
                                         hh[:][:, 512:512 + hw],
                                         q3, q4, blk, hw, boff=256,
                                         wait_pod=wpa)
                hq_pt[hq_k] = pt
                pend_prev = hq_pend
                hq_pend = (hq_k, pt, tc - 1, ntr, hc0)
                hq_k += 1
                if pend_prev is not None:
                    emit_htr(pend_prev)
        if hq_pend is not None:
            emit_htr(hq_pend)
            hq_pend = None
        ve_hqT = se_htr_evac

        PIECES = [(0, 1024), (1024, 1024), (2048, 768)]
        se_wdtr_evac = {}
        wd_pt = {}
        te_wdtr = None
        pi_k = 0
        for rc in range(ND):
            if rc == 0:
                p.sy.wait_ge(sw[0].h, sw[0].v)
                p.sy.wait_ge(sw[1].h, sw[1].v)
                p.sy.wait_ge(p.ved.h, p.ved.v)
                p.sy.wait_ge(p.ted.h, p.ted.v)
                p.sy.wait_ge(p.pod.h, p.pod.v)
            if rc >= 1:
                p.sy.wait_ge(p.ved.h, ve_wdq)
            p.sy.dma_start(out=wdr[:], in_=wd_h.ap()[rc * 128:(rc + 1) * 128, :]) \
                .then_inc(sw[0].h, 16)
            wv = sw[0].bump(16)
            p.ve.wait_ge(sw[0].h, wv)
            if rc == 0:
                p.ve.wait_ge(p.pod.h, p.pod.v)
            for (c0, cw) in PIECES:
                vh, pt = _emit_quant(
                    p, hb2[0][:, 0:cw], wdr[:][:, c0:c0 + cw], q1, q2,
                    blk, cw, boff=0,
                    wait_pod=wd_pt.get(pi_k - 1),
                    gp_wait_ted=(te_wdtr if pi_k >= 1 else te_htr))
                wd_pt[pi_k] = pt
                ve_wdq = vh

                bank = pq[4 + pi_k % 2]
                pb_bf = bank[:].bitcast(BF16)
                p.te.wait_ge(p.pod.h, pt)
                if pi_k >= 2:
                    p.te.wait_ge(p.sed.h, se_wdtr_evac[pi_k - 2])
                npc = cw // 128
                for j in range(npc):
                    inst = p.te.transpose(pb_bf[:, j * 128:(j + 1) * 128],
                                          hb2[0][:, j * 128:(j + 1) * 128],
                                          ident[:])
                inst.then_inc(p.ted.h, 1)
                te_wdtr = p.ted.bump()
                p.se.wait_ge(p.ted.h, te_wdtr)
                p.se.activation(
                    strided_ap(wdT, (c0 // 128) * D + rc * 128, [[D, npc], [1, 128]]),
                    AP(pb_bf.tensor, pb_bf.offset,
                       [list(pb_bf.ap[0]), [128, npc], [1, 128]]),
                    AF.Identity, scale=1.0).then_inc(p.sed.h, 1)
                se_wdtr_evac[pi_k] = p.sed.bump()
                pi_k += 1
        ve_wdT = se_wdtr_evac[pi_k - 1]

        ve_upd = {}
        std_base = [p.std0.v, p.std1.v]
        for tc in range(NT):
            b0 = pq[(tc % 2) * 2]
            b1 = pq[(tc % 2) * 2 + 1]
            if tc == 0:
                p.te.wait_ge(p.sed.h, ve_wdT)
                p.te.wait_ge(p.sed.h, ve_hqT)
            if tc >= 2:
                p.te.wait_ge(p.ved.h, ve_upd[tc - 2])
            for j in range(NHC):
                p.te.matmul(b0[:], hqT[:, j * S + tc * 128: j * S + (tc + 1) * 128],
                            wdT[:, j * D: j * D + 512],
                            start=(j == 0), stop=(j == NHC - 1))
            for j in range(NHC):
                inst = p.te.matmul(b1[:],
                                   hqT[:, j * S + tc * 128: j * S + (tc + 1) * 128],
                                   wdT[:, j * D + 512: j * D + 1024],
                                   start=(j == 0), stop=(j == NHC - 1))
            inst.then_inc(p.ted.h, 1)
            te_dn = p.ted.bump()

            sl = tc % 2
            if tc < 2:
                p.sy.wait_ge(sx[sl].h, sx[sl].v)
            if tc >= 2:
                p.sy.wait_ge(p.ved.h, ve_upd[tc - 2])
            p.sy.dma_start(out=xt[sl][:], in_=x_h.ap()[tc * 128:(tc + 1) * 128, :]) \
                .then_inc(sx[sl].h, 16)
            sx[sl].bump(16)
            if tc >= 1:
                p.sy.wait_ge(p.ved.h, ve_upd[tc - 1])
            p.sy.dma_start(out=st_[:], in_=s_h.ap()[tc * 128:(tc + 1) * 128, :]) \
                .then_inc(sx[sl].h, 16)
            sv = sx[sl].bump(16)
            p.se.wait_ge(p.ted.h, te_dn)
            if tc >= 1:
                p.se.wait_ge(p.ved.h, ve_upd[tc - 1])
            p.se.activation(gt[:, 0:512], b0[:], AF.Silu)
            p.se.activation(gt[:, 512:1024], b1[:], AF.Silu).then_inc(p.sed.h, 1)
            se_g = p.sed.bump()
            p.ve.wait_ge(p.sed.h, se_g)
            p.ve.wait_ge(sx[sl].h, sv)
            if tc >= 1:
                std_p = [p.std0, p.std1][(tc - 1) % 2]
                p.ve.wait_ge(std_p.h, std_p.v)
            p.ve.tensor_tensor(gt[:], st_[:], gt[:], ALU.mult)
            p.ve.drain()
            p.ve.tensor_tensor(x1t[:], xt[sl][:], gt[:], ALU.add) \
                .then_inc(p.ved.h, 1)
            ve_upd[tc] = p.ved.bump()
            std_ = [p.std0, p.std1][tc % 2]
            p.sy.wait_ge(p.ved.h, ve_upd[tc])
            p.sy.dma_start(out=x1_h.ap()[tc * 128:(tc + 1) * 128, :], in_=x1t[:]) \
                .then_inc(std_.h, 16)
            std_.bump(16)

        p.sy.wait_ge(p.std0.h, p.std0.v)
        p.sy.wait_ge(p.std1.h, p.std1.v)

    return narrative


def make_l1_attn(nc, dram, pq, pfx="c_"):
    x_h = dram["x1"]
    w_h = dram["wabcdT"]
    ab_h = dram["ab_act"]
    ow_h = dram["outwT"]
    ob_h = dram["outb_row"]
    nw_h = dram["normw_row"]
    id_h = dram["ident"]
    s1_h = dram["s1"]
    xn1_h = dram["xn1"]
    nc = _NSWrap(nc, pfx)

    xt = [nc.alloc_sbuf_tensor(f"xt{i}", [128, D], F32) for i in range(2)]
    xn1t = [nc.alloc_sbuf_tensor(f"xn1t{i}", [128, D], F32) for i in range(2)]
    xn2b = [nc.alloc_sbuf_tensor(f"xn2b{i}", [128, D], BF16) for i in range(2)]
    tmp = nc.alloc_sbuf_tensor("tmp", [128, D], F32)
    xn2T = nc.alloc_sbuf_tensor("xn2T", [128, ND * S], BF16)
    hT = nc.alloc_sbuf_tensor("hT", [128, ND * S], BF16)
    wsl = [nc.alloc_sbuf_tensor(f"wsl{i}", [128, 4 * ND * 128], BF16) for i in range(2)]
    owT = nc.alloc_sbuf_tensor("owT", [128, ND * D], BF16)
    ab_sb = nc.alloc_sbuf_tensor("ab_sb", [128, 8], F32)
    nwb = nc.alloc_sbuf_tensor("nwb", [128, D], F32)
    obb = nc.alloc_sbuf_tensor("obb", [128, D], F32)
    nwr = nc.alloc_sbuf_tensor("nwr", [1, D], BF16)
    obr = nc.alloc_sbuf_tensor("obr", [1, D], BF16)
    onesb = nc.alloc_sbuf_tensor("onesb", [1, 128], BF16)
    ident = nc.alloc_sbuf_tensor("ident_s", [128, 128], BF16)
    stats = nc.alloc_sbuf_tensor("stats", [128, 32], F32)
    ga = nc.alloc_sbuf_tensor("ga", [128, S], F32)
    gb_ = nc.alloc_sbuf_tensor("gb_", [128, S], F32)
    gc = nc.alloc_sbuf_tensor("gc", [128, S], F32)
    gd = nc.alloc_sbuf_tensor("gd", [128, S], F32)
    hs_ = nc.alloc_sbuf_tensor("hs_", [128, S], F32)
    zcol = nc.alloc_sbuf_tensor("zcol", [128, 1], F32)
    s1t = [nc.alloc_sbuf_tensor(f"s1t{i}", [128, D], F32) for i in range(2)]



    def narrative(p):
        p.gp.dma_start(out=ident[:], in_=id_h.ap()).then_inc(p.gpd.h, 16)
        p.gpd.bump(16)
        p.gp.dma_start(out=ab_sb[:], in_=ab_h.ap()).then_inc(p.gpd.h, 16)
        p.gpd.bump(16)
        p.gp.dma_start(out=nwr[:], in_=nw_h.ap()).then_inc(p.gpd.h, 16)
        p.gpd.bump(16)
        p.gp.dma_start(out=obr[:], in_=ob_h.ap()).then_inc(p.gpd.h, 16)
        p.gpd.bump(16)
        for j in range(ND):
            p.gp.dma_start(out=owT[:, j * D:(j + 1) * D],
                           in_=ow_h.ap()[j * 128:(j + 1) * 128, :]) \
                .then_inc(p.gpd.h, 16)
            p.gpd.bump(16)
        gp_all = p.gpd.v

        p.ve.wait_ge(p.gpd.h, gp_all)
        p.ve.memset(onesb[:], 1.0)
        p.ve.memset(zcol[:], 0.0).then_inc(p.ved.h, 1)
        ve_init = p.ved.bump()

        se_bc = []
        for (dst, row) in ((nwb, nwr), (obb, obr)):
            for vh in range(2):
                bank = pq[6 + vh]
                p.te.wait_ge(p.ved.h, ve_init)
                if len(se_bc) >= 2:
                    p.te.wait_ge(p.sed.h, se_bc[-2])
                p.te.matmul(bank[:], onesb[:], row[:, vh * 512:(vh + 1) * 512],
                            start=True, stop=True).then_inc(p.ted.h, 1)
                te_b = p.ted.bump()
                p.se.wait_ge(p.ted.h, te_b)
                p.se.activation(dst[:, vh * 512:(vh + 1) * 512], bank[:],
                                AF.Identity, scale=1.0).then_inc(p.sed.h, 1)
                se_bc.append(p.sed.bump())
        se_bcast = se_bc[-1]

        sx = [p.sx0, p.sx1]
        stx = [p.stx0, p.stx1]
        stx_base = [p.stx0.v, p.stx1.v]
        se_xn = []
        ve_x2 = []
        te_tr = []
        ve_evac = []
        for i in range(NT):
            sl = i % 2
            if i >= 2:
                p.sy.wait_ge(p.ved.h, ve_x2[i - 2])
                p.sy.wait_ge(stx[sl].h, stx_base[sl] + 16 * (i // 2))
            p.sy.dma_start(out=xt[sl][:], in_=x_h.ap()[i * 128:(i + 1) * 128, :]) \
                .then_inc(sx[sl].h, 16)
            xv = sx[sl].bump(16)
            p.ve.wait_ge(sx[sl].h, xv)
            if i >= 2:
                p.ve.wait_ge(p.sed.h, se_xn[i - 2])
            stt = stats[:, 16 * sl:16 * sl + 16]
            _emit_layernorm(p, xn1t[sl][:], xt[sl][:], stt).then_inc(p.sed.h, 1)
            se_xn.append(p.sed.bump())
            p.sy.wait_ge(p.sed.h, se_xn[i])
            p.sy.dma_start(out=xn1_h.ap()[i * 128:(i + 1) * 128, :],
                           in_=xn1t[sl][:]).then_inc(stx[sl].h, 16)
            stx[sl].bump(16)
            p.se.wait_ge(p.sed.h, se_xn[i])
            p.se.activation(tmp[:], xn1t[sl][:], AF.Square,
                            accum_out=stt[:, 0:1]).then_inc(p.sed.h, 1)
            se_ssq = p.sed.bump()
            p.ve.wait_ge(p.sed.h, se_ssq)
            p.ve.tensor_scalar(stt[:, 1:2], stt[:, 0:1], 1.0 / D, 1e-5,
                               ALU.mult, ALU.add).then_inc(p.ved.h, 1)
            ve_ss = p.ved.bump()
            p.se.wait_ge(p.ved.h, ve_ss)
            p.se.activation(stt[:, 2:3], stt[:, 1:2], AF.Sqrt).then_inc(p.sed.h, 1)
            se_rm = p.sed.bump()
            p.ve.wait_ge(p.sed.h, se_rm)
            p.ve.reciprocal(stt[:, 2:3], stt[:, 2:3])
            p.ve.drain()
            if se_bcast is not None and i == 0:
                p.ve.wait_ge(p.sed.h, se_bcast)
            if i >= 2:
                p.ve.wait_ge(p.ted.h, te_tr[i - 2])
            p.ve.tensor_tensor(tmp[:], xn1t[sl][:], nwb[:], ALU.mult) \
                .then_inc(p.ved.h, 1)
            ve_t = p.ved.bump()
            p.se.wait_ge(p.ved.h, ve_t)
            p.se.activation(xn2b[sl][:], tmp[:], AF.Identity,
                            scale=stt[:, 2:3]).then_inc(p.sed.h, 1)
            se_x2 = p.sed.bump()
            p.ve.drain()
            p.ve.nop()
            ve_x2.append(p.ved.v)
            bank = pq[4 + i % 2]
            pb_bf = bank[:].bitcast(BF16)
            p.te.wait_ge(p.sed.h, se_x2)
            if i >= 2:
                p.te.wait_ge(p.ved.h, ve_evac[i - 2])
            for j in range(ND):
                inst = p.te.transpose(pb_bf[:, j * 128:(j + 1) * 128],
                                      xn2b[sl][:, j * 128:(j + 1) * 128], ident[:])
            inst.then_inc(p.ted.h, 1)
            te_tr.append(p.ted.bump())
            p.ve.wait_ge(p.ted.h, te_tr[i])
            p.ve.tensor_copy(
                strided_ap(xn2T, i * 128, [[S, ND], [1, 128]]),
                pb_bf[:].rearrange("p (j c) -> p j c", j=ND)).then_inc(p.ved.h, 1)
            ve_evac.append(p.ved.bump())
            ve_x2[i] = ve_evac[i]
        ve_xn2T = ve_evac[-1]

        sw = [p.gw0, p.gw1]
        ve_wl_prev = None
        ve_hT_evac = None
        se_gprev = None
        ve_dbprev = None
        for nk in range(ND):
            sl = nk % 2
            if nk >= 2:
                p.gp.wait_ge(p.ted.h, te_wused[nk - 2])
            src4 = w_h.ap().rearrange("(j pp) (w n) -> pp w j n", pp=128, w=4) \
                [:, :, :, nk * 128:(nk + 1) * 128]
            p.gp.dma_start(
                out=wsl[sl][:].rearrange("p (w j n) -> p w j n", w=4, j=ND),
                in_=src4).then_inc(sw[sl].h, 16)
            wv = sw[sl].bump(16)

            p.te.wait_ge(sw[sl].h, wv)
            if nk == 0:
                p.te.wait_ge(p.ved.h, ve_xn2T)
            gbuf = {0: ga, 1: gb_, 2: gc, 3: gd}
            for w in range(4):
                for ch in range(4):
                    bank = pq[(w * 4 + ch) % 2]
                    if not (nk == 0 and w == 0 and ch < 2):
                        p.te.wait_ge(p.sed.h, se_ev[(w * 4 + ch - 2) % 100]
                                     if False else se_gprev2)
                    for j in range(ND):
                        inst = p.te.matmul(
                            bank[:],
                            wsl[sl][:, (w * ND + j) * 128:(w * ND + j + 1) * 128],
                            xn2T[:, j * S + ch * 512: j * S + (ch + 1) * 512],
                            start=(j == 0), stop=(j == ND - 1))
                    inst.then_inc(p.ted.h, 1)
                    te_mm = p.ted.bump()
                    p.se.wait_ge(p.ted.h, te_mm)
                    if w == 0:
                        p.se.wait_ge(p.ved.h, ve_dbprev) if ve_dbprev else None
                        p.se.activation(ga[:, ch * 512:(ch + 1) * 512], bank[:],
                                        AF.Sigmoid, bias=ab_sb[:, nk:nk + 1]) \
                            .then_inc(p.sed.h, 1)
                    elif w == 1:
                        p.se.activation(gb_[:, ch * 512:(ch + 1) * 512], bank[:],
                                        AF.Identity, scale=1.0).then_inc(p.sed.h, 1)
                    elif w == 2:
                        p.se.activation(gc[:, ch * 512:(ch + 1) * 512], bank[:],
                                        AF.Sigmoid).then_inc(p.sed.h, 1)
                    else:
                        p.se.activation(gd[:, ch * 512:(ch + 1) * 512], bank[:],
                                        AF.Sigmoid).then_inc(p.sed.h, 1)
                    se_gprev2 = p.sed.bump()
            te_wused = {} if nk == 0 else te_wused
            te_wused[nk] = p.ted.v
            se_abcd = se_gprev2

            p.ve.wait_ge(p.sed.h, se_abcd)
            p.ve.tensor_tensor(gb_[:], gd[:], gb_[:], ALU.mult)
            p.ve.drain()
            p.ve.tensor_tensor_scan(hs_[:], ga[:], gb_[:], zcol[:],
                                    ALU.mult, ALU.add)
            p.ve.drain()
            p.ve.tensor_tensor(hT[:, nk * S:(nk + 1) * S], gc[:], hs_[:],
                               ALU.mult).then_inc(p.ved.h, 1)
            ve_db = p.ved.bump()
            ve_dbprev = ve_db
        ve_hT = ve_dbprev

        std_base2 = [p.std0.v, p.std1.v]
        for tc in range(NT):
            sl = tc % 2
            b0 = pq[2 + (tc % 2) * 2]
            b1 = pq[2 + (tc % 2) * 2 + 1]
            if tc == 0:
                p.te.wait_ge(p.ved.h, ve_hT)
            if tc >= 2:
                p.te.wait_ge(p.ved.h, ve_s1[tc - 2])
            for half in range(2):
                bk = b0 if half == 0 else b1
                for j in range(ND):
                    inst = p.te.matmul(
                        bk[:], hT[:, j * S + tc * 128: j * S + (tc + 1) * 128],
                        owT[:, j * D + half * 512: j * D + (half + 1) * 512],
                        start=(j == 0), stop=(j == ND - 1))
            inst.then_inc(p.ted.h, 1)
            te_y = p.ted.bump()
            if tc < 2:
                p.sy.wait_ge(sx[sl].h, sx[sl].v)
                p.sy.wait_ge(p.stx0.h, p.stx0.v)
                p.sy.wait_ge(p.stx1.h, p.stx1.v)
            if tc >= 2:
                p.sy.wait_ge(p.ved.h, ve_s1[tc - 2])
            p.sy.dma_start(out=xt[sl][:], in_=xn1_h.ap()[tc * 128:(tc + 1) * 128, :]) \
                .then_inc(sx[sl].h, 16)
            xv = sx[sl].bump(16)
            p.ve.wait_ge(p.ted.h, te_y)
            p.ve.wait_ge(sx[sl].h, xv)
            if tc >= 2:
                gsl = tc % 2
                std_g = [p.std0, p.std1][gsl]
                p.ve.wait_ge(std_g.h, std_base2[gsl] + 16 * (tc // 2))
            p.ve.tensor_tensor(s1t[sl][:, 0:512], b0[:], obb[:, 0:512], ALU.add)
            p.ve.tensor_tensor(s1t[sl][:, 512:1024], b1[:], obb[:, 512:1024], ALU.add)
            p.ve.drain()
            p.ve.tensor_tensor(s1t[sl][:], s1t[sl][:], xt[sl][:], ALU.add) \
                .then_inc(p.ved.h, 1)
            if tc == 0:
                ve_s1 = {}
            ve_s1[tc] = p.ved.bump()
            std_ = [p.std0, p.std1][tc % 2]
            p.sy.wait_ge(p.ved.h, ve_s1[tc])
            p.sy.dma_start(out=s1_h.ap()[tc * 128:(tc + 1) * 128, :], in_=s1t[sl][:]) \
                .then_inc(std_.h, 16)
            std_.bump(16)

        p.sy.wait_ge(p.std0.h, p.std0.v)
        p.sy.wait_ge(p.std1.h, p.std1.v)
        p.sy.wait_ge(p.stx0.h, p.stx0.v)
        p.sy.wait_ge(p.stx1.h, p.stx1.v)

    return narrative


def make_l1_ffn(nc, dram, pq, pfx="d_"):
    x_h = dram["x1"]
    xn_h = dram["xn1"]
    s_h = dram["s1"]
    nw_h = dram["normw2_row"]
    gw_h = dram["gwT"]
    w1_h = dram["w1T"]
    w2_h = dram["w2T"]
    b1_h = dram["b1_act"]
    b2_h = dram["b2_row"]
    id_h = dram["ident"]
    o_h = dram["out"]
    nc = _NSWrap(nc, pfx)

    NHC2 = MOE_H // 128

    xt = [nc.alloc_sbuf_tensor(f"xt{i}", [128, D], F32) for i in range(2)]
    tmp = nc.alloc_sbuf_tensor("tmp", [128, D], F32)
    tb = [nc.alloc_sbuf_tensor(f"tb{i}", [128, D], BF16) for i in range(2)]
    tT = nc.alloc_sbuf_tensor("tT", [128, ND * S], BF16)
    acc = nc.alloc_sbuf_tensor("acc", [128, NT * D], F32)
    hTh = nc.alloc_sbuf_tensor("hTh", [128, NHC2 * 512], BF16)
    w1e = nc.alloc_sbuf_tensor("w1e", [128, ND * MOE_H], BF16)
    w2e = nc.alloc_sbuf_tensor("w2e", [128, NHC2 * D], BF16)
    gwTs = nc.alloc_sbuf_tensor("gwTs", [128, ND * NE], BF16)
    b1s = nc.alloc_sbuf_tensor("b1s", [128, 128], F32)
    b2r = nc.alloc_sbuf_tensor("b2r", [1, D], BF16)
    b2b = nc.alloc_sbuf_tensor("b2b", [128, D], F32)
    nwb = nc.alloc_sbuf_tensor("nwb", [128, D], F32)
    nwr = nc.alloc_sbuf_tensor("nwr", [1, D], BF16)
    onesb = nc.alloc_sbuf_tensor("onesb", [1, 128], BF16)
    ident = nc.alloc_sbuf_tensor("ident_s", [128, 128], BF16)
    stats = nc.alloc_sbuf_tensor("stats", [128, 32], F32)
    gl = nc.alloc_sbuf_tensor("gl", [128, NT * NE], F32)
    comb = nc.alloc_sbuf_tensor("comb", [128, NT * NE], F32)
    gsc = nc.alloc_sbuf_tensor("gsc", [128, 16], F32)
    _ha = nc.lookup_mloc(hTh).addr
    s1t = nc.alloc_sbuf_tensor_at("s1t", [128, D], F32, offset=_ha)
    ot = nc.alloc_sbuf_tensor_at("ot", [128, D], F32, offset=_ha + 4096)



    def narrative(p):
        p.gp.dma_start(out=ident[:], in_=id_h.ap()).then_inc(p.gpd.h, 16)
        p.gpd.bump(16)
        p.gp.dma_start(out=nwr[:], in_=nw_h.ap()).then_inc(p.gpd.h, 16)
        p.gpd.bump(16)

        p.gp.dma_start(out=b1s[:], in_=b1_h.ap()).then_inc(p.gpd.h, 16)
        p.gpd.bump(16)
        p.gp.dma_start(out=gwTs[:].rearrange("p (j e) -> p j e", j=ND),
                       in_=gw_h.ap().rearrange("(j p) e -> p j e", p=128)) \
            .then_inc(p.gpd.h, 16)
        p.gpd.bump(16)
        gp_all = p.gpd.v
        p.sy.dma_start(out=acc[:].rearrange("p (tc d) -> p tc d", tc=NT),
                       in_=xn_h.ap().rearrange("(tc p) d -> p tc d", p=128)) \
            .then_inc(p.sac.h, 16)
        p.sac.bump(16)

        p.ve.wait_ge(p.gpd.h, gp_all)
        p.ve.memset(onesb[:], 1.0).then_inc(p.ved.h, 1)
        ve_init = p.ved.bump()

        se_bc = []
        for vh in range(2):
            bank = pq[6 + vh]
            p.te.wait_ge(p.ved.h, ve_init)
            p.te.matmul(bank[:], onesb[:], nwr[:, vh * 512:(vh + 1) * 512],
                        start=True, stop=True).then_inc(p.ted.h, 1)
            te_b = p.ted.bump()
            p.se.wait_ge(p.ted.h, te_b)
            p.se.activation(nwb[:, vh * 512:(vh + 1) * 512], bank[:],
                            AF.Identity, scale=1.0).then_inc(p.sed.h, 1)
            se_bc.append(p.sed.bump())
        se_bcast = se_bc[-1]

        sx = [p.sx0, p.sx1]
        se_t = []
        te_tr = []
        ve_evac = []
        for i in range(NT):
            sl = i % 2
            if i >= 2:
                p.sy.wait_ge(p.sed.h, se_t[i - 2])
            p.sy.dma_start(out=xt[sl][:], in_=xn_h.ap()[i * 128:(i + 1) * 128, :]) \
                .then_inc(sx[sl].h, 16)
            xv = sx[sl].bump(16)
            stt = stats[:, 16 * sl:16 * sl + 16]
            p.se.wait_ge(sx[sl].h, xv)
            if i >= 1:
                p.se.wait_ge(p.sed.h, se_t[i - 1])
            if i >= 2:
                p.se.wait_ge(p.ved.h, ve_evac[i - 2])
            p.se.activation(tmp[:], xt[sl][:], AF.Square,
                            accum_out=stt[:, 0:1]).then_inc(p.sed.h, 1)
            se_sq = p.sed.bump()
            p.ve.wait_ge(p.sed.h, se_sq)
            p.ve.tensor_scalar(stt[:, 1:2], stt[:, 0:1], 1.0 / D, 1e-6,
                               ALU.mult, ALU.add).then_inc(p.ved.h, 1)
            ve_ss = p.ved.bump()
            p.se.wait_ge(p.ved.h, ve_ss)
            p.se.activation(stt[:, 2:3], stt[:, 1:2], AF.Sqrt).then_inc(p.sed.h, 1)
            se_rm = p.sed.bump()
            p.ve.wait_ge(p.sed.h, se_rm)
            p.ve.reciprocal(stt[:, 2:3], stt[:, 2:3])
            p.ve.drain()
            if i == 0:
                p.ve.wait_ge(p.sed.h, se_bcast)
            p.ve.tensor_tensor(tmp[:], xt[sl][:], nwb[:], ALU.mult) \
                .then_inc(p.ved.h, 1)
            ve_t = p.ved.bump()
            p.se.wait_ge(p.ved.h, ve_t)
            if i >= 2:
                p.se.wait_ge(p.ted.h, te_tr[i - 2])
            p.se.activation(tb[sl][:], tmp[:], AF.Identity,
                            scale=stt[:, 2:3]).then_inc(p.sed.h, 1)
            se_t.append(p.sed.bump())

            bank = pq[4 + i % 2]
            pb_bf = bank[:].bitcast(BF16)
            p.te.wait_ge(p.sed.h, se_t[i])
            if i == 0:
                p.te.wait_ge(p.gpd.h, gp_all)
            if i >= 2:
                p.te.wait_ge(p.ved.h, ve_evac[i - 2])
            for j in range(ND):
                inst = p.te.transpose(pb_bf[:, j * 128:(j + 1) * 128],
                                      tb[sl][:, j * 128:(j + 1) * 128], ident[:])
            inst.then_inc(p.ted.h, 1)
            te_tr.append(p.ted.bump())
            p.ve.wait_ge(p.ted.h, te_tr[i])
            p.ve.tensor_copy(
                strided_ap(tT, i * 128, [[S, ND], [1, 128]]),
                pb_bf[:].rearrange("p (j c) -> p j c", j=ND)).then_inc(p.ved.h, 1)
            ve_evac.append(p.ved.bump())
        ve_tT = ve_evac[-1]

        se_lg = {}
        for tc in range(NT):
            bank = pq[tc % 2]
            p.te.wait_ge(p.ved.h, ve_tT)
            if tc >= 2:
                p.te.wait_ge(p.sed.h, se_lg[tc - 2])
            for j in range(ND):
                inst = p.te.matmul(bank[:, 0:NE],
                                   tT[:, j * S + tc * 128: j * S + (tc + 1) * 128],
                                   gwTs[:, j * NE:(j + 1) * NE],
                                   start=(j == 0), stop=(j == ND - 1))
            inst.then_inc(p.ted.h, 1)
            te_g = p.ted.bump()
            p.se.wait_ge(p.ted.h, te_g)
            p.se.activation(gl[:, tc * NE:(tc + 1) * NE], bank[:, 0:NE],
                            AF.Identity, scale=1.0).then_inc(p.sed.h, 1)
            se_lg[tc] = p.sed.bump()
        se_logits = se_lg[NT - 1]

        ve_cb = {}
        for tc in range(NT):
            g8 = gl[:, tc * NE:(tc + 1) * NE]
            c8 = comb[:, tc * NE:(tc + 1) * NE]
            p.ve.wait_ge(p.sed.h, se_lg[tc])
            p.ve.tensor_reduce(gsc[:, 0:1], g8, axis=AX.X, op=ALU.max)
            p.ve.drain()
            p.ve.tensor_scalar(c8, g8, gsc[:, 0:1], 1e30, ALU.is_equal, ALU.mult)
            p.ve.drain()
            p.ve.tensor_tensor(c8, g8, c8, ALU.subtract)
            p.ve.drain()
            p.ve.tensor_reduce(gsc[:, 1:2], c8, axis=AX.X, op=ALU.max)
            p.ve.drain()
            p.ve.tensor_scalar(c8, g8, gsc[:, 1:2], None, ALU.is_ge)
            p.ve.tensor_scalar(gsc[:, 2:3], gsc[:, 0:1], -1.0, None, ALU.mult) \
                .then_inc(p.ved.h, 1)
            ve_i = p.ved.bump()
            p.se.wait_ge(p.ved.h, ve_i)
            p.se.activation(gsc[:, 4:12], g8, AF.Exp, bias=gsc[:, 2:3]) \
                .then_inc(p.sed.h, 1)
            se_e = p.sed.bump()
            p.ve.wait_ge(p.sed.h, se_e)
            p.ve.tensor_tensor(c8, c8, gsc[:, 4:12], ALU.mult)
            p.ve.drain()
            p.ve.tensor_reduce(gsc[:, 12:13], c8, axis=AX.X, op=ALU.add)
            p.ve.drain()
            p.ve.reciprocal(gsc[:, 12:13], gsc[:, 12:13])
            p.ve.drain()
            p.ve.tensor_scalar(c8, c8, gsc[:, 12:13], None, ALU.mult) \
                .then_inc(p.ved.h, 1)
            ve_cb[tc] = p.ved.bump()
        ve_comb = ve_cb[NT - 1]

        sw1, sw2 = p.gw0, p.gw1
        se_hprev = {}
        ve_accprev = {}
        ve_b2used = None
        te_hc3 = {}
        te_hc3_prev = None
        for e in range(NE):
            if e >= 1:
                p.gp.wait_ge(p.sed.h, se_b2prev)
            p.gp.dma_start(out=b2r[:], in_=b2_h.ap()[:, e * D:(e + 1) * D]) \
                .then_inc(p.sb2.h, 16)
            gb2v = p.sb2.bump(16)
            w1qv = []
            qsem = [sw1, p.gw3]
            for q in range(4):
                sq = qsem[q % 2]
                if e >= 1:
                    p.gp.wait_ge(p.ted.h, te_hc3_prev[q * 4 + 3])
                if q >= 2:
                    p.gp.wait_ge(sq.h, sq.v)
                p.gp.dma_start(
                    out=w1e[:].rearrange("p (j h) -> p j h", j=ND)
                    [:, :, q * 512:(q + 1) * 512],
                    in_=w1_h.ap()[e].rearrange("(j p) h -> p j h", p=128)
                    [:, :, q * 512:(q + 1) * 512]).then_inc(sq.h, 16)
                w1qv.append((sq, sq.bump(16)))
            if e >= 1:
                p.gp.wait_ge(p.ted.h, te_b_done)
            p.gp.dma_start(out=w2e[:].rearrange("p (j d) -> p j d", j=NHC2),
                           in_=w2_h.ap()[e].rearrange("(j p) d -> p j d", p=128)) \
                .then_inc(sw2.h, 16)
            w2v = sw2.bump(16)

            p.te.wait_ge(p.sb2.h, gb2v)
            if ve_b2used is not None:
                p.te.wait_ge(p.ved.h, ve_b2used)
            for vh in range(2):
                bank = pq[4 + vh]
                p.te.matmul(bank[:], onesb[:], b2r[:, vh * 512:(vh + 1) * 512],
                            start=True, stop=True).then_inc(p.ted.h, 1)
                te_b2 = p.ted.bump()
                p.se.wait_ge(p.ted.h, te_b2)
                p.se.activation(b2b[:, vh * 512:(vh + 1) * 512], bank[:],
                                AF.Identity, scale=1.0).then_inc(p.sed.h, 1)
                se_b2prev = p.sed.bump()

            for hf in range(4):
                if e == 0 and hf == 0:
                    p.te.wait_ge(p.ved.h, ve_tT)
                    p.te.wait_ge(p.sed.h, se_logits)
                for hc in range(NHC2):
                    if hf == 0 and hc % 4 == 0:
                        sq, qv = w1qv[hc // 4]
                        p.te.wait_ge(sq.h, qv)
                    bank = pq[hc % 4]
                    kk = se_hprev.get(hc % 4)
                    if kk is not None:
                        p.te.wait_ge(p.sed.h, kk)
                    for j in range(ND):
                        inst = p.te.matmul(
                            bank[:],
                            w1e[:, j * MOE_H + hc * 128: j * MOE_H + (hc + 1) * 128],
                            tT[:, j * S + hf * 512: j * S + (hf + 1) * 512],
                            start=(j == 0), stop=(j == ND - 1))
                    inst.then_inc(p.ted.h, 1)
                    te_h = p.ted.bump()
                    if hf == 3:
                        te_hc3[hc] = te_h
                    p.se.wait_ge(p.ted.h, te_h)
                    p.se.activation(
                        hTh[:, hc * 512:(hc + 1) * 512],
                        bank[:], AF.Gelu, bias=b1s[:, e * 16 + hc: e * 16 + hc + 1]) \
                        .then_inc(p.sed.h, 1)
                    se_hprev[hc % 4] = p.sed.bump()
                te_a_done = p.ted.v
                se_a_done = p.sed.v

                p.te.wait_ge(sw2.h, w2v)
                p.te.wait_ge(p.sed.h, se_a_done)
                for tr in range(4):
                    tc = hf * 4 + tr
                    for dh in range(2):
                        bank = pq[4 + (tr % 2) * 2 + dh]
                        kk = ve_accprev.get((tr % 2) * 2 + dh)
                        if kk is not None:
                            p.te.wait_ge(p.ved.h, kk)
                        for j in range(NHC2):
                            inst = p.te.matmul(
                                bank[:],
                                hTh[:, j * 512 + tr * 128: j * 512 + (tr + 1) * 128],
                                w2e[:, j * D + dh * 512: j * D + (dh + 1) * 512],
                                start=(j == 0), stop=(j == NHC2 - 1))
                        inst.then_inc(p.ted.h, 1)
                        te_o = p.ted.bump()
                        p.ve.wait_ge(p.ted.h, te_o)
                        if e == 0 and tc < 2:
                            p.ve.wait_ge(p.sac.h, p.sac.v)
                        if e == 0 and tc == 0 and dh == 0:
                            p.ve.wait_ge(p.ved.h, ve_comb)
                        if tc == 0 and dh == 0:
                            p.ve.wait_ge(p.sed.h, se_b2prev)
                        a_sl = acc[:, tc * D + dh * 512: tc * D + (dh + 1) * 512]
                        ce = comb[:, tc * NE + e: tc * NE + e + 1]
                        p.ve.scalar_tensor_tensor(a_sl, bank[:], ce, a_sl,
                                                  ALU.mult, ALU.add)
                        p.ve.drain()
                        p.ve.scalar_tensor_tensor(
                            a_sl, b2b[:, dh * 512:(dh + 1) * 512], ce, a_sl,
                            ALU.mult, ALU.add).then_inc(p.ved.h, 1)
                        ve_accprev[(tr % 2) * 2 + dh] = p.ved.bump()
                        p.ve.drain()
                te_b_done = p.ted.v
            ve_b2used = p.ved.v
            te_hc3_prev = dict(te_hc3)

        ve_moe = p.ved.v

        std_base3 = [p.std0.v, p.std1.v]
        ve_fin = {}
        for tc in range(NT):
            sl = tc % 2
            if tc < 2:
                p.sy.wait_ge(sx[sl].h, sx[sl].v)
                p.sy.wait_ge(p.ved.h, ve_tT)
            if tc >= 2:
                p.sy.wait_ge(p.ved.h, ve_fin[tc - 2])
            p.sy.dma_start(out=xt[sl][:], in_=x_h.ap()[tc * 128:(tc + 1) * 128, :]) \
                .then_inc(sx[sl].h, 16)
            sx[sl].bump(16)
            if tc == 0:
                p.sy.wait_ge(p.ved.h, ve_moe)
            if tc >= 1:
                p.sy.wait_ge(p.ved.h, ve_fin[tc - 1])
            p.sy.dma_start(out=s1t[:], in_=s_h.ap()[tc * 128:(tc + 1) * 128, :]) \
                .then_inc(sx[sl].h, 16)
            sv = sx[sl].bump(16)
            p.se.wait_ge(p.ved.h, ve_moe)
            if tc >= 1:
                p.se.wait_ge(p.ved.h, ve_fin[tc - 1])
            p.se.activation(tmp[:], acc[:, tc * D:(tc + 1) * D], AF.Silu) \
                .then_inc(p.sed.h, 1)
            se_g = p.sed.bump()
            p.ve.wait_ge(p.sed.h, se_g)
            p.ve.wait_ge(sx[sl].h, sv)
            if tc >= 2:
                gsl = tc % 2
                std_g = [p.std0, p.std1][gsl]
                p.ve.wait_ge(std_g.h, std_base3[gsl] + 16 * (tc // 2))
            if tc >= 1:
                std_p = [p.std0, p.std1][(tc - 1) % 2]
                p.ve.wait_ge(std_p.h, std_p.v)
            p.ve.tensor_tensor(ot[:], s1t[:], tmp[:], ALU.mult)
            p.ve.drain()
            p.ve.tensor_tensor(ot[:], xt[sl][:], ot[:], ALU.add) \
                .then_inc(p.ved.h, 1)
            ve_fin[tc] = p.ved.bump()
            std_ = [p.std0, p.std1][tc % 2]
            p.sy.wait_ge(p.ved.h, ve_fin[tc])
            p.sy.dma_start(out=o_h.ap()[tc * 128:(tc + 1) * 128, :], in_=ot[:]) \
                .then_inc(std_.h, 16)
            std_.bump(16)

        p.sy.wait_ge(p.std0.h, p.std0.v)
        p.sy.wait_ge(p.std1.h, p.std1.v)

    return narrative


_EXT_INPUTS = [
    ("x", [S, D]), ("wqkvT", [D, 3 * D]), ("bq_act", [128, 8]),
    ("bk_act", [128, 8]), ("bv_row", [1, D]), ("masks", [4, 128, 512]),
    ("ident", [128, 128]), ("wgu", [2 * FFN_I, D]), ("wd", [D, FFN_I]),
    ("wabcdT", [D, 4 * D]), ("ab_act", [128, 8]), ("outwT", [D, D]),
    ("outb_row", [1, D]), ("normw_row", [1, D]), ("normw2_row", [1, D]),
    ("gwT", [D, NE]), ("w1T", [NE, D, MOE_H]), ("w2T", [NE, MOE_H, D]),
    ("b1_act", [128, 128]), ("b2_row", [1, NE * D]),
]
_INTERNALS = [("s0", [S, D]), ("x1", [S, D]), ("xn1", [S, D]), ("s1", [S, D])]
def _wq_units():
    units = []
    for i in range((FFN_I + 511) // 512):
        hs0, hw = i * 512, min(512, FFN_I - i * 512)
        for gu in range(2):
            for rp in range((hw // 128) // 2):
                row0 = gu * FFN_I + hs0 + rp * 256
                units.append(row0)
                units.append(row0 + 128)
    return units


WQ_UNITS = _wq_units()

_INTERNALS_BF = [("wq_pre", [len(WQ_UNITS) * 128, D])]


def build_fused():
    nc = bass.Bass("TRN2", target_bir_lowering=False)
    dram = {}
    for n, sh in _EXT_INPUTS:
        dram[n] = nc.dram_tensor(n, sh, F32, kind="ExternalInput")
    for n, sh in _INTERNALS:
        dram[n] = nc.dram_tensor(n, sh, F32, kind="Internal")
    for n, sh in _INTERNALS_BF:
        dram[n] = nc.dram_tensor(n, sh, BF16, kind="Internal")
    dram["out"] = nc.dram_tensor("out", [S, D], F32, kind="ExternalOutput")
    pq = [nc.alloc_psum_tensor(f"pq{i}", [128, 512], F32) for i in range(8)]
    sems = alloc_sems(nc, SEM_NAMES)
    base = nc.sbuf_base
    nc.sbuf_base = base
    narrA = make_l0_attn(nc, dram, pq)
    nc.sbuf_base = base
    narrB = make_l0_ffn(nc, dram, pq)
    nc.sbuf_base = base
    narrC = make_l1_attn(nc, dram, pq)
    nc.sbuf_base = base
    narrD = make_l1_ffn(nc, dram, pq)

    def narrative(p):
        narrA(p)
        phase_barrier(p)
        narrB(p)
        phase_barrier(p)
        narrC(p)
        phase_barrier(p)
        narrD(p)

    run_block(nc, sems, narrative)
    return nc


def _build_single(make_fn, in_names, internal_names, out_names):
    nc = bass.Bass("TRN2", target_bir_lowering=False)
    shapes = dict(_EXT_INPUTS + _INTERNALS + [("out", [S, D])])
    dram = {}
    for n in in_names:
        dram[n] = nc.dram_tensor(n, shapes[n], F32, kind="ExternalInput")
    for n in internal_names:
        dram[n] = nc.dram_tensor(n, shapes[n], F32, kind="Internal")
    for n in out_names:
        dram[n] = nc.dram_tensor(n, shapes[n], F32, kind="ExternalOutput")
    pq = [nc.alloc_psum_tensor(f"pq{i}", [128, 512], F32) for i in range(8)]
    sems = alloc_sems(nc, SEM_NAMES)
    run_block(nc, sems, make_fn(nc, dram, pq))
    return nc


def build_l0_attn():
    return _build_single(make_l0_attn,
                         ["x", "wqkvT", "bq_act", "bk_act", "bv_row", "masks",
                          "ident"], [], ["s0"])


def build_l0_ffn():
    return _build_single(make_l0_ffn,
                         ["x", "s0", "wgu", "wd", "ident"], [], ["x1"])


def build_l1_attn():
    return _build_single(make_l1_attn,
                         ["x1", "wabcdT", "ab_act", "outwT", "outb_row",
                          "normw_row", "ident"], [], ["s1", "xn1"])


def build_l1_ffn():
    return _build_single(make_l1_ffn,
                         ["x1", "xn1", "s1", "normw2_row", "gwT", "w1T", "w2T",
                          "b1_act", "b2_row", "ident"], [], ["out"])


def _prep_inputs(inputs):
    x = np.ascontiguousarray(np.asarray(inputs["L0_attn_x"], dtype=np.float32))
    Wq = np.asarray(inputs["L0_attn_Wq"], dtype=np.float32).reshape(D, D)
    Wk = np.asarray(inputs["L0_attn_Wk"], dtype=np.float32).reshape(D, D)
    Wv = np.asarray(inputs["L0_attn_Wv"], dtype=np.float32).reshape(D, D)
    wqkvT = np.ascontiguousarray(
        np.concatenate([Wq.T, Wk.T, Wv.T], axis=1).astype(np.float32))
    bq = np.asarray(inputs["L0_attn_bq"], dtype=np.float32).reshape(H * HD)
    bk = np.asarray(inputs["L0_attn_bk"], dtype=np.float32).reshape(H * HD)
    bv = np.asarray(inputs["L0_attn_bv"], dtype=np.float32).reshape(1, H * HD)
    m = np.arange(4)[:, None, None]
    pp = np.arange(128)[None, :, None]
    ff = np.arange(512)[None, None, :]
    shared = {
        "wqkvT": wqkvT,
        "bq_act": np.ascontiguousarray(bq.reshape(8, 128).T),
        "bk_act": np.ascontiguousarray(bk.reshape(8, 128).T),
        "bv_row": np.ascontiguousarray(bv),
        "masks": np.ascontiguousarray((ff >= 128 * m + pp).astype(np.float32)),
        "ident": np.eye(128, dtype=np.float32),
        "wgu": np.ascontiguousarray(np.concatenate(
            [np.asarray(inputs["L0_ffn_w_gate"], dtype=np.float32),
             np.asarray(inputs["L0_ffn_w_up"], dtype=np.float32)], axis=0)),
        "wd": np.ascontiguousarray(
            np.asarray(inputs["L0_ffn_w_down"], dtype=np.float32)),
        "wabcdT": np.ascontiguousarray(np.concatenate(
            [np.asarray(inputs["L1_attn_A_w"], dtype=np.float32).T,
             np.asarray(inputs["L1_attn_B_w"], dtype=np.float32).T,
             np.asarray(inputs["L1_attn_C_w"], dtype=np.float32).T,
             np.asarray(inputs["L1_attn_D_w"], dtype=np.float32).T], axis=1)),
        "ab_act": np.ascontiguousarray(
            np.asarray(inputs["L1_attn_A_b"], dtype=np.float32).reshape(8, 128).T),
        "outwT": np.ascontiguousarray(
            np.asarray(inputs["L1_attn_out_w"], dtype=np.float32).T),
        "outb_row": np.ascontiguousarray(
            np.asarray(inputs["L1_attn_out_b"], dtype=np.float32).reshape(1, D)),
        "normw_row": np.ascontiguousarray(
            np.asarray(inputs["L1_attn_norm_w"], dtype=np.float32).reshape(1, D)),
        "normw2_row": np.ascontiguousarray(
            np.asarray(inputs["L1_ffn_norm_weight"],
                       dtype=np.float32).reshape(1, D)),
        "gwT": np.ascontiguousarray(
            np.asarray(inputs["L1_ffn_gate_w"], dtype=np.float32).T),
        "w1T": np.ascontiguousarray(
            np.asarray(inputs["L1_ffn_w1"], dtype=np.float32).transpose(0, 2, 1)),
        "w2T": np.ascontiguousarray(
            np.asarray(inputs["L1_ffn_w2"], dtype=np.float32).transpose(0, 2, 1)),
        "b1_act": np.ascontiguousarray(
            np.asarray(inputs["L1_ffn_b1"], dtype=np.float32)
            .reshape(8, 16, 128).transpose(2, 0, 1).reshape(128, 128)),
        "b2_row": np.ascontiguousarray(
            np.asarray(inputs["L1_ffn_b2"], dtype=np.float32).reshape(1, NE * D)),
    }
    return shared, x


def kernel(**inputs):
    shared, x = _prep_inputs(inputs)
    nc = build_fused()
    maps = [dict(shared, x=np.ascontiguousarray(x[b])) for b in range(B)]
    res = run_bass_kernel_spmd(nc, maps, core_ids=list(range(B)))
    return np.stack([r["out"] for r in res.results])


# version 13 (final)
# speedup vs baseline: 1.0579x; 1.0063x over previous
import numpy as np
import ml_dtypes
import concourse.bass as bass
import concourse.mybir as mybir
from concourse.bass_types import AP
from concourse.bass_utils import run_bass_kernel_spmd

F32 = mybir.dt.float32
BF16 = mybir.dt.bfloat16
I32 = mybir.dt.int32
AF = mybir.ActivationFunctionType
ALU = mybir.AluOpType
AX = mybir.AxisListType

B, S, D = 8, 2048, 1024
NT = S // 128
ND = D // 128
H, HD = 16, 64
NP = H // 2
FFN_I = 2816
MOE_H = 2048
NE = 8


class _NullInst:
    def then_inc(self, *a, **k):
        return self


class _NullEng:
    def __getattr__(self, name):
        def f(*a, **k):
            return _NullInst()
        return f


class Ctr:
    def __init__(self, handle):
        self.h = handle
        self.v = 0

    def bump(self, n=1):
        self.v += n
        return self.v


class Prog:
    def __init__(self, which, eng, sems):
        null = _NullEng()
        self.which = which
        self.te = eng if which == "tensor" else null
        self.ve = eng if which == "vector" else null
        self.se = eng if which == "scalar" else null
        self.gp = eng if which == "gpsimd" else null
        self.sy = eng if which == "sync" else null
        for name, h in sems.items():
            setattr(self, name, Ctr(h))


def run_block(nc, sems, narrative):
    with nc.Block() as block:
        @block.sync
        def _(e):
            narrative(Prog("sync", e, sems))

        @block.gpsimd
        def _(e):
            narrative(Prog("gpsimd", e, sems))

        @block.tensor
        def _(e):
            narrative(Prog("tensor", e, sems))

        @block.vector
        def _(e):
            narrative(Prog("vector", e, sems))

        @block.scalar
        def _(e):
            narrative(Prog("scalar", e, sems))


def alloc_sems(nc, names):
    return {n: nc.semaphore(n).__enter__() for n in names}


SEM_NAMES = ["sx0", "sx1", "sw0", "sw1", "sw2", "sw3", "gw0", "gw1", "gw2",
             "gw3", "gpd", "sac", "sb2", "ted", "ved", "sed", "std0", "std1",
             "stx0", "stx1", "pod"]


class _NSWrap:

    def __init__(self, nc, pfx):
        self._nc = nc
        self._pfx = pfx

    def alloc_sbuf_tensor(self, name, shape, dtype, **kw):
        return self._nc.alloc_sbuf_tensor(self._pfx + name, shape, dtype, **kw)

    def alloc_sbuf_tensor_at(self, name, shape, dtype, **kw):
        return self._nc.alloc_sbuf_tensor_at(self._pfx + name, shape, dtype, **kw)

    def __getattr__(self, a):
        return getattr(self._nc, a)


def phase_barrier(p):
    ctrs = [getattr(p, n) for n in SEM_NAMES]
    for eng in (p.te, p.ve, p.se, p.gp, p.sy):
        for c in ctrs:
            if c.v > 0:
                eng.wait_ge(c.h, c.v)


def strided_ap(t, offset, dims):
    base = t[:].ap[0]
    return AP(t[:].tensor, offset, [list(base)] + [list(d) for d in dims])


def make_l0_attn(nc, dram, pq, pfx="a_"):
    x_h = dram["x"]
    w_h = dram["wqkvT"]
    bq_h = dram["bq_act"]
    bk_h = dram["bk_act"]
    bv_h = dram["bv_row"]
    mk_h = dram["masks"]
    id_h = dram["ident"]
    s0_h = dram["s0"]
    wgu_h = dram.get("wgu")
    wqp_h = dram.get("wq_pre")
    _sb = nc.alloc_sbuf_tensor
    def alloc_sbuf(name, shape, dtype):
        return _sb(pfx + name, shape, dtype)
    nc = _NSWrap(nc, pfx)

    xt = [nc.alloc_sbuf_tensor(f"xt{i}", [128, D], F32) for i in range(2)]
    xnb = [nc.alloc_sbuf_tensor(f"xnb{i}", [128, D], BF16) for i in range(2)]
    xnT = nc.alloc_sbuf_tensor("xnT", [128, ND * S], BF16)
    wT = nc.alloc_sbuf_tensor("wT", [128, ND * 3 * D], BF16)
    qT = nc.alloc_sbuf_tensor("qT", [128, NP * S], BF16)
    kT = nc.alloc_sbuf_tensor("kT", [128, NP * S], BF16)
    vsb = nc.alloc_sbuf_tensor("vsb", [128, NT * H * 65], BF16)
    _wa = nc.lookup_mloc(wT).addr
    sTqe = nc.alloc_sbuf_tensor_at("sTqe", [65, NP * 512], F32, offset=_wa)
    sTqo = nc.alloc_sbuf_tensor_at("sTqo", [65, NP * 512], F32, offset=_wa + 16384)
    expA = nc.alloc_sbuf_tensor_at("expA", [128, 2 * 512], BF16, offset=_wa + 32768)
    expB = nc.alloc_sbuf_tensor_at("expB", [128, 2 * 512], BF16, offset=_wa + 34816)
    s0c = [nc.alloc_sbuf_tensor_at(f"s0c{i}", [128, H * 65], F32,
                                   offset=_wa + 36864 + i * 4160) for i in range(2)]
    rsum = nc.alloc_sbuf_tensor("rsum", [128, 16], F32)
    identf = nc.alloc_sbuf_tensor("identf", [128, 128], F32)
    _xa = nc.lookup_mloc(xnT).addr
    wrowA = [nc.alloc_sbuf_tensor_at(f"wrowA{i}", [128, D], F32,
                                     offset=_xa + i * 4096) for i in range(2)]
    q1A = nc.alloc_sbuf_tensor_at("q1A", [128, D], F32, offset=_xa + 8192)
    q2A = nc.alloc_sbuf_tensor_at("q2A", [128, D], F32, offset=_xa + 12288)
    qbfA = [nc.alloc_sbuf_tensor_at(f"qbfA{i}", [128, D], BF16,
                                    offset=_xa + 16384 + i * 2048) for i in range(2)]
    blkA = nc.alloc_sbuf_tensor_at("blkA", [128, 256], F32, offset=_xa + 20480)
    masks = nc.alloc_sbuf_tensor("masks_s", [128, 4 * 512], BF16)
    ident = nc.alloc_sbuf_tensor("ident_s", [128, 128], BF16)
    bqs = nc.alloc_sbuf_tensor("bqs", [128, 8], F32)
    bks = nc.alloc_sbuf_tensor("bks", [128, 8], F32)
    bvb = nc.alloc_sbuf_tensor("bvb", [128, D], F32)
    bvr = nc.alloc_sbuf_tensor("bvr", [1, D], BF16)
    onesb = nc.alloc_sbuf_tensor("onesb", [1, 128], BF16)
    onesf = nc.alloc_sbuf_tensor("onesf", [1, 64], F32)
    stats = nc.alloc_sbuf_tensor("stats", [128, 32], F32)
    rrow = nc.alloc_sbuf_tensor("rrow", [1, 512], F32)



    def narrative(p):
        for j in range(ND):
            p.gp.dma_start(out=wT[:, j * 3 * D:(j + 1) * 3 * D],
                           in_=w_h.ap()[j * 128:(j + 1) * 128, :]).then_inc(p.gpd.h, 16)
            p.gpd.bump(16)
        p.gp.dma_start(out=masks[:].rearrange("p (m f) -> p m f", m=4),
                       in_=mk_h.ap().rearrange("m p f -> p m f")).then_inc(p.gpd.h, 16)
        p.gpd.bump(16)
        p.gp.dma_start(out=ident[:], in_=id_h.ap()).then_inc(p.gpd.h, 16)
        p.gpd.bump(16)
        p.gp.dma_start(out=bqs[:], in_=bq_h.ap()).then_inc(p.gpd.h, 16)
        p.gpd.bump(16)
        p.gp.dma_start(out=bks[:], in_=bk_h.ap()).then_inc(p.gpd.h, 16)
        p.gpd.bump(16)
        p.gp.dma_start(out=bvr[:], in_=bv_h.ap()).then_inc(p.gpd.h, 16)
        p.gpd.bump(16)
        p.gp.dma_start(out=identf[:], in_=id_h.ap()).then_inc(p.gpd.h, 16)
        p.gpd.bump(16)
        gp_all = p.gpd.v

        p.ve.wait_ge(p.gpd.h, gp_all)
        p.ve.tensor_scalar(bqs[:], bqs[:], 0.125, None, ALU.mult)
        p.ve.memset(onesb[:], 1.0)
        p.ve.memset(onesf[:], 1.0)
        p.ve.memset(strided_ap(vsb, 64, [[65, NT * H], [1, 1]]), 1.0) \
            .then_inc(p.ved.h, 1)
        ve_init = p.ved.bump()

        sx = [p.sx0, p.sx1]
        se_xn = []
        ve_evac = []
        for i in range(NT):
            sl = i % 2
            if i >= 2:
                p.sy.wait_ge(p.sed.h, se_xn[i - 2])
            p.sy.dma_start(out=xt[sl][:], in_=x_h.ap()[i * 128:(i + 1) * 128, :]) \
                .then_inc(sx[sl].h, 16)
            xv = sx[sl].bump(16)

            p.ve.wait_ge(sx[sl].h, xv)
            if i >= 2:
                p.ve.wait_ge(p.sed.h, se_xn[i - 2])
            sb16 = 16 * sl
            st_ = stats[:, sb16:sb16 + 16]
            p.ve.bn_stats(st_[:, 0:6], xt[sl][:, 0:512])
            p.ve.bn_stats(st_[:, 6:12], xt[sl][:, 512:1024])
            p.ve.drain()
            p.ve.bn_aggr(st_[:, 12:14], st_[:, 0:12])
            p.ve.drain()
            p.ve.tensor_scalar(st_[:, 14:15], st_[:, 13:14], 1e-5, None, ALU.add) \
                .then_inc(p.ved.h, 1)
            ve_st = p.ved.bump()
            p.se.wait_ge(p.ved.h, ve_st)
            p.se.activation(st_[:, 15:16], st_[:, 14:15], AF.Sqrt).then_inc(p.sed.h, 1)
            se_sq = p.sed.bump()
            p.ve.wait_ge(p.sed.h, se_sq)
            p.ve.reciprocal(st_[:, 15:16], st_[:, 15:16])
            p.ve.drain()
            p.ve.scalar_tensor_tensor(st_[:, 12:13], st_[:, 12:13], -1.0,
                                      st_[:, 15:16], ALU.mult, ALU.mult) \
                .then_inc(p.ved.h, 1)
            ve_rs = p.ved.bump()
            p.se.wait_ge(p.ved.h, ve_rs)
            p.se.activation(xnb[sl][:], xt[sl][:], AF.Identity,
                            bias=st_[:, 12:13], scale=st_[:, 15:16]) \
                .then_inc(p.sed.h, 1)
            se_xn.append(p.sed.bump())

            bank = pq[i % 2]
            pb_bf = bank[:].bitcast(BF16)
            p.te.wait_ge(p.sed.h, se_xn[i])
            if i >= 2:
                p.te.wait_ge(p.ved.h, ve_evac[i - 2])
            if i == 0:
                p.te.wait_ge(p.gpd.h, gp_all)
            for j in range(ND):
                inst = p.te.transpose(pb_bf[:, j * 128:(j + 1) * 128],
                                      xnb[sl][:, j * 128:(j + 1) * 128], ident[:])
                if j == ND - 1:
                    inst.then_inc(p.ted.h, 1)
            te_tr = p.ted.bump()

            p.ve.wait_ge(p.ted.h, te_tr)
            p.ve.tensor_copy(
                strided_ap(xnT, i * 128, [[S, ND], [1, 128]]),
                pb_bf[:].rearrange("p (j c) -> p j c", j=ND)).then_inc(p.ved.h, 1)
            ve_evac.append(p.ved.bump())
        ve_xnT = ve_evac[-1]

        def xnT_c(j):
            return xnT[:, j * S:(j + 1) * S]

        def wT_c(j, c0, c1):
            return wT[:, j * 3 * D + c0: j * 3 * D + c1]

        p.te.wait_ge(p.ved.h, ve_xnT)
        se_qkv = []
        for which, dst, bias, scl in (("q", qT, bqs, 0.125), ("k", kT, bks, 1.0)):
            base = 0 if which == "q" else D
            for hp in range(NP):
                for qt in range(4):
                    bank = pq[len(se_qkv) % 2]
                    if len(se_qkv) >= 2:
                        p.te.wait_ge(p.sed.h, se_qkv[-2])
                    for j in range(ND):
                        inst = p.te.matmul(bank[:],
                                           wT_c(j, base + hp * 128, base + hp * 128 + 128),
                                           xnT_c(j)[:, qt * 512:(qt + 1) * 512],
                                           start=(j == 0), stop=(j == ND - 1))
                    inst.then_inc(p.ted.h, 1)
                    te_mm = p.ted.bump()
                    p.se.wait_ge(p.ted.h, te_mm)
                    if not se_qkv:
                        p.se.wait_ge(p.ved.h, ve_init)
                    p.se.activation(dst[:, hp * S + qt * 512: hp * S + (qt + 1) * 512],
                                    bank[:], AF.Identity, bias=bias[:, hp:hp + 1],
                                    scale=scl).then_inc(p.sed.h, 1)
                    se_qkv.append(p.sed.bump())

        ve_bvb = None
        for vh in range(2):
            bank = pq[2 + vh]
            p.te.wait_ge(p.ved.h, ve_init)
            p.te.matmul(bank[:], onesb[:], bvr[:, vh * 512:(vh + 1) * 512],
                        start=True, stop=True).then_inc(p.ted.h, 1)
            te_b = p.ted.bump()
            p.ve.wait_ge(p.ted.h, te_b)
            p.ve.tensor_copy(bvb[:, vh * 512:(vh + 1) * 512], bank[:]) \
                .then_inc(p.ved.h, 1)
            ve_bvb = p.ved.bump()

        ve_v = []
        for tc in range(NT):
            for vh in range(2):
                bank = pq[4 + (len(ve_v) % 2)]
                if len(ve_v) >= 2:
                    p.te.wait_ge(p.ved.h, ve_v[-2])
                for j in range(ND):
                    inst = p.te.matmul(bank[:],
                                       xnT_c(j)[:, tc * 128:(tc + 1) * 128],
                                       wT_c(j, 2 * D + vh * 512, 2 * D + (vh + 1) * 512),
                                       start=(j == 0), stop=(j == ND - 1))
                inst.then_inc(p.ted.h, 1)
                te_mm = p.ted.bump()
                p.ve.wait_ge(p.ted.h, te_mm)
                if not ve_v:
                    p.ve.wait_ge(p.ved.h, ve_bvb)
                dst = strided_ap(vsb, tc * H * 65 + vh * 8 * 65, [[65, 8], [1, 64]])
                p.ve.tensor_tensor(dst, bank[:].rearrange("p (h c) -> p h c", h=8),
                                   bvb[:, vh * 512:(vh + 1) * 512]
                                   .rearrange("p (h c) -> p h c", h=8),
                                   ALU.add).then_inc(p.ved.h, 1)
                ve_v.append(p.ved.bump())
        ve_vall = ve_v[-1]
        te_v_done = p.ted.v

        wq_state = {"u": 0, "head": {}, "pt": {}}

        def emit_wq_unit():
            u = wq_state["u"]
            if wgu_h is None or u >= len(WQ_UNITS):
                return
            wq_state["u"] = u + 1
            sl_ = u % 2
            if u < 2:
                p.gp.wait_ge(p.ted.h, te_v_done)
            else:
                p.gp.wait_ge(p.ved.h, wq_state["head"][u - 2])
            srow = WQ_UNITS[u]
            p.gp.dma_start(out=wrowA[sl_][:],
                           in_=wgu_h.ap()[srow:srow + 128, :]) \
                .then_inc(p.gw0.h, 16)
            gv = p.gw0.bump(16)
            st_ = [p.gw1, p.gw2][sl_]
            p.ve.wait_ge(p.gw0.h, gv)
            if u >= 2:
                p.ve.wait_ge(st_.h, st_.v)
            vh, pt = _emit_quant(p, qbfA[sl_][:], wrowA[sl_][:], q1A, q2A,
                                 blkA, D, boff=0,
                                 wait_pod=wq_state["pt"].get(u - 1))
            wq_state["head"][u] = vh
            wq_state["pt"][u] = pt
            p.gp.wait_ge(p.pod.h, pt)
            p.gp.dma_start(out=wqp_h.ap()[u * 128:(u + 1) * 128, :],
                           in_=qbfA[sl_][:]).then_inc(st_.h, 16)
            st_.bump(16)

        std_base = [p.std0.v, p.std1.v]
        se_exp_prev = None
        se_stq_prev = None
        ve_str_last = None
        ve_tail02 = None
        ve_tail13 = None
        te_str = None
        for qt in range(4):
            nkc = (qt + 1) * 4
            for hp in range(NP):
                se_exp = {}
                te_av = {}
                for kc in range(nkc):
                    sa_ = pq[kc % 2]
                    sb_ = pq[2 + kc % 2]
                    if kc == 0:
                        p.te.wait_ge(p.ved.h, ve_vall)
                        p.te.wait_ge(p.sed.h, se_qkv[-1])
                        if se_exp_prev is not None:
                            p.te.wait_ge(p.sed.h, se_exp_prev)
                        if hp == 0 and ve_tail02 is not None:
                            p.te.wait_ge(p.ved.h, ve_tail02)
                    if kc >= 2:
                        p.te.wait_ge(p.sed.h, se_exp[kc - 2])
                    p.te.matmul(sa_[:],
                                kT[0:64, hp * S + kc * 128: hp * S + kc * 128 + 128],
                                qT[0:64, hp * S + qt * 512: hp * S + (qt + 1) * 512],
                                start=True, stop=True, tile_position=(0, 0))
                    p.te.matmul(sb_[:],
                                kT[64:128, hp * S + kc * 128: hp * S + kc * 128 + 128],
                                qT[64:128, hp * S + qt * 512: hp * S + (qt + 1) * 512],
                                start=True, stop=True, tile_position=(64, 0)) \
                        .then_inc(p.ted.h, 1)
                    te_s = p.ted.bump()

                    esl = kc % 2
                    p.se.wait_ge(p.ted.h, te_s)
                    if kc >= 2:
                        p.se.wait_ge(p.ted.h, te_av[kc - 2])
                    p.se.activation(expA[:, esl * 512:(esl + 1) * 512], sa_[:], AF.Exp)
                    p.se.activation(expB[:, esl * 512:(esl + 1) * 512], sb_[:], AF.Exp) \
                        .then_inc(p.sed.h, 1)
                    se_exp[kc] = p.sed.bump()

                    masked = kc >= nkc - 4
                    if masked:
                        mi = kc - (nkc - 4)
                        p.ve.wait_ge(p.sed.h, se_exp[kc])
                        p.ve.tensor_tensor(expA[:, esl * 512:(esl + 1) * 512],
                                           expA[:, esl * 512:(esl + 1) * 512],
                                           masks[:, mi * 512:(mi + 1) * 512], ALU.mult)
                        p.ve.tensor_tensor(expB[:, esl * 512:(esl + 1) * 512],
                                           expB[:, esl * 512:(esl + 1) * 512],
                                           masks[:, mi * 512:(mi + 1) * 512], ALU.mult) \
                            .then_inc(p.ved.h, 1)
                        ve_mask = p.ved.bump()
                        p.te.wait_ge(p.ved.h, ve_mask)
                    else:
                        p.te.wait_ge(p.sed.h, se_exp[kc])
                    if kc == 0 and se_stq_prev is not None:
                        p.te.wait_ge(p.sed.h, se_stq_prev)
                    if kc == 0 and hp == 0 and ve_tail13 is not None:
                        p.te.wait_ge(p.ved.h, ve_tail13)
                    headA, headB = 2 * hp, 2 * hp + 1
                    vA = vsb[:, kc * H * 65 + headA * 65: kc * H * 65 + headA * 65 + 65]
                    vB = vsb[:, kc * H * 65 + headB * 65: kc * H * 65 + headB * 65 + 65]
                    p.te.matmul(pq[4][0:65, :], vA, expA[:, esl * 512:(esl + 1) * 512],
                                start=(kc == 0), stop=(kc == nkc - 1))
                    p.te.matmul(pq[5][0:65, :], vB, expB[:, esl * 512:(esl + 1) * 512],
                                start=(kc == 0), stop=(kc == nkc - 1)) \
                        .then_inc(p.ted.h, 1)
                    te_av[kc] = p.ted.bump()

                te_fin = te_av[nkc - 1]
                se_exp_prev = se_exp[nkc - 1]
                p.se.wait_ge(p.ted.h, te_fin)
                if hp == 0 and qt >= 1:
                    p.se.wait_ge(p.ted.h, te_str)
                cs = hp * 512
                p.se.activation(sTqe[:, cs:cs + 512], pq[4][0:65, :], AF.Identity)
                p.se.activation(sTqo[:, cs:cs + 512], pq[5][0:65, :], AF.Identity) \
                    .then_inc(p.sed.h, 1)
                se_stq_prev = p.sed.bump()
                emit_wq_unit()
                if qt in (1, 2):
                    emit_wq_unit()
            se_stq_all = se_stq_prev

            ve_tail = {}
            for tt in range(4):
                ssl = tt % 2
                b0 = 0 if tt % 2 == 0 else 4
                p.te.wait_ge(p.sed.h, se_stq_all)
                if tt >= 2 and ve_tail.get(tt - 2) is not None:
                    p.te.wait_ge(p.ved.h, ve_tail[tt - 2])
                for hp in range(NP):
                    b = b0 + hp // 4
                    c = (hp % 4) * 65
                    p.te.transpose(
                        pq[b][:, c:c + 65],
                        sTqe[:, hp * 512 + tt * 128: hp * 512 + (tt + 1) * 128],
                        identf[0:65, 0:65])
                    inst = p.te.transpose(
                        pq[b0 + 2 + hp // 4][:, c:c + 65],
                        sTqo[:, hp * 512 + tt * 128: hp * 512 + (tt + 1) * 128],
                        identf[0:65, 0:65])
                inst.then_inc(p.ted.h, 1)
                te_str = p.ted.bump()
                p.ve.wait_ge(p.ted.h, te_str)
                if not (qt == 0 and tt < 2):
                    std_ = [p.std0, p.std1][ssl]
                    p.ve.wait_ge(std_.h, std_base[ssl] + 16 * ((qt * 4 + tt) // 2))
                p.ve.tensor_copy(
                    strided_ap(s0c[ssl], 0, [[130, 4], [1, 65]]),
                    strided_ap(pq[b0 + 0], 0, [[65, 4], [1, 65]]))
                p.ve.tensor_copy(
                    strided_ap(s0c[ssl], 4 * 130, [[130, 4], [1, 65]]),
                    strided_ap(pq[b0 + 1], 0, [[65, 4], [1, 65]]))
                p.ve.tensor_copy(
                    strided_ap(s0c[ssl], 65, [[130, 4], [1, 65]]),
                    strided_ap(pq[b0 + 2], 0, [[65, 4], [1, 65]]))
                p.ve.tensor_copy(
                    strided_ap(s0c[ssl], 4 * 130 + 65, [[130, 4], [1, 65]]),
                    strided_ap(pq[b0 + 3], 0, [[65, 4], [1, 65]])) \
                    .then_inc(p.ved.h, 1)
                ve_tail[tt] = p.ved.bump()
                ve_str_last = ve_tail[tt]
                p.ve.drain()
                p.ve.reciprocal(rsum[:], strided_ap(s0c[ssl], 64, [[65, 16], [1, 1]]))
                p.ve.drain()
                p.ve.tensor_tensor(
                    strided_ap(s0c[ssl], 0, [[64, 16], [1, 64]]),
                    strided_ap(s0c[ssl], 0, [[65, 16], [1, 64]]),
                    AP(rsum[:].tensor, rsum[:].offset,
                       [list(rsum[:].ap[0]), [1, 16], [0, 64]]),
                    ALU.mult).then_inc(p.ved.h, 1)
                ve_tail[tt] = p.ved.bump()
                ve_str_last = ve_tail[tt]
                std_ = [p.std0, p.std1][ssl]
                p.sy.wait_ge(p.ved.h, ve_str_last)
                p.sy.dma_start(
                    out=s0_h.ap()[(qt * 4 + tt) * 128:(qt * 4 + tt + 1) * 128, :],
                    in_=s0c[ssl][:, 0:D]).then_inc(std_.h, 16)
                std_.bump(16)
            ve_tail02 = ve_tail[2]
            ve_tail13 = ve_tail[3]

        if wgu_h is not None:
            assert wq_state["u"] == len(WQ_UNITS), wq_state["u"]
        p.sy.wait_ge(p.std0.h, p.std0.v)
        p.sy.wait_ge(p.std1.h, p.std1.v)

    return narrative


def _emit_quant(p, out_bf, t0, t1, t2, blk, W, boff=0, wait_pod=None,
                gp_wait_ted=None):
    nb = W // 16
    bm, hs = blk[:, boff:boff + nb], blk[:, boff + nb:boff + 2 * nb]
    r3 = lambda ap: ap.rearrange("p (nb w) -> p nb w", w=16)
    b3 = lambda ap: AP(ap.tensor, ap.offset, [list(ap.ap[0]), list(ap.ap[1]), [0, 16]])
    t1 = t1[:][:, 0:W]
    t2 = t2[:][:, 0:W]
    if wait_pod:
        p.ve.wait_ge(p.pod.h, wait_pod)
    p.ve.tensor_reduce(bm.rearrange("p (nb one) -> p nb one", one=1), r3(t0),
                       axis=AX.X, op=ALU.max, apply_absolute_value=True)
    p.ve.drain()
    p.ve.tensor_scalar(bm[:], bm[:], 1.0 / 6.0, 1e-12, ALU.mult, ALU.max)
    p.ve.drain()
    p.ve.tensor_scalar(hs[:], bm[:], 0.5, None, ALU.mult)
    p.ve.reciprocal(bm[:], bm[:])
    p.ve.drain()
    p.ve.tensor_tensor(r3(t1), r3(t0), b3(bm[:]), ALU.mult)
    p.ve.drain()
    p.ve.tensor_scalar(t2.bitcast(I32), t1.bitcast(I32), 0x7F800000, None,
                       ALU.bitwise_and)
    p.ve.drain()
    p.ve.tensor_scalar(t2, t2, 1.0, None, ALU.max)
    p.ve.drain()
    p.ve.tensor_scalar(t2.bitcast(I32), t2.bitcast(I32), 0x7F800000, None,
                       ALU.bitwise_xor)
    p.ve.drain()
    p.ve.tensor_tensor(t1, t1, t2, ALU.mult)
    p.ve.drain()
    p.ve.tensor_scalar(t1, t1, 8388608.0, 8388608.0, ALU.add, ALU.subtract)
    p.ve.tensor_scalar(t2.bitcast(I32), t2.bitcast(I32), 0x7F800000, None,
                       ALU.bitwise_xor).then_inc(p.ved.h, 1)
    ved_head = p.ved.bump()
    p.gp.wait_ge(p.ved.h, ved_head)
    if gp_wait_ted:
        p.gp.wait_ge(p.ted.h, gp_wait_ted)
    p.gp.tensor_tensor(r3(t2), r3(t2), b3(hs[:]), ALU.mult) \
        .then_inc(p.pod.h, 1)
    p.gp.wait_ge(p.pod.h, p.pod.bump())
    p.gp.tensor_tensor(out_bf, t1, t2, ALU.mult).then_inc(p.pod.h, 1)
    pod_tail = p.pod.bump()
    return ved_head, pod_tail


def _emit_quant2(p, out_bfs, t0s, t1s, t2s, blk, boffs, W, wait_pod=None):
    nb = W // 16
    r3 = lambda ap: ap.rearrange("p (nb w) -> p nb w", w=16)
    b3 = lambda ap: AP(ap.tensor, ap.offset, [list(ap.ap[0]), list(ap.ap[1]), [0, 16]])
    if wait_pod:
        p.ve.wait_ge(p.pod.h, wait_pod)
    last = [0, 0]
    chains = []
    for c in range(2):
        bm = blk[:, boffs[c]:boffs[c] + nb]
        hs = blk[:, boffs[c] + nb:boffs[c] + 2 * nb]
        t0, t1, t2 = t0s[c], t1s[c][:, 0:W], t2s[c][:, 0:W]
        out = out_bfs[c]
        chains.append([
            lambda t0=t0, bm=bm: p.ve.tensor_reduce(
                bm.rearrange("p (nb one) -> p nb one", one=1), r3(t0),
                axis=AX.X, op=ALU.max, apply_absolute_value=True),
            lambda bm=bm: p.ve.tensor_scalar(bm[:], bm[:], 1.0 / 6.0, 1e-12,
                                             ALU.mult, ALU.max),
            lambda bm=bm, hs=hs: p.ve.tensor_scalar(hs[:], bm[:], 0.5, None,
                                                    ALU.mult),
            lambda bm=bm: p.ve.reciprocal(bm[:], bm[:]),
            lambda t0=t0, t1=t1, bm=bm: p.ve.tensor_tensor(
                r3(t1), r3(t0), b3(bm[:]), ALU.mult),
            lambda t1=t1, t2=t2: p.ve.tensor_scalar(
                t2.bitcast(I32), t1.bitcast(I32), 0x7F800000, None,
                ALU.bitwise_and),
            lambda t2=t2: p.ve.tensor_scalar(t2, t2, 1.0, None, ALU.max),
            lambda t2=t2: p.ve.tensor_scalar(
                t2.bitcast(I32), t2.bitcast(I32), 0x7F800000, None,
                ALU.bitwise_xor),
            lambda t1=t1, t2=t2: p.ve.tensor_tensor(t1, t1, t2, ALU.mult),
            lambda t1=t1: p.ve.tensor_scalar(t1, t1, 8388608.0, 8388608.0,
                                             ALU.add, ALU.subtract),
            lambda t2=t2: p.ve.tensor_scalar(
                t2.bitcast(I32), t2.bitcast(I32), 0x7F800000, None,
                ALU.bitwise_xor),
            lambda t2=t2, hs=hs: p.ve.tensor_tensor(r3(t2), r3(t2), b3(hs[:]),
                                                    ALU.mult),
            lambda t1=t1, t2=t2, out=out: p.ve.tensor_tensor(out, t1, t2,
                                                             ALU.mult),
        ])
    nst = len(chains[0])
    for s in range(nst):
        for c in range(2):
            if last[c]:
                p.ve.wait_ge(p.ved.h, last[c])
            chains[c][s]().then_inc(p.ved.h, 1)
            last[c] = p.ved.bump()
    return max(last)


def _emit_layernorm(p, xn_out, xtile, stt, eps=1e-5, out2=None):
    p.ve.bn_stats(stt[:, 0:6], xtile[:, 0:512])
    p.ve.bn_stats(stt[:, 6:12], xtile[:, 512:1024])
    p.ve.drain()
    p.ve.bn_aggr(stt[:, 12:14], stt[:, 0:12])
    p.ve.drain()
    p.ve.tensor_scalar(stt[:, 14:15], stt[:, 13:14], eps, None, ALU.add) \
        .then_inc(p.ved.h, 1)
    ve_va = p.ved.bump()
    p.se.wait_ge(p.ved.h, ve_va)
    p.se.activation(stt[:, 15:16], stt[:, 14:15], AF.Sqrt).then_inc(p.sed.h, 1)
    se_sq = p.sed.bump()
    p.ve.wait_ge(p.sed.h, se_sq)
    p.ve.reciprocal(stt[:, 15:16], stt[:, 15:16])
    p.ve.drain()
    p.ve.scalar_tensor_tensor(stt[:, 12:13], stt[:, 12:13], -1.0,
                              stt[:, 15:16], ALU.mult, ALU.mult) \
        .then_inc(p.ved.h, 1)
    ve_rs = p.ved.bump()
    p.se.wait_ge(p.ved.h, ve_rs)
    inst = p.se.activation(xn_out, xtile, AF.Identity,
                           bias=stt[:, 12:13], scale=stt[:, 15:16])
    if out2 is not None:
        inst = p.se.activation(out2, xtile, AF.Identity,
                               bias=stt[:, 12:13], scale=stt[:, 15:16])
    return inst


def make_l0_ffn(nc, dram, pq, pfx="b_"):
    x_h = dram["x"]
    s_h = dram["s0"]
    wgu_h = dram["wgu"]
    wd_h = dram["wd"]
    id_h = dram["ident"]
    x1_h = dram["x1"]
    wqp_h = dram.get("wq_pre")
    nc = _NSWrap(nc, pfx)

    NHC = FFN_I // 128
    SLICES = [(i * 512, min(512, FFN_I - i * 512)) for i in range((FFN_I + 511) // 512)]

    xt = [nc.alloc_sbuf_tensor(f"xt{i}", [128, D], F32) for i in range(2)]
    q0 = nc.alloc_sbuf_tensor("q0", [128, D], F32)
    q1 = nc.alloc_sbuf_tensor("q1", [128, D], F32)
    q2 = nc.alloc_sbuf_tensor("q2", [128, D], F32)
    q3 = nc.alloc_sbuf_tensor("q3", [128, D], F32)
    q4 = nc.alloc_sbuf_tensor("q4", [128, D], F32)
    qbf = nc.alloc_sbuf_tensor("qbf", [128, 2 * D], BF16)
    qbfB = nc.alloc_sbuf_tensor("qbfB", [128, 2 * D], BF16)
    blk = nc.alloc_sbuf_tensor("blk", [128, 544], F32)
    hqT = nc.alloc_sbuf_tensor("hqT", [128, NHC * S], BF16)
    ident = nc.alloc_sbuf_tensor("ident_s", [128, 128], BF16)
    gt = nc.alloc_sbuf_tensor("gt", [128, D], F32)
    st_ = nc.alloc_sbuf_tensor("st_", [128, D], F32)
    x1t = nc.alloc_sbuf_tensor("x1t", [128, D], F32)
    wdr = nc.alloc_sbuf_tensor_at("wdr", [128, FFN_I], F32,
                                  offset=nc.lookup_mloc(q3).addr)
    xqT = nc.alloc_sbuf_tensor("xqT", [128, ND * S], BF16)
    wgTs = nc.alloc_sbuf_tensor("wgTs", [128, ND * 512], BF16)
    wuTs = nc.alloc_sbuf_tensor("wuTs", [128, ND * 512], BF16)
    wrow = [nc.alloc_sbuf_tensor(f"wrow{i}", [128, D], F32) for i in range(2)]
    sg = nc.alloc_sbuf_tensor("sg", [128, 2 * 512], F32)
    hh = nc.alloc_sbuf_tensor("hh", [128, 2 * 512], F32)
    hb2 = [nc.alloc_sbuf_tensor(f"hb{i}", [128, 1024], BF16) for i in range(2)]
    _wa = nc.lookup_mloc(xqT).addr
    wdT = nc.alloc_sbuf_tensor_at("wdT", [128, NHC * D], BF16, offset=_wa)



    def narrative(p):
        p.gp.dma_start(out=ident[:], in_=id_h.ap()).then_inc(p.gpd.h, 16)
        gp_all = p.gpd.bump(16)
        sx = [p.sx0, p.sx1]
        sw = [p.sw0, p.sw1]
        stt = blk[:, 528:544]

        ve_q = []
        quv = []
        se_xn_prev = {}
        te_tr = []
        ve_evac = []
        for i in range(NT):
            sl = i % 2
            if i >= 2:
                p.sy.wait_ge(p.sed.h, se_xn_prev[i - 2])
            p.sy.dma_start(out=xt[sl][:], in_=x_h.ap()[i * 128:(i + 1) * 128, :]) \
                .then_inc(sx[sl].h, 16)
            xv = sx[sl].bump(16)

            p.ve.wait_ge(sx[sl].h, xv)
            _emit_layernorm(p, q0[:], xt[sl][:], stt).then_inc(p.sed.h, 1)
            se_xn = p.sed.bump()
            se_xn_prev[i] = se_xn
            p.ve.wait_ge(p.sed.h, se_xn)
            t12 = (q1, q2) if i % 2 == 0 else (q3, q4)
            vh, pt = _emit_quant(
                p, qbf[:, (i % 2) * D:(i % 2) * D + D], q0[:], t12[0], t12[1],
                blk, D, boff=(i % 2) * 256,
                wait_pod=(quv[i - 2][1] if i >= 2 else None),
                gp_wait_ted=(te_tr[i - 2] if i >= 2 else None))
            quv.append((vh, pt))
            ve_q.append(vh)

            bank = pq[4 + i % 2]
            pb_bf = bank[:].bitcast(BF16)
            p.te.wait_ge(p.pod.h, pt)
            if i == 0:
                p.te.wait_ge(p.gpd.h, gp_all)
            if i >= 2:
                p.te.wait_ge(p.sed.h, ve_evac[i - 2])
            for j in range(ND):
                inst = p.te.transpose(
                    pb_bf[:, j * 128:(j + 1) * 128],
                    qbf[:, (i % 2) * D + j * 128: (i % 2) * D + (j + 1) * 128],
                    ident[:])
            inst.then_inc(p.ted.h, 1)
            te_tr.append(p.ted.bump())
            p.se.wait_ge(p.ted.h, te_tr[i])
            p.se.activation(
                strided_ap(xqT, i * 128, [[S, ND], [1, 128]]),
                pb_bf[:].rearrange("p (j c) -> p j c", j=ND),
                AF.Identity, scale=1.0).then_inc(p.sed.h, 1)
            ve_evac.append(p.sed.bump())
        se_xqT = ve_evac[-1]

        wk = 0
        ve_wq = {}
        se_wtr_evac = {}
        te_wtr_prev = None
        te_wtr_hist = {}
        ve_hprev = {0: None, 1: None}
        se_htr_evac = None
        hq_pt = {}
        hq_te = {}
        hq_k = 0
        hq_pend = None
        te_htr = None
        slice_idx = -1

        def emit_htr(pend):
            nonlocal se_htr_evac, te_htr
            k_, pt_, tc0_, ntr_, hc0_ = pend
            bank = pq[6 + k_ % 2]
            pb_bf = bank[:].bitcast(BF16)
            hbs_ = hb2[k_ % 2]
            p.te.wait_ge(p.pod.h, pt_)
            if se_htr_evac is not None:
                p.te.wait_ge(p.sed.h, se_htr_evac)
            for c in range(2):
                for j in range(ntr_):
                    inst = p.te.transpose(
                        pb_bf[:, c * 512 + j * 128: c * 512 + (j + 1) * 128],
                        hbs_[:, c * 512 + j * 128: c * 512 + (j + 1) * 128],
                        ident[:])
            inst.then_inc(p.ted.h, 1)
            te_htr = p.ted.bump()
            hq_te[k_] = te_htr
            p.se.wait_ge(p.ted.h, te_htr)
            p.se.activation(
                strided_ap(hqT, hc0_ * S + tc0_ * 128, [[S, ntr_], [1, 128]]),
                AP(pb_bf.tensor, pb_bf.offset,
                   [list(pb_bf.ap[0]), [128, ntr_], [1, 128]]),
                AF.Identity, scale=1.0)
            p.se.activation(
                strided_ap(hqT, hc0_ * S + (tc0_ + 1) * 128, [[S, ntr_], [1, 128]]),
                AP(pb_bf.tensor, pb_bf.offset + 512,
                   [list(pb_bf.ap[0]), [128, ntr_], [1, 128]]),
                AF.Identity, scale=1.0).then_inc(p.sed.h, 1)
            se_htr_evac = p.sed.bump()
        pu = 0
        for (hs0, hw) in SLICES:
            slice_idx += 1
            pre = wqp_h is not None
            nrc = hw // 128
            for gu in range(2):
                for rp in range(nrc // 2):
                    row0 = gu * FFN_I + hs0 + rp * 256
                    if pre:
                        u0 = pu
                        pu += 2
                        pj = wk // 2
                        qb_ = [qbf, qbfB][pj % 2]
                        if pj == 0:
                            p.sy.wait_ge(p.gw1.h, p.gw1.v)
                            p.sy.wait_ge(p.gw2.h, p.gw2.v)
                            p.sy.wait_ge(p.ted.h, te_tr[-1])
                        elif pj >= 2:
                            p.sy.wait_ge(p.ted.h, te_wtr_hist[pj - 2])
                        sp_ = [p.sw2, p.sw3][pj % 2]
                        p.sy.dma_start(
                            out=qb_[:].rearrange("p (c d) -> p c d", c=2),
                            in_=wqp_h.ap()[u0 * 128:(u0 + 2) * 128, :]
                            .rearrange("(c p) d -> p c d", p=128)) \
                            .then_inc(sp_.h, 16)
                        wv = sp_.bump(16)
                        wk += 2
                        p.te.wait_ge(sp_.h, wv)
                    else:
                        qb_ = qbf
                        for c in range(2):
                            if wk >= 2 and ve_wq.get(wk - 2):
                                p.sy.wait_ge(p.ved.h, ve_wq[wk - 2])
                            p.sy.dma_start(out=wrow[c][:],
                                           in_=wgu_h.ap()[row0 + c * 128:
                                                          row0 + (c + 1) * 128, :]) \
                                .then_inc(sw[c].h, 16)
                            wv = sw[c].bump(16)
                            p.ve.wait_ge(sw[c].h, wv)
                            wk += 1
                        if te_wtr_prev is not None:
                            p.ve.wait_ge(p.ted.h, te_wtr_prev)
                        else:
                            p.ve.wait_ge(p.ted.h, te_tr[-1])
                            p.ve.wait_ge(p.pod.h, p.pod.v)
                        vq = _emit_quant2(
                            p, [qbf[:, 0:D], qbf[:, D:2 * D]],
                            [wrow[0][:], wrow[1][:]], [q1, q3], [q2, q4],
                            blk, [0, 256], D,
                            wait_pod=(p.pod.v if p.pod.v else None))
                        ve_wq[wk - 2] = vq
                        ve_wq[wk - 1] = vq
                        p.te.wait_ge(p.ved.h, vq)

                    if wk >= 4:
                        p.te.wait_ge(p.sed.h, se_wtr_evac[wk - 4])
                    elif wk == 2:
                        p.te.wait_ge(p.sed.h, se_xqT)
                    for c in range(2):
                        pb_bf = pq[4 + c][:].bitcast(BF16)
                        for j in range(ND):
                            inst = p.te.transpose(
                                pb_bf[:, j * 128:(j + 1) * 128],
                                qb_[:, c * D + j * 128: c * D + (j + 1) * 128],
                                ident[:])
                    inst.then_inc(p.ted.h, 1)
                    te_wtr_prev = p.ted.bump()
                    te_wtr_hist[wk // 2 - 1] = te_wtr_prev
                    dstT = wgTs if gu == 0 else wuTs
                    p.se.wait_ge(p.ted.h, te_wtr_prev)
                    p.se.activation(
                        strided_ap(dstT, rp * 256, [[512, ND], [1, 128]]),
                        pq[4][:].bitcast(BF16).rearrange("p (j c) -> p j c", j=ND),
                        AF.Identity, scale=1.0)
                    p.se.activation(
                        strided_ap(dstT, rp * 256 + 128, [[512, ND], [1, 128]]),
                        pq[5][:].bitcast(BF16).rearrange("p (j c) -> p j c", j=ND),
                        AF.Identity, scale=1.0).then_inc(p.sed.h, 1)
                    se_wtr_evac[wk - 2] = p.sed.bump()
            ve_wslice = se_wtr_evac[wk - 2]

            hc0 = hs0 // 128
            ntr = hw // 128
            for tc in range(NT):
                gb = pq[(tc % 2) * 2]
                ub = pq[(tc % 2) * 2 + 1]
                p.te.wait_ge(p.sed.h, ve_wslice)
                if tc == 0 and hs0 == 0:
                    p.te.wait_ge(p.sed.h, se_xqT)
                if ve_hprev[tc % 2] is not None:
                    p.te.wait_ge(p.ved.h, ve_hprev[tc % 2])
                for j in range(ND):
                    p.te.matmul(gb[:, 0:hw],
                                xqT[:, j * S + tc * 128: j * S + (tc + 1) * 128],
                                wgTs[:, j * 512: j * 512 + hw],
                                start=(j == 0), stop=(j == ND - 1))
                for j in range(ND):
                    inst = p.te.matmul(ub[:, 0:hw],
                                       xqT[:, j * S + tc * 128: j * S + (tc + 1) * 128],
                                       wuTs[:, j * 512: j * 512 + hw],
                                       start=(j == 0), stop=(j == ND - 1))
                inst.then_inc(p.ted.h, 1)
                te_gu = p.ted.bump()

                ssl = tc % 2
                p.se.wait_ge(p.ted.h, te_gu)
                p.se.activation(sg[:, ssl * 512: ssl * 512 + hw], gb[:, 0:hw],
                                AF.Silu).then_inc(p.sed.h, 1)
                se_sg = p.sed.bump()
                p.ve.wait_ge(p.sed.h, se_sg)
                p.ve.tensor_tensor(hh[:, ssl * 512: ssl * 512 + hw],
                                   sg[:, ssl * 512: ssl * 512 + hw], ub[:, 0:hw],
                                   ALU.mult).then_inc(p.ved.h, 1)
                ve_hprev[tc % 2] = p.ved.bump()
                if tc % 2 == 0:
                    continue
                p.ve.drain()
                par = hq_k % 2
                t12 = (q1, q2) if par == 0 else (q3, q4)
                wp = hq_pt.get(hq_k - 2)
                gwt = hq_te.get(hq_k - 2)
                hbs = hb2[par]
                if hw == 512:
                    vh, pt = _emit_quant(p, hbs[:, 0:1024], hh[:][:, 0:1024],
                                         t12[0], t12[1], blk, 1024,
                                         boff=par * 256, wait_pod=wp,
                                         gp_wait_ted=gwt)
                else:
                    wpa = hq_pt.get(hq_k - 1)
                    _emit_quant(p, hbs[:, 0:hw], hh[:][:, 0:hw],
                                q1, q2, blk, hw, boff=0,
                                wait_pod=(wpa - 2 if wpa else wp),
                                gp_wait_ted=gwt)
                    vh, pt = _emit_quant(p, hbs[:, 512:512 + hw],
                                         hh[:][:, 512:512 + hw],
                                         q3, q4, blk, hw, boff=256,
                                         wait_pod=wpa)
                hq_pt[hq_k] = pt
                pend_prev = hq_pend
                hq_pend = (hq_k, pt, tc - 1, ntr, hc0)
                hq_k += 1
                if pend_prev is not None:
                    emit_htr(pend_prev)
        if hq_pend is not None:
            emit_htr(hq_pend)
            hq_pend = None
        ve_hqT = se_htr_evac

        PIECES = [(0, 1024), (1024, 1024), (2048, 768)]
        se_wdtr_evac = {}
        wd_pt = {}
        te_wdtr = None
        pi_k = 0
        for rc in range(ND):
            if rc == 0:
                p.sy.wait_ge(sw[0].h, sw[0].v)
                p.sy.wait_ge(sw[1].h, sw[1].v)
                p.sy.wait_ge(p.ved.h, p.ved.v)
                p.sy.wait_ge(p.ted.h, p.ted.v)
                p.sy.wait_ge(p.pod.h, p.pod.v)
            if rc >= 1:
                p.sy.wait_ge(p.ved.h, ve_wdq)
            p.sy.dma_start(out=wdr[:], in_=wd_h.ap()[rc * 128:(rc + 1) * 128, :]) \
                .then_inc(sw[0].h, 16)
            wv = sw[0].bump(16)
            p.ve.wait_ge(sw[0].h, wv)
            if rc == 0:
                p.ve.wait_ge(p.pod.h, p.pod.v)
            for (c0, cw) in PIECES:
                vh, pt = _emit_quant(
                    p, hb2[0][:, 0:cw], wdr[:][:, c0:c0 + cw], q1, q2,
                    blk, cw, boff=0,
                    wait_pod=wd_pt.get(pi_k - 1),
                    gp_wait_ted=(te_wdtr if pi_k >= 1 else te_htr))
                wd_pt[pi_k] = pt
                ve_wdq = vh

                bank = pq[4 + pi_k % 2]
                pb_bf = bank[:].bitcast(BF16)
                p.te.wait_ge(p.pod.h, pt)
                if pi_k >= 2:
                    p.te.wait_ge(p.sed.h, se_wdtr_evac[pi_k - 2])
                npc = cw // 128
                for j in range(npc):
                    inst = p.te.transpose(pb_bf[:, j * 128:(j + 1) * 128],
                                          hb2[0][:, j * 128:(j + 1) * 128],
                                          ident[:])
                inst.then_inc(p.ted.h, 1)
                te_wdtr = p.ted.bump()
                p.se.wait_ge(p.ted.h, te_wdtr)
                p.se.activation(
                    strided_ap(wdT, (c0 // 128) * D + rc * 128, [[D, npc], [1, 128]]),
                    AP(pb_bf.tensor, pb_bf.offset,
                       [list(pb_bf.ap[0]), [128, npc], [1, 128]]),
                    AF.Identity, scale=1.0).then_inc(p.sed.h, 1)
                se_wdtr_evac[pi_k] = p.sed.bump()
                pi_k += 1
        ve_wdT = se_wdtr_evac[pi_k - 1]

        ve_upd = {}
        std_base = [p.std0.v, p.std1.v]
        for tc in range(NT):
            b0 = pq[(tc % 2) * 2]
            b1 = pq[(tc % 2) * 2 + 1]
            if tc == 0:
                p.te.wait_ge(p.sed.h, ve_wdT)
                p.te.wait_ge(p.sed.h, ve_hqT)
            if tc >= 2:
                p.te.wait_ge(p.ved.h, ve_upd[tc - 2])
            for j in range(NHC):
                p.te.matmul(b0[:], hqT[:, j * S + tc * 128: j * S + (tc + 1) * 128],
                            wdT[:, j * D: j * D + 512],
                            start=(j == 0), stop=(j == NHC - 1))
            for j in range(NHC):
                inst = p.te.matmul(b1[:],
                                   hqT[:, j * S + tc * 128: j * S + (tc + 1) * 128],
                                   wdT[:, j * D + 512: j * D + 1024],
                                   start=(j == 0), stop=(j == NHC - 1))
            inst.then_inc(p.ted.h, 1)
            te_dn = p.ted.bump()

            sl = tc % 2
            if tc < 2:
                p.sy.wait_ge(sx[sl].h, sx[sl].v)
            if tc >= 2:
                p.sy.wait_ge(p.ved.h, ve_upd[tc - 2])
            p.sy.dma_start(out=xt[sl][:], in_=x_h.ap()[tc * 128:(tc + 1) * 128, :]) \
                .then_inc(sx[sl].h, 16)
            sx[sl].bump(16)
            if tc >= 1:
                p.sy.wait_ge(p.ved.h, ve_upd[tc - 1])
            p.sy.dma_start(out=st_[:], in_=s_h.ap()[tc * 128:(tc + 1) * 128, :]) \
                .then_inc(sx[sl].h, 16)
            sv = sx[sl].bump(16)
            p.se.wait_ge(p.ted.h, te_dn)
            if tc >= 1:
                p.se.wait_ge(p.ved.h, ve_upd[tc - 1])
            p.se.activation(gt[:, 0:512], b0[:], AF.Silu)
            p.se.activation(gt[:, 512:1024], b1[:], AF.Silu).then_inc(p.sed.h, 1)
            se_g = p.sed.bump()
            p.ve.wait_ge(p.sed.h, se_g)
            p.ve.wait_ge(sx[sl].h, sv)
            if tc >= 1:
                std_p = [p.std0, p.std1][(tc - 1) % 2]
                p.ve.wait_ge(std_p.h, std_p.v)
            p.ve.tensor_tensor(gt[:], st_[:], gt[:], ALU.mult)
            p.ve.drain()
            p.ve.tensor_tensor(x1t[:], xt[sl][:], gt[:], ALU.add) \
                .then_inc(p.ved.h, 1)
            ve_upd[tc] = p.ved.bump()
            std_ = [p.std0, p.std1][tc % 2]
            p.sy.wait_ge(p.ved.h, ve_upd[tc])
            p.sy.dma_start(out=x1_h.ap()[tc * 128:(tc + 1) * 128, :], in_=x1t[:]) \
                .then_inc(std_.h, 16)
            std_.bump(16)

        p.sy.wait_ge(p.std0.h, p.std0.v)
        p.sy.wait_ge(p.std1.h, p.std1.v)

    return narrative


def make_l1_attn(nc, dram, pq, pfx="c_"):
    x_h = dram["x1"]
    w_h = dram["wabcdT"]
    ab_h = dram["ab_act"]
    ow_h = dram["outwT"]
    ob_h = dram["outb_row"]
    nw_h = dram["normw_row"]
    id_h = dram["ident"]
    s1_h = dram["s1"]
    xn1_h = dram["xn1"]
    nc = _NSWrap(nc, pfx)

    xt = [nc.alloc_sbuf_tensor(f"xt{i}", [128, D], F32) for i in range(2)]
    xn1t = [nc.alloc_sbuf_tensor(f"xn1t{i}", [128, D], F32) for i in range(2)]
    xn2b = [nc.alloc_sbuf_tensor(f"xn2b{i}", [128, D], BF16) for i in range(2)]
    tmp = nc.alloc_sbuf_tensor("tmp", [128, D], F32)
    xn2T = nc.alloc_sbuf_tensor("xn2T", [128, ND * S], BF16)
    hT = nc.alloc_sbuf_tensor("hT", [128, ND * S], BF16)
    wsl = [nc.alloc_sbuf_tensor(f"wsl{i}", [128, 4 * ND * 128], BF16) for i in range(2)]
    owT = nc.alloc_sbuf_tensor("owT", [128, ND * D], BF16)
    ab_sb = nc.alloc_sbuf_tensor("ab_sb", [128, 8], F32)
    nwb = nc.alloc_sbuf_tensor("nwb", [128, D], F32)
    obb = nc.alloc_sbuf_tensor("obb", [128, D], F32)
    nwr = nc.alloc_sbuf_tensor("nwr", [1, D], BF16)
    obr = nc.alloc_sbuf_tensor("obr", [1, D], BF16)
    onesb = nc.alloc_sbuf_tensor("onesb", [1, 128], BF16)
    ident = nc.alloc_sbuf_tensor("ident_s", [128, 128], BF16)
    stats = nc.alloc_sbuf_tensor("stats", [128, 32], F32)
    ga = nc.alloc_sbuf_tensor("ga", [128, S], F32)
    gb_ = nc.alloc_sbuf_tensor("gb_", [128, S], F32)
    gc = nc.alloc_sbuf_tensor("gc", [128, S], F32)
    gd = nc.alloc_sbuf_tensor("gd", [128, S], F32)
    hs_ = nc.alloc_sbuf_tensor("hs_", [128, S], F32)
    zcol = nc.alloc_sbuf_tensor("zcol", [128, 1], F32)
    s1t = [nc.alloc_sbuf_tensor(f"s1t{i}", [128, D], F32) for i in range(2)]



    def narrative(p):
        p.gp.dma_start(out=ident[:], in_=id_h.ap()).then_inc(p.gpd.h, 16)
        p.gpd.bump(16)
        p.gp.dma_start(out=ab_sb[:], in_=ab_h.ap()).then_inc(p.gpd.h, 16)
        p.gpd.bump(16)
        p.gp.dma_start(out=nwr[:], in_=nw_h.ap()).then_inc(p.gpd.h, 16)
        p.gpd.bump(16)
        p.gp.dma_start(out=obr[:], in_=ob_h.ap()).then_inc(p.gpd.h, 16)
        p.gpd.bump(16)
        for j in range(ND):
            p.gp.dma_start(out=owT[:, j * D:(j + 1) * D],
                           in_=ow_h.ap()[j * 128:(j + 1) * 128, :]) \
                .then_inc(p.gpd.h, 16)
            p.gpd.bump(16)
        gp_all = p.gpd.v

        p.ve.wait_ge(p.gpd.h, gp_all)
        p.ve.memset(onesb[:], 1.0)
        p.ve.memset(zcol[:], 0.0).then_inc(p.ved.h, 1)
        ve_init = p.ved.bump()

        se_bc = []
        for (dst, row) in ((nwb, nwr), (obb, obr)):
            for vh in range(2):
                bank = pq[6 + vh]
                p.te.wait_ge(p.ved.h, ve_init)
                if len(se_bc) >= 2:
                    p.te.wait_ge(p.sed.h, se_bc[-2])
                p.te.matmul(bank[:], onesb[:], row[:, vh * 512:(vh + 1) * 512],
                            start=True, stop=True).then_inc(p.ted.h, 1)
                te_b = p.ted.bump()
                p.se.wait_ge(p.ted.h, te_b)
                p.se.activation(dst[:, vh * 512:(vh + 1) * 512], bank[:],
                                AF.Identity, scale=1.0).then_inc(p.sed.h, 1)
                se_bc.append(p.sed.bump())
        se_bcast = se_bc[-1]

        sx = [p.sx0, p.sx1]
        stx = [p.stx0, p.stx1]
        stx_base = [p.stx0.v, p.stx1.v]
        se_xn = []
        ve_x2 = []
        te_tr = []
        ve_evac = []
        for i in range(NT):
            sl = i % 2
            if i >= 2:
                p.sy.wait_ge(p.ved.h, ve_x2[i - 2])
                p.sy.wait_ge(stx[sl].h, stx_base[sl] + 16 * (i // 2))
            p.sy.dma_start(out=xt[sl][:], in_=x_h.ap()[i * 128:(i + 1) * 128, :]) \
                .then_inc(sx[sl].h, 16)
            xv = sx[sl].bump(16)
            p.ve.wait_ge(sx[sl].h, xv)
            if i >= 2:
                p.ve.wait_ge(p.sed.h, se_xn[i - 2])
            stt = stats[:, 16 * sl:16 * sl + 16]
            _emit_layernorm(p, xn1t[sl][:], xt[sl][:], stt).then_inc(p.sed.h, 1)
            se_xn.append(p.sed.bump())
            p.sy.wait_ge(p.sed.h, se_xn[i])
            p.sy.dma_start(out=xn1_h.ap()[i * 128:(i + 1) * 128, :],
                           in_=xn1t[sl][:]).then_inc(stx[sl].h, 16)
            stx[sl].bump(16)
            p.se.wait_ge(p.sed.h, se_xn[i])
            p.se.activation(tmp[:], xn1t[sl][:], AF.Square,
                            accum_out=stt[:, 0:1]).then_inc(p.sed.h, 1)
            se_ssq = p.sed.bump()
            p.ve.wait_ge(p.sed.h, se_ssq)
            p.ve.tensor_scalar(stt[:, 1:2], stt[:, 0:1], 1.0 / D, 1e-5,
                               ALU.mult, ALU.add).then_inc(p.ved.h, 1)
            ve_ss = p.ved.bump()
            p.se.wait_ge(p.ved.h, ve_ss)
            p.se.activation(stt[:, 2:3], stt[:, 1:2], AF.Sqrt).then_inc(p.sed.h, 1)
            se_rm = p.sed.bump()
            p.ve.wait_ge(p.sed.h, se_rm)
            p.ve.reciprocal(stt[:, 2:3], stt[:, 2:3])
            p.ve.drain()
            if se_bcast is not None and i == 0:
                p.ve.wait_ge(p.sed.h, se_bcast)
            if i >= 2:
                p.ve.wait_ge(p.ted.h, te_tr[i - 2])
            p.ve.tensor_tensor(tmp[:], xn1t[sl][:], nwb[:], ALU.mult) \
                .then_inc(p.ved.h, 1)
            ve_t = p.ved.bump()
            p.se.wait_ge(p.ved.h, ve_t)
            p.se.activation(xn2b[sl][:], tmp[:], AF.Identity,
                            scale=stt[:, 2:3]).then_inc(p.sed.h, 1)
            se_x2 = p.sed.bump()
            p.ve.drain()
            p.ve.nop()
            ve_x2.append(p.ved.v)
            bank = pq[4 + i % 2]
            pb_bf = bank[:].bitcast(BF16)
            p.te.wait_ge(p.sed.h, se_x2)
            if i >= 2:
                p.te.wait_ge(p.ved.h, ve_evac[i - 2])
            for j in range(ND):
                inst = p.te.transpose(pb_bf[:, j * 128:(j + 1) * 128],
                                      xn2b[sl][:, j * 128:(j + 1) * 128], ident[:])
            inst.then_inc(p.ted.h, 1)
            te_tr.append(p.ted.bump())
            p.ve.wait_ge(p.ted.h, te_tr[i])
            p.ve.tensor_copy(
                strided_ap(xn2T, i * 128, [[S, ND], [1, 128]]),
                pb_bf[:].rearrange("p (j c) -> p j c", j=ND)).then_inc(p.ved.h, 1)
            ve_evac.append(p.ved.bump())
            ve_x2[i] = ve_evac[i]
        ve_xn2T = ve_evac[-1]

        sw = [p.gw0, p.gw1]
        ve_wl_prev = None
        ve_hT_evac = None
        se_gprev = None
        ve_dbprev = None
        for nk in range(ND):
            sl = nk % 2
            if nk >= 2:
                p.gp.wait_ge(p.ted.h, te_wused[nk - 2])
            src4 = w_h.ap().rearrange("(j pp) (w n) -> pp w j n", pp=128, w=4) \
                [:, :, :, nk * 128:(nk + 1) * 128]
            p.gp.dma_start(
                out=wsl[sl][:].rearrange("p (w j n) -> p w j n", w=4, j=ND),
                in_=src4).then_inc(sw[sl].h, 16)
            wv = sw[sl].bump(16)

            p.te.wait_ge(sw[sl].h, wv)
            if nk == 0:
                p.te.wait_ge(p.ved.h, ve_xn2T)
            gbuf = {0: ga, 1: gb_, 2: gc, 3: gd}
            for w in range(4):
                for ch in range(4):
                    bank = pq[(w * 4 + ch) % 2]
                    if not (nk == 0 and w == 0 and ch < 2):
                        p.te.wait_ge(p.sed.h, se_ev[(w * 4 + ch - 2) % 100]
                                     if False else se_gprev2)
                    for j in range(ND):
                        inst = p.te.matmul(
                            bank[:],
                            wsl[sl][:, (w * ND + j) * 128:(w * ND + j + 1) * 128],
                            xn2T[:, j * S + ch * 512: j * S + (ch + 1) * 512],
                            start=(j == 0), stop=(j == ND - 1))
                    inst.then_inc(p.ted.h, 1)
                    te_mm = p.ted.bump()
                    p.se.wait_ge(p.ted.h, te_mm)
                    if w == 0:
                        p.se.wait_ge(p.ved.h, ve_dbprev) if ve_dbprev else None
                        p.se.activation(ga[:, ch * 512:(ch + 1) * 512], bank[:],
                                        AF.Sigmoid, bias=ab_sb[:, nk:nk + 1]) \
                            .then_inc(p.sed.h, 1)
                    elif w == 1:
                        p.se.activation(gb_[:, ch * 512:(ch + 1) * 512], bank[:],
                                        AF.Identity, scale=1.0).then_inc(p.sed.h, 1)
                    elif w == 2:
                        p.se.activation(gc[:, ch * 512:(ch + 1) * 512], bank[:],
                                        AF.Sigmoid).then_inc(p.sed.h, 1)
                    else:
                        p.se.activation(gd[:, ch * 512:(ch + 1) * 512], bank[:],
                                        AF.Sigmoid).then_inc(p.sed.h, 1)
                    se_gprev2 = p.sed.bump()
            te_wused = {} if nk == 0 else te_wused
            te_wused[nk] = p.ted.v
            se_abcd = se_gprev2

            p.ve.wait_ge(p.sed.h, se_abcd)
            p.ve.tensor_tensor(gb_[:], gd[:], gb_[:], ALU.mult)
            p.ve.drain()
            p.ve.tensor_tensor_scan(hs_[:], ga[:], gb_[:], zcol[:],
                                    ALU.mult, ALU.add)
            p.ve.drain()
            p.ve.tensor_tensor(hT[:, nk * S:(nk + 1) * S], gc[:], hs_[:],
                               ALU.mult).then_inc(p.ved.h, 1)
            ve_db = p.ved.bump()
            ve_dbprev = ve_db
        ve_hT = ve_dbprev

        std_base2 = [p.std0.v, p.std1.v]
        for tc in range(NT):
            sl = tc % 2
            b0 = pq[2 + (tc % 2) * 2]
            b1 = pq[2 + (tc % 2) * 2 + 1]
            if tc == 0:
                p.te.wait_ge(p.ved.h, ve_hT)
            if tc >= 2:
                p.te.wait_ge(p.ved.h, ve_s1[tc - 2])
            for half in range(2):
                bk = b0 if half == 0 else b1
                for j in range(ND):
                    inst = p.te.matmul(
                        bk[:], hT[:, j * S + tc * 128: j * S + (tc + 1) * 128],
                        owT[:, j * D + half * 512: j * D + (half + 1) * 512],
                        start=(j == 0), stop=(j == ND - 1))
            inst.then_inc(p.ted.h, 1)
            te_y = p.ted.bump()
            if tc < 2:
                p.sy.wait_ge(sx[sl].h, sx[sl].v)
                p.sy.wait_ge(p.stx0.h, p.stx0.v)
                p.sy.wait_ge(p.stx1.h, p.stx1.v)
            if tc >= 2:
                p.sy.wait_ge(p.ved.h, ve_s1[tc - 2])
            p.sy.dma_start(out=xt[sl][:], in_=xn1_h.ap()[tc * 128:(tc + 1) * 128, :]) \
                .then_inc(sx[sl].h, 16)
            xv = sx[sl].bump(16)
            p.ve.wait_ge(p.ted.h, te_y)
            p.ve.wait_ge(sx[sl].h, xv)
            if tc >= 2:
                gsl = tc % 2
                std_g = [p.std0, p.std1][gsl]
                p.ve.wait_ge(std_g.h, std_base2[gsl] + 16 * (tc // 2))
            p.ve.tensor_tensor(s1t[sl][:, 0:512], b0[:], obb[:, 0:512], ALU.add)
            p.ve.tensor_tensor(s1t[sl][:, 512:1024], b1[:], obb[:, 512:1024], ALU.add)
            p.ve.drain()
            p.ve.tensor_tensor(s1t[sl][:], s1t[sl][:], xt[sl][:], ALU.add) \
                .then_inc(p.ved.h, 1)
            if tc == 0:
                ve_s1 = {}
            ve_s1[tc] = p.ved.bump()
            std_ = [p.std0, p.std1][tc % 2]
            p.sy.wait_ge(p.ved.h, ve_s1[tc])
            p.sy.dma_start(out=s1_h.ap()[tc * 128:(tc + 1) * 128, :], in_=s1t[sl][:]) \
                .then_inc(std_.h, 16)
            std_.bump(16)

        p.sy.wait_ge(p.std0.h, p.std0.v)
        p.sy.wait_ge(p.std1.h, p.std1.v)
        p.sy.wait_ge(p.stx0.h, p.stx0.v)
        p.sy.wait_ge(p.stx1.h, p.stx1.v)

    return narrative


def make_l1_ffn(nc, dram, pq, pfx="d_"):
    x_h = dram["x1"]
    xn_h = dram["xn1"]
    s_h = dram["s1"]
    nw_h = dram["normw2_row"]
    gw_h = dram["gwT"]
    w1_h = dram["w1T"]
    w2_h = dram["w2T"]
    b1_h = dram["b1_act"]
    b2_h = dram["b2_row"]
    id_h = dram["ident"]
    o_h = dram["out"]
    nc = _NSWrap(nc, pfx)

    NHC2 = MOE_H // 128

    xt = [nc.alloc_sbuf_tensor(f"xt{i}", [128, D], F32) for i in range(2)]
    tmp = nc.alloc_sbuf_tensor("tmp", [128, D], F32)
    tb = [nc.alloc_sbuf_tensor(f"tb{i}", [128, D], BF16) for i in range(2)]
    tT = nc.alloc_sbuf_tensor("tT", [128, ND * S], BF16)
    acc = nc.alloc_sbuf_tensor("acc", [128, NT * D], F32)
    hTh = nc.alloc_sbuf_tensor("hTh", [128, NHC2 * 512], BF16)
    w1e = nc.alloc_sbuf_tensor("w1e", [128, ND * MOE_H], BF16)
    w2e = nc.alloc_sbuf_tensor("w2e", [128, NHC2 * D], BF16)
    gwTs = nc.alloc_sbuf_tensor("gwTs", [128, ND * NE], BF16)
    b1s = nc.alloc_sbuf_tensor("b1s", [128, 128], F32)
    b2r = nc.alloc_sbuf_tensor("b2r", [1, D], BF16)
    b2b = nc.alloc_sbuf_tensor("b2b", [128, D], F32)
    nwb = nc.alloc_sbuf_tensor("nwb", [128, D], F32)
    nwr = nc.alloc_sbuf_tensor("nwr", [1, D], BF16)
    onesb = nc.alloc_sbuf_tensor("onesb", [1, 128], BF16)
    ident = nc.alloc_sbuf_tensor("ident_s", [128, 128], BF16)
    stats = nc.alloc_sbuf_tensor("stats", [128, 32], F32)
    gl = nc.alloc_sbuf_tensor("gl", [128, NT * NE], F32)
    comb = nc.alloc_sbuf_tensor("comb", [128, NT * NE], F32)
    gsc = nc.alloc_sbuf_tensor("gsc", [128, 16], F32)
    _ha = nc.lookup_mloc(hTh).addr
    s1b = [nc.alloc_sbuf_tensor_at(f"s1t{i}", [128, D], F32,
                                   offset=_ha + i * 8192) for i in range(2)]
    otb = [nc.alloc_sbuf_tensor_at(f"ot{i}", [128, D], F32,
                                   offset=_ha + 4096 + i * 8192) for i in range(2)]



    def narrative(p):
        p.gp.dma_start(out=ident[:], in_=id_h.ap()).then_inc(p.gpd.h, 16)
        p.gpd.bump(16)
        p.gp.dma_start(out=nwr[:], in_=nw_h.ap()).then_inc(p.gpd.h, 16)
        p.gpd.bump(16)

        p.gp.dma_start(out=b1s[:], in_=b1_h.ap()).then_inc(p.gpd.h, 16)
        p.gpd.bump(16)
        p.gp.dma_start(out=gwTs[:].rearrange("p (j e) -> p j e", j=ND),
                       in_=gw_h.ap().rearrange("(j p) e -> p j e", p=128)) \
            .then_inc(p.gpd.h, 16)
        p.gpd.bump(16)
        gp_all = p.gpd.v
        p.sy.dma_start(out=acc[:].rearrange("p (tc d) -> p tc d", tc=NT),
                       in_=xn_h.ap().rearrange("(tc p) d -> p tc d", p=128)) \
            .then_inc(p.sac.h, 16)
        p.sac.bump(16)

        p.ve.wait_ge(p.gpd.h, gp_all)
        p.ve.memset(onesb[:], 1.0).then_inc(p.ved.h, 1)
        ve_init = p.ved.bump()

        se_bc = []
        for vh in range(2):
            bank = pq[6 + vh]
            p.te.wait_ge(p.ved.h, ve_init)
            p.te.matmul(bank[:], onesb[:], nwr[:, vh * 512:(vh + 1) * 512],
                        start=True, stop=True).then_inc(p.ted.h, 1)
            te_b = p.ted.bump()
            p.se.wait_ge(p.ted.h, te_b)
            p.se.activation(nwb[:, vh * 512:(vh + 1) * 512], bank[:],
                            AF.Identity, scale=1.0).then_inc(p.sed.h, 1)
            se_bc.append(p.sed.bump())
        se_bcast = se_bc[-1]

        sx = [p.sx0, p.sx1]
        se_t = []
        te_tr = []
        ve_evac = []
        for i in range(NT):
            sl = i % 2
            if i >= 2:
                p.sy.wait_ge(p.sed.h, se_t[i - 2])
            p.sy.dma_start(out=xt[sl][:], in_=xn_h.ap()[i * 128:(i + 1) * 128, :]) \
                .then_inc(sx[sl].h, 16)
            xv = sx[sl].bump(16)
            stt = stats[:, 16 * sl:16 * sl + 16]
            p.se.wait_ge(sx[sl].h, xv)
            if i >= 1:
                p.se.wait_ge(p.sed.h, se_t[i - 1])
            if i >= 2:
                p.se.wait_ge(p.ved.h, ve_evac[i - 2])
            p.se.activation(tmp[:], xt[sl][:], AF.Square,
                            accum_out=stt[:, 0:1]).then_inc(p.sed.h, 1)
            se_sq = p.sed.bump()
            p.ve.wait_ge(p.sed.h, se_sq)
            p.ve.tensor_scalar(stt[:, 1:2], stt[:, 0:1], 1.0 / D, 1e-6,
                               ALU.mult, ALU.add).then_inc(p.ved.h, 1)
            ve_ss = p.ved.bump()
            p.se.wait_ge(p.ved.h, ve_ss)
            p.se.activation(stt[:, 2:3], stt[:, 1:2], AF.Sqrt).then_inc(p.sed.h, 1)
            se_rm = p.sed.bump()
            p.ve.wait_ge(p.sed.h, se_rm)
            p.ve.reciprocal(stt[:, 2:3], stt[:, 2:3])
            p.ve.drain()
            if i == 0:
                p.ve.wait_ge(p.sed.h, se_bcast)
            p.ve.tensor_tensor(tmp[:], xt[sl][:], nwb[:], ALU.mult) \
                .then_inc(p.ved.h, 1)
            ve_t = p.ved.bump()
            p.se.wait_ge(p.ved.h, ve_t)
            if i >= 2:
                p.se.wait_ge(p.ted.h, te_tr[i - 2])
            p.se.activation(tb[sl][:], tmp[:], AF.Identity,
                            scale=stt[:, 2:3]).then_inc(p.sed.h, 1)
            se_t.append(p.sed.bump())

            bank = pq[4 + i % 2]
            pb_bf = bank[:].bitcast(BF16)
            p.te.wait_ge(p.sed.h, se_t[i])
            if i == 0:
                p.te.wait_ge(p.gpd.h, gp_all)
            if i >= 2:
                p.te.wait_ge(p.ved.h, ve_evac[i - 2])
            for j in range(ND):
                inst = p.te.transpose(pb_bf[:, j * 128:(j + 1) * 128],
                                      tb[sl][:, j * 128:(j + 1) * 128], ident[:])
            inst.then_inc(p.ted.h, 1)
            te_tr.append(p.ted.bump())
            p.ve.wait_ge(p.ted.h, te_tr[i])
            p.ve.tensor_copy(
                strided_ap(tT, i * 128, [[S, ND], [1, 128]]),
                pb_bf[:].rearrange("p (j c) -> p j c", j=ND)).then_inc(p.ved.h, 1)
            ve_evac.append(p.ved.bump())
        ve_tT = ve_evac[-1]

        se_lg = {}
        for tc in range(NT):
            bank = pq[tc % 2]
            p.te.wait_ge(p.ved.h, ve_tT)
            if tc >= 2:
                p.te.wait_ge(p.sed.h, se_lg[tc - 2])
            for j in range(ND):
                inst = p.te.matmul(bank[:, 0:NE],
                                   tT[:, j * S + tc * 128: j * S + (tc + 1) * 128],
                                   gwTs[:, j * NE:(j + 1) * NE],
                                   start=(j == 0), stop=(j == ND - 1))
            inst.then_inc(p.ted.h, 1)
            te_g = p.ted.bump()
            p.se.wait_ge(p.ted.h, te_g)
            p.se.activation(gl[:, tc * NE:(tc + 1) * NE], bank[:, 0:NE],
                            AF.Identity, scale=1.0).then_inc(p.sed.h, 1)
            se_lg[tc] = p.sed.bump()
        se_logits = se_lg[NT - 1]

        ve_cb = {}
        for tc in range(NT):
            g8 = gl[:, tc * NE:(tc + 1) * NE]
            c8 = comb[:, tc * NE:(tc + 1) * NE]
            p.ve.wait_ge(p.sed.h, se_lg[tc])
            p.ve.tensor_reduce(gsc[:, 0:1], g8, axis=AX.X, op=ALU.max)
            p.ve.drain()
            p.ve.tensor_scalar(c8, g8, gsc[:, 0:1], 1e30, ALU.is_equal, ALU.mult)
            p.ve.drain()
            p.ve.tensor_tensor(c8, g8, c8, ALU.subtract)
            p.ve.drain()
            p.ve.tensor_reduce(gsc[:, 1:2], c8, axis=AX.X, op=ALU.max)
            p.ve.drain()
            p.ve.tensor_scalar(c8, g8, gsc[:, 1:2], None, ALU.is_ge)
            p.ve.tensor_scalar(gsc[:, 2:3], gsc[:, 0:1], -1.0, None, ALU.mult) \
                .then_inc(p.ved.h, 1)
            ve_i = p.ved.bump()
            p.se.wait_ge(p.ved.h, ve_i)
            p.se.activation(gsc[:, 4:12], g8, AF.Exp, bias=gsc[:, 2:3]) \
                .then_inc(p.sed.h, 1)
            se_e = p.sed.bump()
            p.ve.wait_ge(p.sed.h, se_e)
            p.ve.tensor_tensor(c8, c8, gsc[:, 4:12], ALU.mult)
            p.ve.drain()
            p.ve.tensor_reduce(gsc[:, 12:13], c8, axis=AX.X, op=ALU.add)
            p.ve.drain()
            p.ve.reciprocal(gsc[:, 12:13], gsc[:, 12:13])
            p.ve.drain()
            p.ve.tensor_scalar(c8, c8, gsc[:, 12:13], None, ALU.mult) \
                .then_inc(p.ved.h, 1)
            ve_cb[tc] = p.ved.bump()
        ve_comb = ve_cb[NT - 1]

        sw1, sw2 = p.gw0, p.gw1
        se_hprev = {}
        ve_accprev = {}
        ve_b2used = None
        te_hc3 = {}
        te_hc3_prev = None
        for e in range(NE):
            if e >= 1:
                p.gp.wait_ge(p.sed.h, se_b2prev)
            p.gp.dma_start(out=b2r[:], in_=b2_h.ap()[:, e * D:(e + 1) * D]) \
                .then_inc(p.sb2.h, 16)
            gb2v = p.sb2.bump(16)
            w1qv = []
            qsem = [sw1, p.gw3]
            for q in range(4):
                sq = qsem[q % 2]
                if e >= 1:
                    p.gp.wait_ge(p.ted.h, te_hc3_prev[q * 4 + 3])
                if q >= 2:
                    p.gp.wait_ge(sq.h, sq.v)
                p.gp.dma_start(
                    out=w1e[:].rearrange("p (j h) -> p j h", j=ND)
                    [:, :, q * 512:(q + 1) * 512],
                    in_=w1_h.ap()[e].rearrange("(j p) h -> p j h", p=128)
                    [:, :, q * 512:(q + 1) * 512]).then_inc(sq.h, 16)
                w1qv.append((sq, sq.bump(16)))
            if e >= 1:
                p.gp.wait_ge(p.ted.h, te_b_done)
            p.gp.dma_start(out=w2e[:].rearrange("p (j d) -> p j d", j=NHC2),
                           in_=w2_h.ap()[e].rearrange("(j p) d -> p j d", p=128)) \
                .then_inc(sw2.h, 16)
            w2v = sw2.bump(16)

            p.te.wait_ge(p.sb2.h, gb2v)
            if ve_b2used is not None:
                p.te.wait_ge(p.ved.h, ve_b2used)
            for vh in range(2):
                bank = pq[4 + vh]
                p.te.matmul(bank[:], onesb[:], b2r[:, vh * 512:(vh + 1) * 512],
                            start=True, stop=True).then_inc(p.ted.h, 1)
                te_b2 = p.ted.bump()
                p.se.wait_ge(p.ted.h, te_b2)
                p.se.activation(b2b[:, vh * 512:(vh + 1) * 512], bank[:],
                                AF.Identity, scale=1.0).then_inc(p.sed.h, 1)
                se_b2prev = p.sed.bump()

            for hf in range(4):
                if e == 0 and hf == 0:
                    p.te.wait_ge(p.ved.h, ve_tT)
                    p.te.wait_ge(p.sed.h, se_logits)
                for hc in range(NHC2):
                    if hf == 0 and hc % 4 == 0:
                        sq, qv = w1qv[hc // 4]
                        p.te.wait_ge(sq.h, qv)
                    bank = pq[hc % 4]
                    kk = se_hprev.get(hc % 4)
                    if kk is not None:
                        p.te.wait_ge(p.sed.h, kk)
                    for j in range(ND):
                        inst = p.te.matmul(
                            bank[:],
                            w1e[:, j * MOE_H + hc * 128: j * MOE_H + (hc + 1) * 128],
                            tT[:, j * S + hf * 512: j * S + (hf + 1) * 512],
                            start=(j == 0), stop=(j == ND - 1))
                    inst.then_inc(p.ted.h, 1)
                    te_h = p.ted.bump()
                    if hf == 3:
                        te_hc3[hc] = te_h
                    p.se.wait_ge(p.ted.h, te_h)
                    p.se.activation(
                        hTh[:, hc * 512:(hc + 1) * 512],
                        bank[:], AF.Gelu, bias=b1s[:, e * 16 + hc: e * 16 + hc + 1]) \
                        .then_inc(p.sed.h, 1)
                    se_hprev[hc % 4] = p.sed.bump()
                te_a_done = p.ted.v
                se_a_done = p.sed.v

                p.te.wait_ge(sw2.h, w2v)
                p.te.wait_ge(p.sed.h, se_a_done)
                for tr in range(4):
                    tc = hf * 4 + tr
                    for dh in range(2):
                        bank = pq[4 + (tr % 2) * 2 + dh]
                        kk = ve_accprev.get((tr % 2) * 2 + dh)
                        if kk is not None:
                            p.te.wait_ge(p.ved.h, kk)
                        for j in range(NHC2):
                            inst = p.te.matmul(
                                bank[:],
                                hTh[:, j * 512 + tr * 128: j * 512 + (tr + 1) * 128],
                                w2e[:, j * D + dh * 512: j * D + (dh + 1) * 512],
                                start=(j == 0), stop=(j == NHC2 - 1))
                        inst.then_inc(p.ted.h, 1)
                        te_o = p.ted.bump()
                        p.ve.wait_ge(p.ted.h, te_o)
                        if e == 0 and tc < 2:
                            p.ve.wait_ge(p.sac.h, p.sac.v)
                        if e == 0 and tc == 0 and dh == 0:
                            p.ve.wait_ge(p.ved.h, ve_comb)
                        if tc == 0 and dh == 0:
                            p.ve.wait_ge(p.sed.h, se_b2prev)
                        a_sl = acc[:, tc * D + dh * 512: tc * D + (dh + 1) * 512]
                        ce = comb[:, tc * NE + e: tc * NE + e + 1]
                        p.ve.scalar_tensor_tensor(a_sl, bank[:], ce, a_sl,
                                                  ALU.mult, ALU.add)
                        p.ve.drain()
                        p.ve.scalar_tensor_tensor(
                            a_sl, b2b[:, dh * 512:(dh + 1) * 512], ce, a_sl,
                            ALU.mult, ALU.add).then_inc(p.ved.h, 1)
                        ve_accprev[(tr % 2) * 2 + dh] = p.ved.bump()
                        p.ve.drain()
                te_b_done = p.ted.v
            ve_b2used = p.ved.v
            te_hc3_prev = dict(te_hc3)

        ve_moe = p.ved.v

        stg = [p.gw2, p.gw3]
        stg_base = [p.gw2.v, p.gw3.v]
        sws = [p.sw2, p.sw3]
        ve_fin = {}
        for tc in range(NT):
            sl = tc % 2
            if tc < 2:
                p.sy.wait_ge(sx[sl].h, sx[sl].v)
                p.sy.wait_ge(p.ved.h, ve_tT)
            if tc >= 2:
                p.sy.wait_ge(p.ved.h, ve_fin[tc - 2])
            p.sy.dma_start(out=xt[sl][:], in_=x_h.ap()[tc * 128:(tc + 1) * 128, :]) \
                .then_inc(sx[sl].h, 16)
            xv = sx[sl].bump(16)
            if tc < 2:
                p.se.wait_ge(p.ved.h, ve_moe)
            else:
                p.se.wait_ge(p.ved.h, ve_fin[tc - 2])
            p.se.dma_start(out=s1b[sl][:], in_=s_h.ap()[tc * 128:(tc + 1) * 128, :]) \
                .then_inc(sws[sl].h, 16)
            sv = sws[sl].bump(16)
            if tc == 1:
                p.se.wait_ge(p.ved.h, ve_fin[0])
            tdst = tmp[:] if tc < 2 else acc[:, (tc - 2) * D:(tc - 1) * D]
            p.se.activation(tdst, acc[:, tc * D:(tc + 1) * D], AF.Silu) \
                .then_inc(p.sed.h, 1)
            se_g = p.sed.bump()
            p.ve.wait_ge(p.sed.h, se_g)
            p.ve.wait_ge(sx[sl].h, xv)
            p.ve.wait_ge(sws[sl].h, sv)
            if tc >= 2:
                p.ve.wait_ge(stg[sl].h, stg_base[sl] + 16 * (tc // 2))
            p.ve.tensor_tensor(otb[sl][:], s1b[sl][:], tdst, ALU.mult)
            p.ve.drain()
            p.ve.tensor_tensor(otb[sl][:], xt[sl][:], otb[sl][:], ALU.add) \
                .then_inc(p.ved.h, 1)
            ve_fin[tc] = p.ved.bump()
            p.gp.wait_ge(p.ved.h, ve_fin[tc])
            p.gp.dma_start(out=o_h.ap()[tc * 128:(tc + 1) * 128, :],
                           in_=otb[sl][:]).then_inc(stg[sl].h, 16)
            stg[sl].bump(16)

        p.sy.wait_ge(p.gw2.h, p.gw2.v)
        p.sy.wait_ge(p.gw3.h, p.gw3.v)
        p.sy.wait_ge(p.std0.h, p.std0.v)
        p.sy.wait_ge(p.std1.h, p.std1.v)

    return narrative


_EXT_INPUTS = [
    ("x", [S, D]), ("wqkvT", [D, 3 * D]), ("bq_act", [128, 8]),
    ("bk_act", [128, 8]), ("bv_row", [1, D]), ("masks", [4, 128, 512]),
    ("ident", [128, 128]), ("wgu", [2 * FFN_I, D]), ("wd", [D, FFN_I]),
    ("wabcdT", [D, 4 * D]), ("ab_act", [128, 8]), ("outwT", [D, D]),
    ("outb_row", [1, D]), ("normw_row", [1, D]), ("normw2_row", [1, D]),
    ("gwT", [D, NE]), ("w1T", [NE, D, MOE_H]), ("w2T", [NE, MOE_H, D]),
    ("b1_act", [128, 128]), ("b2_row", [1, NE * D]),
]
_INTERNALS = [("s0", [S, D]), ("x1", [S, D]), ("xn1", [S, D]), ("s1", [S, D])]
def _wq_units():
    units = []
    for i in range((FFN_I + 511) // 512):
        hs0, hw = i * 512, min(512, FFN_I - i * 512)
        for gu in range(2):
            for rp in range((hw // 128) // 2):
                row0 = gu * FFN_I + hs0 + rp * 256
                units.append(row0)
                units.append(row0 + 128)
    return units


WQ_UNITS = _wq_units()

_INTERNALS_BF = [("wq_pre", [len(WQ_UNITS) * 128, D])]


def build_fused():
    nc = bass.Bass("TRN2", target_bir_lowering=False)
    dram = {}
    for n, sh in _EXT_INPUTS:
        dram[n] = nc.dram_tensor(n, sh, F32, kind="ExternalInput")
    for n, sh in _INTERNALS:
        dram[n] = nc.dram_tensor(n, sh, F32, kind="Internal")
    for n, sh in _INTERNALS_BF:
        dram[n] = nc.dram_tensor(n, sh, BF16, kind="Internal")
    dram["out"] = nc.dram_tensor("out", [S, D], F32, kind="ExternalOutput")
    pq = [nc.alloc_psum_tensor(f"pq{i}", [128, 512], F32) for i in range(8)]
    sems = alloc_sems(nc, SEM_NAMES)
    base = nc.sbuf_base
    nc.sbuf_base = base
    narrA = make_l0_attn(nc, dram, pq)
    nc.sbuf_base = base
    narrB = make_l0_ffn(nc, dram, pq)
    nc.sbuf_base = base
    narrC = make_l1_attn(nc, dram, pq)
    nc.sbuf_base = base
    narrD = make_l1_ffn(nc, dram, pq)

    def narrative(p):
        narrA(p)
        phase_barrier(p)
        narrB(p)
        phase_barrier(p)
        narrC(p)
        phase_barrier(p)
        narrD(p)

    run_block(nc, sems, narrative)
    return nc


def _build_single(make_fn, in_names, internal_names, out_names):
    nc = bass.Bass("TRN2", target_bir_lowering=False)
    shapes = dict(_EXT_INPUTS + _INTERNALS + [("out", [S, D])])
    dram = {}
    for n in in_names:
        dram[n] = nc.dram_tensor(n, shapes[n], F32, kind="ExternalInput")
    for n in internal_names:
        dram[n] = nc.dram_tensor(n, shapes[n], F32, kind="Internal")
    for n in out_names:
        dram[n] = nc.dram_tensor(n, shapes[n], F32, kind="ExternalOutput")
    pq = [nc.alloc_psum_tensor(f"pq{i}", [128, 512], F32) for i in range(8)]
    sems = alloc_sems(nc, SEM_NAMES)
    run_block(nc, sems, make_fn(nc, dram, pq))
    return nc


def build_l0_attn():
    return _build_single(make_l0_attn,
                         ["x", "wqkvT", "bq_act", "bk_act", "bv_row", "masks",
                          "ident"], [], ["s0"])


def build_l0_ffn():
    return _build_single(make_l0_ffn,
                         ["x", "s0", "wgu", "wd", "ident"], [], ["x1"])


def build_l1_attn():
    return _build_single(make_l1_attn,
                         ["x1", "wabcdT", "ab_act", "outwT", "outb_row",
                          "normw_row", "ident"], [], ["s1", "xn1"])


def build_l1_ffn():
    return _build_single(make_l1_ffn,
                         ["x1", "xn1", "s1", "normw2_row", "gwT", "w1T", "w2T",
                          "b1_act", "b2_row", "ident"], [], ["out"])


def _prep_inputs(inputs):
    x = np.ascontiguousarray(np.asarray(inputs["L0_attn_x"], dtype=np.float32))
    Wq = np.asarray(inputs["L0_attn_Wq"], dtype=np.float32).reshape(D, D)
    Wk = np.asarray(inputs["L0_attn_Wk"], dtype=np.float32).reshape(D, D)
    Wv = np.asarray(inputs["L0_attn_Wv"], dtype=np.float32).reshape(D, D)
    wqkvT = np.ascontiguousarray(
        np.concatenate([Wq.T, Wk.T, Wv.T], axis=1).astype(np.float32))
    bq = np.asarray(inputs["L0_attn_bq"], dtype=np.float32).reshape(H * HD)
    bk = np.asarray(inputs["L0_attn_bk"], dtype=np.float32).reshape(H * HD)
    bv = np.asarray(inputs["L0_attn_bv"], dtype=np.float32).reshape(1, H * HD)
    m = np.arange(4)[:, None, None]
    pp = np.arange(128)[None, :, None]
    ff = np.arange(512)[None, None, :]
    shared = {
        "wqkvT": wqkvT,
        "bq_act": np.ascontiguousarray(bq.reshape(8, 128).T),
        "bk_act": np.ascontiguousarray(bk.reshape(8, 128).T),
        "bv_row": np.ascontiguousarray(bv),
        "masks": np.ascontiguousarray((ff >= 128 * m + pp).astype(np.float32)),
        "ident": np.eye(128, dtype=np.float32),
        "wgu": np.ascontiguousarray(np.concatenate(
            [np.asarray(inputs["L0_ffn_w_gate"], dtype=np.float32),
             np.asarray(inputs["L0_ffn_w_up"], dtype=np.float32)], axis=0)),
        "wd": np.ascontiguousarray(
            np.asarray(inputs["L0_ffn_w_down"], dtype=np.float32)),
        "wabcdT": np.ascontiguousarray(np.concatenate(
            [np.asarray(inputs["L1_attn_A_w"], dtype=np.float32).T,
             np.asarray(inputs["L1_attn_B_w"], dtype=np.float32).T,
             np.asarray(inputs["L1_attn_C_w"], dtype=np.float32).T,
             np.asarray(inputs["L1_attn_D_w"], dtype=np.float32).T], axis=1)),
        "ab_act": np.ascontiguousarray(
            np.asarray(inputs["L1_attn_A_b"], dtype=np.float32).reshape(8, 128).T),
        "outwT": np.ascontiguousarray(
            np.asarray(inputs["L1_attn_out_w"], dtype=np.float32).T),
        "outb_row": np.ascontiguousarray(
            np.asarray(inputs["L1_attn_out_b"], dtype=np.float32).reshape(1, D)),
        "normw_row": np.ascontiguousarray(
            np.asarray(inputs["L1_attn_norm_w"], dtype=np.float32).reshape(1, D)),
        "normw2_row": np.ascontiguousarray(
            np.asarray(inputs["L1_ffn_norm_weight"],
                       dtype=np.float32).reshape(1, D)),
        "gwT": np.ascontiguousarray(
            np.asarray(inputs["L1_ffn_gate_w"], dtype=np.float32).T),
        "w1T": np.ascontiguousarray(
            np.asarray(inputs["L1_ffn_w1"], dtype=np.float32).transpose(0, 2, 1)),
        "w2T": np.ascontiguousarray(
            np.asarray(inputs["L1_ffn_w2"], dtype=np.float32).transpose(0, 2, 1)),
        "b1_act": np.ascontiguousarray(
            np.asarray(inputs["L1_ffn_b1"], dtype=np.float32)
            .reshape(8, 16, 128).transpose(2, 0, 1).reshape(128, 128)),
        "b2_row": np.ascontiguousarray(
            np.asarray(inputs["L1_ffn_b2"], dtype=np.float32).reshape(1, NE * D)),
    }
    return shared, x


def kernel(**inputs):
    shared, x = _prep_inputs(inputs)
    nc = build_fused()
    maps = [dict(shared, x=np.ascontiguousarray(x[b])) for b in range(B)]
    res = run_bass_kernel_spmd(nc, maps, core_ids=list(range(B)))
    return np.stack([r["out"] for r in res.results])
